# version 06
# speedup vs baseline: 1.2100x; 1.0428x over previous
import numpy as np
import ml_dtypes
import concourse.bass as bass
import concourse.mybir as mybir
from concourse.bass_utils import run_bass_kernel_spmd

F32 = mybir.dt.float32
BF16 = mybir.dt.bfloat16
U32 = mybir.dt.uint32
I32 = mybir.dt.int32
AF = mybir.ActivationFunctionType
ALU = mybir.AluOpType
BF = ml_dtypes.bfloat16

NCORES = 8
B = 2
L = 16384
NF = 2 * L
HD = 1024
T = B * L
TPC = T // NCORES
CPC = HD // NCORES
NSEQ = B * CPC
NMODE = 64
E = 64
EPC = E // NCORES
TOPK = 8
CAP = 512
ISH = 2048
IEX = 512
EPS = 1e-5
K1H = 129


def _host_consts():
    c = {}
    l1 = np.arange(128)
    k1 = np.arange(K1H)
    l2 = np.arange(128)
    k2 = np.arange(128)

    F1 = np.exp(-2j * np.pi * np.outer(l1, k1) / 256.0)
    c["f1"] = np.concatenate([F1.real, F1.imag], axis=1).astype(BF)

    TWm = np.exp(-2j * np.pi * np.outer(l2, k1) / NF)
    c["twr2"] = np.concatenate([TWm.real, TWm.real], axis=1).astype(BF)
    c["twi2"] = np.concatenate([TWm.imag, TWm.imag], axis=1).astype(BF)

    ang = 2 * np.pi * np.outer(l2, k2) / 128.0
    c["f2re"] = np.cos(ang).astype(BF)
    c["f2im"] = (-np.sin(ang)).astype(BF)
    c["f2imn"] = (np.sin(ang)).astype(BF)

    F2i_r = np.cos(ang)
    F2i_i = np.sin(ang)
    c["ra"] = np.concatenate([F2i_r, F2i_i], axis=1).astype(BF)
    c["rb"] = np.concatenate([-F2i_i, F2i_r], axis=1).astype(BF)
    itwe = np.exp(+2j * np.pi * l2 * 128.0 / NF)
    era_r = F2i_r * itwe.real[None, :] - F2i_i * itwe.imag[None, :]
    era_i = F2i_r * itwe.imag[None, :] + F2i_i * itwe.real[None, :]
    c["rae"] = np.concatenate([era_r, era_i], axis=1).astype(BF)
    c["rbe"] = np.concatenate([-era_i, era_r], axis=1).astype(BF)

    IT = np.exp(+2j * np.pi * np.outer(k1[:128], l2) / NF)
    c["itr2"] = np.concatenate([IT.real, IT.real], axis=1).astype(BF)
    c["iti2"] = np.concatenate([IT.imag, IT.imag], axis=1).astype(BF)

    cc = np.ones(K1H)
    cc[0] = 0.5
    cc[128] = 0.5
    G1 = (2.0 * cc[:, None] / NF) * np.exp(+2j * np.pi * np.outer(k1, l1) / 256.0)
    c["g1re"] = np.ascontiguousarray(G1.real[:128]).astype(BF)
    c["g1imn"] = np.ascontiguousarray(-G1.imag[:128]).astype(BF)
    c["g1er"] = np.ascontiguousarray(G1.real[128:129]).astype(BF)
    c["g1ein"] = np.ascontiguousarray(-G1.imag[128:129]).astype(BF)

    lf = np.arange(L, dtype=np.float32)
    ang2 = (np.complex64(-2j * np.pi) * lf.astype(np.complex64) / np.float32(L)).astype(np.complex64)
    z = np.exp(ang2).astype(np.complex64)
    one = np.complex64(1.0)
    w = ((one - z) / (one + z)).astype(np.complex64)
    tz = (np.complex64(2.0) / (one + z)).astype(np.complex64)
    c["w_re"] = w.real.astype(np.float32)
    c["w_im"] = w.imag.astype(np.float32)
    c["tz_re"] = tz.real.astype(np.float32)
    c["tz_im"] = tz.imag.astype(np.float32)

    m1 = np.arange(128)
    m2 = np.arange(128)
    FA = np.exp(+2j * np.pi * np.outer(m1, l2) / 128.0) / 128.0
    c["ifab"] = np.concatenate([FA.real, FA.imag], axis=1).astype(BF)
    c["ifbb"] = np.concatenate([-FA.imag, FA.real], axis=1).astype(BF)
    ITW2 = np.exp(+2j * np.pi * np.outer(m2, l2) / L)
    c["iftr2"] = np.concatenate([ITW2.real, ITW2.real], axis=1).astype(BF)
    c["ifti2"] = np.concatenate([ITW2.imag, ITW2.imag], axis=1).astype(BF)
    FB = np.exp(+2j * np.pi * np.outer(m2, l1) / 128.0) / 128.0
    c["ifbre"] = FB.real.astype(BF)
    c["ifbim"] = FB.imag.astype(BF)
    c["ifbimn"] = (-FB.imag).astype(BF)

    c["ident"] = np.eye(128, dtype=np.float32).astype(BF)
    c["ident32"] = np.eye(128, dtype=np.float32)
    c["ones1x128"] = np.ones((1, 128), dtype=np.float32).astype(BF)
    c["ones32r"] = np.ones((1, 128), dtype=np.float32)
    c["ones64c"] = np.ones((64, 1), dtype=np.float32)
    sgn = np.ones((128, 1), dtype=np.float32)
    for r in range(8):
        sgn[16 * r + 8:16 * r + 16] = -1.0
    c["sign_col"] = sgn
    return c


_CONSTS = None

def consts():
    global _CONSTS
    if _CONSTS is None:
        _CONSTS = _host_consts()
    return _CONSTS


class Plan:

    def __init__(self, nc, stack):
        self.nc = nc
        self.stack = stack
        self.sems = {}
        self.cnt = {}

    def sem(self, name):
        if name not in self.sems:
            self.sems[name] = self.stack.enter_context(self.nc.semaphore(name))
            self.cnt[name] = 0
        return self.sems[name]

    def inc(self, name, instr, k=1):
        instr.then_inc(self.sem(name), k)
        self.cnt[name] += k
        return self.cnt[name]

    def val(self, name):
        self.sem(name)
        return self.cnt[name]


def dma_in(plan, eng, out, in_, sem="ld"):
    return plan.inc(sem, eng.dma_start(out=out, in_=in_), 16)


CONST_NAMES = ["f1", "twr2", "twi2", "f2re", "f2im", "f2imn", "ra", "rb", "rae",
               "rbe", "itr2", "iti2", "g1re", "g1imn", "g1er", "g1ein",
               "ifab", "ifbb", "iftr2", "ifti2", "ifbre", "ifbim", "ifbimn",
               "ident", "ident32", "ones1x128", "ones32r", "ones64c", "sign_col"]


class Prog:

    def __init__(self, nc, stack, rank):
        self.nc = nc
        self.stack = stack
        self.scopes = [stack]
        self.rank = rank
        self.plan = Plan(nc, stack)
        self.d = {}
        self.s = {}
        self.p = {}
        self.marks = {}

    def push(self):
        import contextlib
        st = contextlib.ExitStack()
        self.scopes.append(st)
        return st

    def pop(self):
        self.scopes.pop().close()

    def dram(self, name, shape, dtype, kind=None):
        if kind is None:
            t = self.nc.dram_tensor(name, shape, dtype)
        else:
            t = self.nc.dram_tensor(name, shape, dtype, kind=kind)
        self.d[name] = t.ap()
        return self.d[name]

    def sbuf(self, name, shape, dtype):
        t = self.scopes[-1].enter_context(self.nc.sbuf_tensor(name, shape, dtype))
        self.s[name] = t
        return t

    def sbuf_global(self, name, shape, dtype):
        t = self.scopes[0].enter_context(
            self.nc.sbuf_tensor(name, shape, dtype, side="right"))
        self.s[name] = t
        return t

    def psum(self, name, shape, dtype=F32):
        t = self.scopes[-1].enter_context(self.nc.psum_tensor(name, shape, dtype))
        self.p[name] = t
        return t


def declare_conv_io(pr, with_inputs=True):
    c = consts()
    pr.dram("x_full", [T, HD], F32, kind="ExternalInput")
    pr.dram("x_ch", [CPC, B, L], F32, kind="ExternalInput")
    for nm in ["B_re", "B_im", "P_re", "P_im", "Q_re", "Q_im", "diag_re", "diag_im"]:
        pr.dram(nm, [NMODE, 1], F32, kind="ExternalInput")
    pr.dram("step", [1, 1], F32, kind="ExternalInput")
    pr.dram("Dv", [1, 1], F32, kind="ExternalInput")
    pr.dram("C_re", [CPC, NMODE], F32, kind="ExternalInput")
    pr.dram("C_im", [CPC, NMODE], F32, kind="ExternalInput")
    pr.dram("w_re", [1, 2048], F32, kind="ExternalInput")
    pr.dram("w_im", [1, 2048], F32, kind="ExternalInput")
    pr.dram("tz_re", [1, 2048], F32, kind="ExternalInput")
    pr.dram("tz_im", [1, 2048], F32, kind="ExternalInput")
    for nm in CONST_NAMES:
        arr = c[nm]
        dt = BF16 if arr.dtype == BF else F32
        pr.dram("c_" + nm, list(arr.shape), dt, kind="ExternalInput")
    pr.dram("x_ts", [TPC, HD], F32, kind="ExternalInput")
    pr.dram("cmix_pre", [128, CPC], F32, kind="ExternalInput")
    pr.dram("gidx", [128, 16], U32, kind="ExternalInput")
    pr.dram("g_agin", [2048, 128], F32)
    pr.dram("g_agout", [16384, 128], F32)
    pr.dram("bas_agin", [16, 66048], BF16)
    pr.dram("bas_agout", [128, 66048], BF16)
    pr.dram("kf", [CPC, 66048], BF16)
    pr.dram("st_agin", [64, 128], F32)
    pr.dram("st_agout", [512, 128], F32)
    pr.dram("ssm_agin", [CPC, T], BF16)
    pr.dram("ssm_agout", [HD, T], BF16)
    pr.dram("kg_rows", [8, 2048], F32)


def conv_const_loads(pr, g):
    plan = pr.plan
    c = consts()
    pr.bnd32767 = g.alloc_register()
    g.reg_mov(pr.bnd32767, 32767)
    pr.bnd16383 = g.alloc_register()
    g.reg_mov(pr.bnd16383, 16383)
    for nm, w in [("twr2", 258), ("twi2", 258)]:
        tile = pr.sbuf(nm + "_4", [128, 4 * w], BF16)
        for j in range(4):
            dma_in(plan, g, tile[:, j * w:(j + 1) * w], pr.d["c_" + nm], "cl")
    for nm, w in [("itr2", 256), ("iti2", 256)]:
        tile = pr.sbuf(nm + "_4", [128, 4 * w], BF16)
        for j in range(4):
            dma_in(plan, g, tile[:, j * w:(j + 1) * w], pr.d["c_" + nm], "cl")
    for nm in ["f1", "f2re", "f2im", "f2imn", "ra", "rb", "rae", "rbe",
               "g1re", "g1imn", "g1er", "g1ein", "ident", "ones1x128",
               "ifab", "ifbb", "iftr2", "ifti2", "ifbre", "ifbim", "ifbimn"]:
        arr = c[nm]
        tile = pr.sbuf(nm, list(arr.shape), BF16)
        dma_in(plan, g, tile[:], pr.d["c_" + nm], "cl")
    for nm in ["ones32r", "ones64c", "sign_col", "ident32"]:
        arr = c[nm]
        tile = pr.sbuf(nm, list(arr.shape), F32)
        dma_in(plan, g, tile[:], pr.d["c_" + nm], "cl")
    for nm in ["B_re", "B_im", "P_re", "P_im", "Q_re", "Q_im", "diag_re", "diag_im"]:
        tile = pr.sbuf("p_" + nm, [NMODE, 1], F32)
        dma_in(plan, g, tile[:], pr.d[nm], "cl")
    for nm in ["step", "Dv"]:
        tile = pr.sbuf("p_" + nm, [1, 1], F32)
        dma_in(plan, g, tile[:], pr.d[nm], "cl")
    for nm in ["w_re", "w_im", "tz_re", "tz_im"]:
        tile = pr.sbuf("r_" + nm, [1, 2048], F32)
        dma_in(plan, g, tile[:], pr.d[nm], "cl")
    for nm in ["C_re", "C_im"]:
        tile = pr.sbuf("s" + nm, [CPC, NMODE], F32)
        dma_in(plan, g, tile[:], pr.d[nm], "cl")
    tile = pr.sbuf("scmix_pre", [128, CPC], F32)
    dma_in(plan, g, tile[:], pr.d["cmix_pre"], "cl")
    return plan.val("cl")


def emit_stats(pr, block):
    plan = pr.plan
    nt = TPC // 128

    x2 = pr.sbuf("st_x", [128, 2 * HD], F32)
    sq_scr = pr.sbuf("st_sqscr", [128, HD], BF16)
    s1c = pr.sbuf("st_s1", [128, nt], F32)
    sqc = pr.sbuf("st_sq", [128, nt], F32)
    stats = pr.sbuf_global("st_out", [128, 64], F32)
    scr = pr.sbuf("st_scr", [128, 4], F32)
    tp = pr.sbuf("st_tp", [64, 128], F32)
    ps = pr.psum("st_ps", [128, 512], F32)

    @block.sync
    def _(sy):
        for i in range(nt):
            if i >= 2:
                sy.wait_ge(plan.sem("sv"), 6 * (i - 2) + 1)
                sy.wait_ge(plan.sem("ss1"), i - 1)
            dma_in(plan, sy, x2[:, (i % 2) * HD:(i % 2 + 1) * HD],
                   pr.d["x_ts"][128 * i:128 * (i + 1), :], "sl%d" % (i % 2))

    @block.scalar
    def _(sc):
        for i in range(nt):
            sc.wait_ge(plan.sem("sl%d" % (i % 2)), 16 * (i // 2 + 1))
            plan.inc("ss1", sc.activation(sq_scr[:], x2[:, (i % 2) * HD:(i % 2 + 1) * HD],
                                          AF.Square, accum_out=sqc[:, i:i + 1]))
            sc.wait_ge(plan.sem("sv"), 6 * i + 6)
            plan.inc("ss2", sc.activation(stats[:, 2 * i + 1:2 * i + 2],
                                          scr[:, (i % 2) * 2 + 1:(i % 2) * 2 + 2], AF.Sqrt))

    @block.vector
    def _(v):
        for i in range(nt):
            v.wait_ge(plan.sem("sl%d" % (i % 2)), 16 * (i // 2 + 1))
            n0 = plan.inc("sv", v.reduce_sum(
                s1c[:, i:i + 1], x2[:, (i % 2) * HD:(i % 2 + 1) * HD],
                axis=mybir.AxisListType.X))
            v.wait_ge(plan.sem("sv"), n0)
            v.wait_ge(plan.sem("ss1"), i + 1)
            if i >= 2:
                v.wait_ge(plan.sem("ss2"), i - 1)
            n1 = plan.inc("sv", v.tensor_scalar(
                stats[:, 2 * i:2 * i + 1], s1c[:, i:i + 1], 1.0 / HD, None, op0=ALU.mult))
            v.wait_ge(plan.sem("sv"), n1)
            n2 = plan.inc("sv", v.tensor_tensor(
                scr[:, (i % 2) * 2:(i % 2) * 2 + 1], stats[:, 2 * i:2 * i + 1],
                stats[:, 2 * i:2 * i + 1], op=ALU.mult))
            v.wait_ge(plan.sem("sv"), n2)
            n3 = plan.inc("sv", v.tensor_scalar(
                scr[:, (i % 2) * 2 + 1:(i % 2) * 2 + 2], sqc[:, i:i + 1],
                1.0 / HD, EPS, op0=ALU.mult, op1=ALU.add))
            v.wait_ge(plan.sem("sv"), n3)
            n4 = plan.inc("sv", v.tensor_tensor(
                scr[:, (i % 2) * 2 + 1:(i % 2) * 2 + 2],
                scr[:, (i % 2) * 2 + 1:(i % 2) * 2 + 2],
                scr[:, (i % 2) * 2:(i % 2) * 2 + 1], op=ALU.subtract))
            v.wait_ge(plan.sem("sv"), n4)
            plan.inc("sv", v.reciprocal(
                scr[:, (i % 2) * 2 + 1:(i % 2) * 2 + 2],
                scr[:, (i % 2) * 2 + 1:(i % 2) * 2 + 2]))
        v.wait_ge(plan.sem("stp"), 1)
        plan.inc("sve", v.tensor_copy(tp[:], ps[0:64, 0:128]))

    @block.tensor
    def _(te):
        te.wait_ge(plan.sem("ss2"), nt)
        te.wait_ge(plan.sem("cl"), plan.val("cl"))
        plan.inc("stp", te.transpose(ps[0:64, 0:128], stats[:], pr.s["ident32"][:]))

    @block.gpsimd
    def _(g):
        g.wait_ge(plan.sem("sve"), 1)
        dma_in(plan, g, pr.d["st_agin"], tp[:], "sgi")
        g.wait_ge(plan.sem("sgi"), 16)
        plan.inc("cc_st", g.collective_compute(
            "AllGather", ALU.bypass, replica_groups=[list(range(NCORES))],
            ins=[pr.d["st_agin"].opt()], outs=[pr.d["st_agout"].opt()]))


def emit_kg(pr, block):
    plan = pr.plan
    s = pr.s
    NCH = 4
    W = 512

    wb = {}
    for nm in ["wre", "wim", "tzre", "tzim", "qr", "qi"]:
        wb[nm] = pr.sbuf("kg_" + nm + "_b", [64, W], F32)
    cols = pr.sbuf("kg_cols", [64, 16], F32)
    rows = {}
    for nm in ["k10r", "k10i", "k11r", "k11i", "qrr", "qir", "scr1", "scr2"]:
        rows[nm] = pr.sbuf("kg_r_" + nm, [1, W], F32)
    t_a = pr.sbuf("kg_ta", [64, W], F32)
    t_b = pr.sbuf("kg_tb", [64, W], F32)
    t_c = pr.sbuf("kg_tc", [64, W], F32)
    dr = pr.sbuf("kg_dr", [64, W], F32)
    di = pr.sbuf("kg_di", [64, W], F32)
    cre = pr.sbuf("kg_cre", [64, W], F32)
    cim = pr.sbuf("kg_cim", [64, W], F32)
    gre = pr.sbuf("kg_gre", [64, W], F32)
    gim = pr.sbuf("kg_gim", [64, W], F32)
    smallc = pr.sbuf("kg_smallc", [1, 8], F32)
    ps0 = pr.psum("kg_ps0", [128, 512], F32)
    ps1 = pr.psum("kg_ps1", [128, 512], F32)

    VS = "vself"
    PS = "kps"

    @block.vector
    def _(v):
        def vop(instr, k=1):
            n = plan.inc(VS, instr, k)
            v.wait_ge(plan.sem(VS), n)
            return n
        v.wait_ge(plan.sem("cl"), plan.val("cl"))
        vop(v.tensor_scalar(smallc[0:1, 0:1], s["p_step"][0:1, 0:1], 1e-6, None, op0=ALU.max))
        vop(v.reciprocal(smallc[0:1, 1:2], smallc[0:1, 0:1]))
        n = vop(v.tensor_scalar(smallc[0:1, 2:3], smallc[0:1, 1:2], 2.0, None, op0=ALU.mult))
        pr.marks["kg_r2"] = n
        vop(v.tensor_tensor(cols[:, 0:1], s["p_Q_re"][:], s["p_B_re"][:], op=ALU.mult))
        vop(v.tensor_tensor(t_a[:, 0:1], s["p_Q_im"][:], s["p_B_im"][:], op=ALU.mult))
        vop(v.tensor_tensor(cols[:, 0:1], cols[:, 0:1], t_a[:, 0:1], op=ALU.subtract))
        vop(v.tensor_tensor(cols[:, 1:2], s["p_Q_re"][:], s["p_B_im"][:], op=ALU.mult))
        vop(v.tensor_tensor(t_a[:, 0:1], s["p_Q_im"][:], s["p_B_re"][:], op=ALU.mult))
        vop(v.tensor_tensor(cols[:, 1:2], cols[:, 1:2], t_a[:, 0:1], op=ALU.add))
        vop(v.tensor_scalar(cols[:, 2:3], cols[:, 1:2], -1.0, None, op0=ALU.mult))
        vop(v.tensor_tensor(cols[:, 3:4], s["p_Q_re"][:], s["p_P_re"][:], op=ALU.mult))
        vop(v.tensor_tensor(t_a[:, 0:1], s["p_Q_im"][:], s["p_P_im"][:], op=ALU.mult))
        vop(v.tensor_tensor(cols[:, 3:4], cols[:, 3:4], t_a[:, 0:1], op=ALU.subtract))
        vop(v.tensor_tensor(cols[:, 4:5], s["p_Q_re"][:], s["p_P_im"][:], op=ALU.mult))
        vop(v.tensor_tensor(t_a[:, 0:1], s["p_Q_im"][:], s["p_P_re"][:], op=ALU.mult))
        vop(v.tensor_tensor(cols[:, 4:5], cols[:, 4:5], t_a[:, 0:1], op=ALU.add))
        vop(v.tensor_scalar(cols[:, 5:6], cols[:, 4:5], -1.0, None, op0=ALU.mult))
        n = vop(v.tensor_scalar(cols[:, 6:7], s["p_P_im"][:], -1.0, None, op0=ALU.mult))
        pr.marks["kg_cols"] = n
        v.wait_ge(plan.sem(PS), 1)
        vop(v.tensor_copy(cols[:, 7:8], ps0[0:64, 0:1]))
        plan.inc("kev", v.nop())
        for ci in range(NCH):
            pb = 1 + ci * 10
            for bi, nm in enumerate(["wre", "wim", "tzre", "tzim"]):
                v.wait_ge(plan.sem(PS), pb + bi + 1)
                vop(v.tensor_copy(wb[nm][:], ps1[0:64, 0:W]))
                plan.inc("kev", v.nop())
            vop(v.tensor_scalar(dr[:], wb["wre"][:], cols[:, 7:8], s["p_diag_re"][:],
                                op0=ALU.mult, op1=ALU.subtract))
            vop(v.tensor_scalar(di[:], wb["wim"][:], cols[:, 7:8], s["p_diag_im"][:],
                                op0=ALU.mult, op1=ALU.subtract))
            vop(v.tensor_tensor(t_a[:], dr[:], dr[:], op=ALU.mult))
            vop(v.tensor_tensor(t_b[:], di[:], di[:], op=ALU.mult))
            vop(v.tensor_tensor(t_a[:], t_a[:], t_b[:], op=ALU.add))
            vop(v.reciprocal(t_a[:], t_a[:]))
            vop(v.tensor_tensor(cre[:], dr[:], t_a[:], op=ALU.mult))
            vop(v.scalar_tensor_tensor(cim[:], di[:], -1.0, t_a[:], op0=ALU.mult, op1=ALU.mult))
            for term, (c1, c2) in enumerate([(0, 2), (1, 0), (3, 5), (4, 3)]):
                vop(v.tensor_scalar(t_b[:], cre[:], cols[:, c1:c1 + 1], None, op0=ALU.mult))
                n = vop(v.scalar_tensor_tensor(t_c[:], cim[:], cols[:, c2:c2 + 1], t_b[:],
                                               op0=ALU.mult, op1=ALU.add))
                pr.marks["kterm_%d_%d" % (ci, term)] = n
                v.wait_ge(plan.sem(PS), pb + 4 + term + 1)
                vop(v.tensor_copy(rows[["k10r", "k10i", "k11r", "k11i"][term]][:],
                                  ps0[0:1, 0:W]))
                plan.inc("kev2", v.nop())
            vop(v.tensor_scalar(rows["scr1"][:], rows["k11r"][:], 1.0, None, op0=ALU.add))
            vop(v.tensor_tensor(rows["scr2"][:], rows["scr1"][:], rows["scr1"][:], op=ALU.mult))
            vop(v.tensor_tensor(rows["qrr"][:], rows["k11i"][:], rows["k11i"][:], op=ALU.mult))
            vop(v.tensor_tensor(rows["scr2"][:], rows["scr2"][:], rows["qrr"][:], op=ALU.add))
            vop(v.reciprocal(rows["scr2"][:], rows["scr2"][:]))
            vop(v.tensor_tensor(rows["qrr"][:], rows["k10r"][:], rows["scr1"][:], op=ALU.mult))
            vop(v.tensor_tensor(rows["qir"][:], rows["k10i"][:], rows["k11i"][:], op=ALU.mult))
            vop(v.tensor_tensor(rows["qrr"][:], rows["qrr"][:], rows["qir"][:], op=ALU.add))
            vop(v.tensor_tensor(rows["qrr"][:], rows["qrr"][:], rows["scr2"][:], op=ALU.mult))
            vop(v.tensor_tensor(rows["qir"][:], rows["k10i"][:], rows["scr1"][:], op=ALU.mult))
            vop(v.tensor_tensor(rows["scr1"][:], rows["k10r"][:], rows["k11i"][:], op=ALU.mult))
            vop(v.tensor_tensor(rows["qir"][:], rows["qir"][:], rows["scr1"][:], op=ALU.subtract))
            n = vop(v.tensor_tensor(rows["qir"][:], rows["qir"][:], rows["scr2"][:], op=ALU.mult))
            pr.marks["kqrow_%d" % ci] = n
            for bi, nm in enumerate(["qr", "qi"]):
                v.wait_ge(plan.sem(PS), pb + 8 + bi + 1)
                vop(v.tensor_copy(wb[nm][:], ps1[0:64, 0:W]))
                plan.inc("kev", v.nop())
            vop(v.tensor_scalar(t_a[:], wb["qr"][:], s["p_P_re"][:], None, op0=ALU.mult))
            vop(v.scalar_tensor_tensor(t_a[:], wb["qi"][:], cols[:, 6:7], t_a[:],
                                       op0=ALU.mult, op1=ALU.add))
            vop(v.tensor_scalar(t_a[:], t_a[:], -1.0, s["p_B_re"][:], op0=ALU.mult, op1=ALU.add))
            vop(v.tensor_scalar(t_b[:], wb["qi"][:], s["p_P_re"][:], None, op0=ALU.mult))
            vop(v.scalar_tensor_tensor(t_b[:], wb["qr"][:], s["p_P_im"][:], t_b[:],
                                       op0=ALU.mult, op1=ALU.add))
            vop(v.tensor_scalar(t_b[:], t_b[:], -1.0, s["p_B_im"][:], op0=ALU.mult, op1=ALU.add))
            if ci > 0:
                v.wait_ge(plan.sem("kgw"), 32 * ci)
            vop(v.tensor_tensor(t_c[:], cre[:], t_a[:], op=ALU.mult))
            vop(v.tensor_tensor(gre[:], cim[:], t_b[:], op=ALU.mult))
            vop(v.tensor_tensor(t_c[:], t_c[:], gre[:], op=ALU.subtract))
            vop(v.tensor_tensor(gim[:], cre[:], t_b[:], op=ALU.mult))
            vop(v.tensor_tensor(gre[:], cim[:], t_a[:], op=ALU.mult))
            vop(v.tensor_tensor(gim[:], gim[:], gre[:], op=ALU.add))
            vop(v.tensor_tensor(gre[:], wb["tzre"][:], t_c[:], op=ALU.mult))
            vop(v.tensor_tensor(t_a[:], wb["tzim"][:], gim[:], op=ALU.mult))
            vop(v.tensor_tensor(gre[:], gre[:], t_a[:], op=ALU.subtract))
            vop(v.tensor_tensor(gim[:], wb["tzre"][:], gim[:], op=ALU.mult))
            vop(v.tensor_tensor(t_a[:], wb["tzim"][:], t_c[:], op=ALU.mult))
            n = vop(v.tensor_tensor(gim[:], gim[:], t_a[:], op=ALU.add))
            pr.marks["kg_done_%d" % ci] = n

    @block.tensor
    def _(te):
        te.wait_ge(plan.sem("cl"), plan.val("cl"))
        te.wait_ge(plan.sem(VS), pr.marks["kg_r2"])
        plan.inc(PS, te.matmul(ps0[0:64, 0:1], s["ones32r"][0:1, 0:64],
                               smallc[0:1, 2:3], start=True, stop=True,
                               skip_group_check=True))
        nev = 0
        nev2 = 0
        for ci in range(NCH):
            for bi, nm in enumerate(["r_w_re", "r_w_im", "r_tz_re", "r_tz_im"]):
                nev += 1
                te.wait_ge(plan.sem("kev"), nev)
                plan.inc(PS, te.matmul(ps1[0:64, 0:W], s["ones32r"][0:1, 0:64],
                                       s[nm][0:1, W * ci:W * (ci + 1)],
                                       start=True, stop=True, skip_group_check=True))
            for term in range(4):
                te.wait_ge(plan.sem(VS), pr.marks["kterm_%d_%d" % (ci, term)])
                nev2 += 1
                if nev2 > 1:
                    te.wait_ge(plan.sem("kev2"), nev2 - 1)
                plan.inc(PS, te.matmul(ps0[0:1, 0:W], s["ones64c"][:], pr.s["kg_tc"][:],
                                       start=True, stop=True, skip_group_check=True))
            te.wait_ge(plan.sem(VS), pr.marks["kqrow_%d" % ci])
            for src in [pr.s["kg_r_qrr"], pr.s["kg_r_qir"]]:
                nev += 1
                te.wait_ge(plan.sem("kev"), nev)
                plan.inc(PS, te.matmul(ps1[0:64, 0:W], s["ones32r"][0:1, 0:64],
                                       src[0:1, 0:W], start=True, stop=True,
                                       skip_group_check=True))

    @block.sync
    def _(sy):
        gv = pr.d["g_agin"].rearrange("(n x) c -> n (x c)", x=16)
        for ci in range(NCH):
            sy.wait_ge(plan.sem(VS), pr.marks["kg_done_%d" % ci])
            dma_in(plan, sy, gv[0:64, W * ci:W * (ci + 1)], pr.s["kg_gre"][:], "kgw")
            dma_in(plan, sy, gv[64:128, W * ci:W * (ci + 1)], pr.s["kg_gim"][:], "kgw")

    @block.gpsimd
    def _(g):
        g.wait_ge(plan.sem("kgw"), 32 * NCH)
        plan.inc("cc_g", g.collective_compute(
            "AllGather", ALU.bypass, replica_groups=[list(range(NCORES))],
            ins=[pr.d["g_agin"].opt()], outs=[pr.d["g_agout"].opt()]))


def emit_basis(pr, block):
    plan = pr.plan
    s = pr.s

    gret = pr.sbuf("kb_gre", [128, 128], BF16)
    gimt = pr.sbuf("kb_gim", [128, 128], BF16)
    a1s = pr.sbuf("kb_a1", [128, 256], BF16)
    a2s = pr.sbuf("kb_a2", [128, 256], BF16)
    krs = pr.sbuf("kb_kr", [128, 128], BF16)
    kis = pr.sbuf("kb_ki", [128, 128], BF16)
    bsb = pr.sbuf("kb_bs", [128, 258], BF16)
    cb = pr.sbuf("kb_c", [128, 258], BF16)
    xb = pr.sbuf("kb_x", [128, 258], BF16)
    gidx = pr.s["kb_gidx"]
    ps2 = pr.psum("kb_ps2", [128, 512], F32)
    ps3 = pr.psum("kb_ps3", [128, 512], F32)
    ps4 = pr.psum("kb_ps4", [128, 512], F32)
    ps5 = pr.psum("kb_ps5", [128, 512], F32)

    @block.gpsimd
    def _(g):
        ngi = dma_in(plan, g, gidx[:], pr.d["gidx"], "cl")
        g.wait_ge(plan.sem("cl"), ngi)
        g.wait_ge(plan.sem("cc_g"), 1)
        for j in range(EPC):
            if j > 0:
                g.wait_ge(plan.sem("kbp"), 14 * (j - 1) + 2)
            plan.inc("kbg", g.indirect_dma_start(
                out=gret[:], out_offset=None, in_=pr.d["g_agout"],
                in_offset=bass.IndirectOffsetOnAxis(ap=gidx[:, 2 * j:2 * j + 1], axis=0),
                bounds_check=pr.bnd16383), 16)
            plan.inc("kbg", g.indirect_dma_start(
                out=gimt[:], out_offset=None, in_=pr.d["g_agout"],
                in_offset=bass.IndirectOffsetOnAxis(ap=gidx[:, 2 * j + 1:2 * j + 2], axis=0),
                bounds_check=pr.bnd16383), 16)

    @block.tensor
    def _(te):
        for j in range(EPC):
            te.wait_ge(plan.sem("kbg"), 32 * (j + 1))
            if j > 0:
                te.wait_ge(plan.sem("kbe"), 5 * (j - 1) + 3)
            te.ldweights(gret[:])
            plan.inc("kbp", te.matmul(ps2[:, 0:256], gret[:], s["ifab"][:],
                                      start=True, stop=False, skip_group_check=True))
            te.ldweights(gimt[:])
            plan.inc("kbp", te.matmul(ps2[:, 0:256], gimt[:], s["ifbb"][:],
                                      start=False, stop=True, skip_group_check=True))
            te.wait_ge(plan.sem("kbv"), 4 * j + 4)
            te.ldweights(s["ifbre"][:])
            plan.inc("kbp", te.matmul(ps3[:, 0:128], s["ifbre"][:], a2s[:, 0:128],
                                      start=True, stop=False, skip_group_check=True))
            te.ldweights(s["ifbimn"][:])
            plan.inc("kbp", te.matmul(ps3[:, 0:128], s["ifbimn"][:], a2s[:, 128:256],
                                      start=False, stop=True, skip_group_check=True))
            te.ldweights(s["ifbre"][:])
            plan.inc("kbp", te.matmul(ps3[:, 128:256], s["ifbre"][:], a2s[:, 128:256],
                                      start=True, stop=False, skip_group_check=True))
            te.ldweights(s["ifbim"][:])
            plan.inc("kbp", te.matmul(ps3[:, 128:256], s["ifbim"][:], a2s[:, 0:128],
                                      start=False, stop=True, skip_group_check=True))
            for r in range(2):
                src = krs if r == 0 else kis
                te.wait_ge(plan.sem("kbe"), 5 * j + 2 + r)
                te.wait_ge(plan.sem("kbe"), (5 * j + 4) if r == 1 else (5 * j if j > 0 else 0))
                te.wait_ge(plan.sem("kbe2"), 2 * j + r)
                te.ldweights(src[:])
                plan.inc("kbp", te.matmul(ps4[:, 0:258], src[:], s["f1"][:],
                                          start=True, stop=True, skip_group_check=True))
                te.wait_ge(plan.sem("kbv2"), (2 * j + r) * 4 + 4)
                te.ldweights(s["f2re"][:])
                plan.inc("kbp", te.matmul(ps5[:, 0:258], s["f2re"][:], cb[:, 0:258],
                                          start=True, stop=False, skip_group_check=True))
                te.ldweights(s["f2imn"][:])
                plan.inc("kbp", te.matmul(ps5[:, 0:129], s["f2imn"][:], cb[:, 129:258],
                                          start=False, stop=False, skip_group_check=True))
                te.ldweights(s["f2im"][:])
                plan.inc("kbp", te.matmul(ps5[:, 129:258], s["f2im"][:], cb[:, 0:129],
                                          start=False, stop=True, skip_group_check=True))

    @block.scalar
    def _(sc):
        for j in range(EPC):
            sc.wait_ge(plan.sem("kbp"), 14 * j + 2)
            sc.wait_ge(plan.sem("kbv"), 4 * j)
            plan.inc("kbe", sc.copy(a1s[:], ps2[:, 0:256]))
            sc.wait_ge(plan.sem("kbp"), 14 * j + 4)
            plan.inc("kbe", sc.copy(krs[:], ps3[:, 0:128]))
            sc.wait_ge(plan.sem("kbp"), 14 * j + 6)
            plan.inc("kbe", sc.copy(kis[:], ps3[:, 128:256]))
            for r in range(2):
                sc.wait_ge(plan.sem("kbp"), 14 * j + 7 + 4 * r)
                sc.wait_ge(plan.sem("kbv2"), (2 * j + r) * 4)
                plan.inc("kbe", sc.copy(bsb[:], ps4[:, 0:258]))
                sc.wait_ge(plan.sem("kbp"), 14 * j + 10 + 4 * r)
                sc.wait_ge(plan.sem("kbw"), 64 * (2 * j + r))
                plan.inc("kbe2", sc.copy(xb[:], ps5[:, 0:258]))

    cs1 = pr.sbuf("kb_cs1", [128, 258], BF16)
    cs2 = pr.sbuf("kb_cs2", [128, 258], BF16)

    def cmul(v, plan, sem, dst, src, tr, ti, w, selfsem="vself"):
        a, b = cs1, cs2
        plan.inc(sem, v.tensor_tensor(a[:, 0:2 * w], src, tr, op=ALU.mult))
        n = plan.inc(selfsem, v.tensor_tensor(b[:, 0:2 * w], src, ti, op=ALU.mult))
        plan.inc(sem, v.nop())
        v.wait_ge(plan.sem(selfsem), n)
        plan.inc(sem, v.tensor_tensor(dst[:, 0:w], a[:, 0:w], b[:, w:2 * w],
                                      op=ALU.subtract))
        plan.inc(sem, v.tensor_tensor(dst[:, w:2 * w], a[:, w:2 * w], b[:, 0:w],
                                      op=ALU.add))

    @block.vector
    def _(v):
        for j in range(EPC):
            v.wait_ge(plan.sem("kbe"), 5 * j + 1)
            if j > 0:
                v.wait_ge(plan.sem("kbp"), 14 * (j - 1) + 6)
            cmul(v, plan, "kbv", a2s, a1s[:, 0:256], s["iftr2"][:], s["ifti2"][:], 128)
            for r in range(2):
                v.wait_ge(plan.sem("kbe"), 5 * j + 4 + r)
                if 2 * j + r > 0:
                    pi = 14 * j + 10 if r == 1 else 14 * (j - 1) + 14
                    v.wait_ge(plan.sem("kbp"), pi)
                cmul(v, plan, "kbv2", cb, bsb[:, 0:258], s["twr2_4"][:, 0:258],
                     s["twi2_4"][:, 0:258], 129)

    @block.sync
    def _(sy):
        for j in range(EPC):
            for r in range(2):
                row = j if r == 0 else 8 + j
                sy.wait_ge(plan.sem("kbe2"), 2 * j + r + 1)
                bv = pr.d["bas_agin"][row:row + 1, :].rearrange("a (k w) -> (a k) w", w=516)
                dma_in(plan, sy, bv[:, 0:129], xb[:, 0:129], "kbw")
                dma_in(plan, sy, bv[:, 129:258], xb[:, 0:129], "kbw")
                dma_in(plan, sy, bv[:, 258:387], xb[:, 129:258], "kbw")
                dma_in(plan, sy, bv[:, 387:516], xb[:, 129:258], "kbw")

    @block.gpsimd
    def _(g):
        g.wait_ge(plan.sem("kbw"), 64 * 16)
        plan.inc("cc_b", g.collective_compute(
            "AllGather", ALU.bypass, replica_groups=[list(range(NCORES))],
            ins=[pr.d["bas_agin"].opt()], outs=[pr.d["bas_agout"].opt()]))


def emit_cmix(pr, block):
    plan = pr.plan
    s = pr.s
    NCW = 65
    TOT = 129 * 512

    def cw(c):
        return min(1024 * (c + 1), TOT) - 1024 * c

    cmixb = pr.sbuf("cm_lhs", [128, 128], BF16)
    rhs = pr.sbuf("cm_rhs", [128, 4 * 1024], BF16)
    stg = pr.sbuf("cm_stg", [128, 4 * 1024], BF16)
    cps = pr.psum("cm_ps", [128, 2048], F32)

    @block.vector
    def _(v):
        v.wait_ge(plan.sem("cl"), plan.val("cl"))
        plan.inc("cmv", v.tensor_scalar(cmixb[:], s["scmix_pre"][:], s["sign_col"][:],
                                        None, op0=ALU.mult))

    @block.sync
    def _(sy):
        def store(j):
            sy.wait_ge(plan.sem("cme"), j + 1)
            plan.inc("cmw%d" % (j % 4), sy.dma_start(
                out=pr.d["kf"][:, 1024 * j:1024 * j + cw(j)],
                in_=stg[:, (j % 4) * 1024:(j % 4) * 1024 + cw(j)]), 16)

        sy.wait_ge(plan.sem("cc_b"), 1)
        for c in range(0, NCW, 2):
            if c >= 4:
                sy.wait_ge(plan.sem("cmp"), c - 3)
            dma_in(plan, sy, rhs[:, (c % 4) * 1024:(c % 4) * 1024 + cw(c)],
                   pr.d["bas_agout"][:, 1024 * c:1024 * c + cw(c)], "cml0")
            if c >= 6:
                store(c - 6)
                store(c - 5)
        for j in range(NCW - 5, NCW):
            store(j)

    @block.gpsimd
    def _(g):
        g.wait_ge(plan.sem("cc_b"), 1)
        for c in range(1, NCW, 2):
            if c >= 4:
                g.wait_ge(plan.sem("cmp"), c - 3)
            plan.inc("cmlg", g.dma_start(
                out=rhs[:, (c % 4) * 1024:(c % 4) * 1024 + cw(c)],
                in_=pr.d["bas_agout"][:, 1024 * c:1024 * c + cw(c)]), 16)

    @block.tensor
    def _(te):
        te.wait_ge(plan.sem("cmv"), 1)
        te.ldweights(cmixb[:])
        for c in range(NCW):
            te.wait_ge(plan.sem("cml0" if c % 2 == 0 else "cmlg"),
                       16 * min(c // 2 + 2, 33 if c % 2 == 0 else 32))
            if c >= 2:
                te.wait_ge(plan.sem("cme"), c - 1)
            for hh in range(0, cw(c), 512):
                _mm = te.matmul(cps[:, (c % 2) * 1024 + hh:(c % 2) * 1024 + hh + 512],
                                cmixb[:],
                                rhs[:, (c % 4) * 1024 + hh:(c % 4) * 1024 + hh + 512],
                                start=True, stop=True, skip_group_check=True)
            plan.inc("cmp", _mm)

    @block.scalar
    def _(sc):
        for c in range(NCW):
            sc.wait_ge(plan.sem("cmp"), c + 1)
            if c >= 4:
                sc.wait_ge(plan.sem("cmw%d" % (c % 4)), 16 * ((c - 4) // 4 + 1))
            plan.inc("cme", sc.copy(stg[:, (c % 4) * 1024:(c % 4) * 1024 + cw(c)],
                                    cps[:, (c % 2) * 1024:(c % 2) * 1024 + cw(c)]))


def emit_conv(pr, block):
    plan = pr.plan
    s = pr.s
    NG = NSEQ // 4

    xin = pr.sbuf("cv_xin", [128, 4 * 128], BF16)
    xn = pr.sbuf("cv_xn", [128, 8 * 128], BF16)
    c4 = pr.sbuf("cv_c", [128, 8 * 258], BF16)
    y4 = pr.sbuf("cv_y", [128, 8 * 258], BF16)
    kf4 = pr.sbuf("cv_kf", [128, 8 * 516], BF16)
    pt4 = pr.sbuf("cv_pt", [128, 8 * 512], BF16)
    pp4 = pr.sbuf("cv_pp", [128, 8 * 256], BF16)
    hs4 = pr.sbuf("cv_hs", [128, 2 * 512], F32)
    sm4 = pr.sbuf("cv_sm", [128, 2 * 512], BF16)
    mus = pr.sbuf("cv_mu", [128, 4 * 128], BF16)
    muq = pr.sbuf("cv_muq", [128, 512], BF16)
    rsq = pr.sbuf("cv_rsq", [128, 512], BF16)
    dcol = pr.sbuf("cv_dcol", [128, 1], F32)
    psA = pr.psum("cv_ps", [128, 4096], F32)

    tw1 = pr.sbuf("cv_tw1", [128, 2 * 516], BF16)
    tw2 = pr.sbuf("cv_tw2", [128, 2 * 516], BF16)
    pw1 = pr.sbuf("cv_pw1", [128, 2 * 516], BF16)
    pw2 = pr.sbuf("cv_pw2", [128, 2 * 516], BF16)
    it1 = pr.sbuf("cv_it1", [128, 2 * 512], BF16)
    it2 = pr.sbuf("cv_it2", [128, 2 * 512], BF16)

    @block.gpsimd
    def _(g):
        g.wait_ge(plan.sem("cc_st"), 1)
        stv = pr.d["st_agout"].rearrange("(q w) c -> q w c", w=2)
        for b in range(B):
            dma_in(plan, g, mus[:, b * 128:(b + 1) * 128],
                   stv[128 * b:128 * (b + 1), 0, :], "cmu")
            dma_in(plan, g, mus[:, (2 + b) * 128:(3 + b) * 128],
                   stv[128 * b:128 * (b + 1), 1, :], "cmu")
        for sq in range(NSEQ):
            chl, b = sq // 2, sq % 2
            if sq >= 4:
                g.wait_ge(plan.sem("cln"), sq // 4)
            xv = pr.d["x_ch"][chl, b, :].rearrange("(a c) -> a c", c=128)
            dma_in(plan, g, xin[:, (sq % 4) * 128:(sq % 4 + 1) * 128], xv,
                   "cxlp%d" % ((sq // 2) % 2))

    @block.sync
    def _(sy):
        sy.wait_ge(plan.sem("cmw0"), 16 * 17)
        sy.wait_ge(plan.sem("cmw1"), 16 * 16)
        sy.wait_ge(plan.sem("cmw2"), 16 * 16)
        sy.wait_ge(plan.sem("cmw3"), 16 * 16)
        for sq in range(NSEQ):
            chl = sq // 2
            if sq >= 8:
                sy.wait_ge(plan.sem("cpw"), (sq - 8) // 2 + 1)
            dma_in(plan, sy, kf4[:, (sq % 8) * 516:(sq % 8 + 1) * 516],
                   pr.d["kf"][chl:chl + 1, :].rearrange("a (k w) -> (a k) w", w=516),
                   "cklp%d" % ((sq // 4) % 2))

    @block.vector
    def _(v):
        v.wait_ge(plan.sem("cmu"), 64)
        plan.inc("vself", v.tensor_copy(muq[:, 0:256], mus[:, 0:256]))
        plan.inc("vself", v.tensor_copy(muq[:, 256:512], mus[:, 0:256]))
        plan.inc("vself", v.tensor_copy(rsq[:, 0:256], mus[:, 256:512]))
        n = plan.inc("vself", v.tensor_copy(rsq[:, 256:512], mus[:, 256:512]))
        v.wait_ge(plan.sem("vself"), n)
        v.wait_ge(plan.sem("kbe2"), 16)
        plan.inc("vself", v.memset(psA[:, 2048 + 256:2048 + 512], 0.0))
        plan.inc("vself", v.memset(psA[:, 2560 + 256:2560 + 512], 0.0))
        v.wait_ge(plan.sem("cdp"), 1)
        n = plan.inc("vself", v.tensor_copy(dcol[:], psA[0:128, 0:1]))
        v.wait_ge(plan.sem("vself"), n)
        plan.inc("cdv", v.nop())
        B1v = psA[:, 0:1024].rearrange("p (s w) -> p s w", w=512)
        Xv = psA[:, 1024:2048].rearrange("p (s w) -> p s w", w=512)
        PTv = psA[:, 2048:3072].rearrange("p (s w) -> p s w", w=512)
        t_r = s["twr2_4"][:, 0:516].rearrange("p (s w) -> p s w", w=258)
        t_i = s["twi2_4"][:, 0:516].rearrange("p (s w) -> p s w", w=258)
        it_r = s["itr2_4"][:, 0:512].rearrange("p (s w) -> p s w", w=256)
        it_i = s["iti2_4"][:, 0:512].rearrange("p (s w) -> p s w", w=256)
        for g_i in range(NG):
            gp = g_i % 2
            v.wait_ge(plan.sem("cxlp0"), 32 * (g_i + 1))
            v.wait_ge(plan.sem("cxlp1"), 32 * (g_i + 1))
            if g_i >= 2:
                v.wait_ge(plan.sem("cpe_s1"), 4 * (g_i - 1))
                v.wait_ge(plan.sem("cepi"), g_i - 1)
            xnv4 = xn[:, (4 * g_i % 8) * 128:((4 * g_i % 8) + 4) * 128]
            nn = plan.inc("vself", v.tensor_tensor(xnv4, xin[:], muq[:],
                                                   op=ALU.subtract))
            v.wait_ge(plan.sem("vself"), nn)
            plan.inc("cln", v.tensor_tensor(xnv4, xnv4, rsq[:], op=ALU.mult))
            for h in range(2):
                cs = (4 * g_i + 2 * h) % 8
                v.wait_ge(plan.sem("cpe_s1"), 4 * g_i + 2 * h + 2)
                if g_i >= 2:
                    v.wait_ge(plan.sem("cpe_s2"), 4 * (g_i - 2) + 2 * h + 2)
                dst = c4[:, cs * 258:(cs + 2) * 258].rearrange("p (s w) -> p s w", w=258)
                a_v = tw1[:, h * 516:(h + 1) * 516].rearrange("p (s w) -> p s w", w=258)
                b_v = tw2[:, h * 516:(h + 1) * 516].rearrange("p (s w) -> p s w", w=258)
                plan.inc("ctw", v.tensor_tensor(a_v, B1v[:, :, 0:258], t_r, op=ALU.mult))
                n = plan.inc("vself", v.tensor_tensor(b_v, B1v[:, :, 0:258], t_i,
                                                      op=ALU.mult))
                v.wait_ge(plan.sem("vself"), n)
                plan.inc("ctw", v.tensor_tensor(dst[:, :, 0:129], a_v[:, :, 0:129],
                                                b_v[:, :, 129:258], op=ALU.subtract))
                plan.inc("ctw", v.tensor_tensor(dst[:, :, 129:258], a_v[:, :, 129:258],
                                                b_v[:, :, 0:129], op=ALU.add))
            for h in range(2):
                cs = (4 * g_i + 2 * h) % 8
                v.wait_ge(plan.sem("cpe_s2"), 4 * g_i + 2 * h + 2)
                v.wait_ge(plan.sem("cklp%d" % (g_i % 2)), 64 * (g_i // 2 + 1))
                if g_i >= 2:
                    v.wait_ge(plan.sem("cpe_is2"), 4 * (g_i - 2) + 2 * h + 2)
                kfp = kf4[:, cs * 516:(cs + 2) * 516].rearrange("p (s w) -> p s w", w=516)
                yp = y4[:, cs * 258:(cs + 2) * 258].rearrange("p (s w) -> p s w", w=258)
                a_v = pw1[:, h * 516:(h + 1) * 516].rearrange("p (s w) -> p s w", w=258)
                b_v = pw2[:, h * 516:(h + 1) * 516].rearrange("p (s w) -> p s w", w=258)
                plan.inc("cpw0", v.tensor_tensor(a_v, Xv[:, :, 0:258],
                                                 kfp[:, :, 0:258], op=ALU.mult))
                n = plan.inc("vself", v.tensor_tensor(b_v, Xv[:, :, 0:258],
                                                      kfp[:, :, 258:516], op=ALU.mult))
                v.wait_ge(plan.sem("vself"), n)
                plan.inc("cpw0", v.tensor_tensor(yp[:, :, 0:129], a_v[:, :, 0:129],
                                                 b_v[:, :, 129:258], op=ALU.subtract))
                plan.inc("cpw", v.tensor_tensor(yp[:, :, 129:258], a_v[:, :, 129:258],
                                                b_v[:, :, 0:129], op=ALU.add))
            v.wait_ge(plan.sem("cet"), 4 * g_i + 4)
            if g_i >= 2:
                v.wait_ge(plan.sem("cpe_is1"), g_i - 1)
            ptv = pt4[:, gp * 2048:(gp + 1) * 2048].rearrange("p (g w) -> p g w", w=512)
            ppr = pp4[:, gp * 1024:(gp + 1) * 1024].rearrange("p (g w) -> p g w", w=256)
            a4 = it1[:, 0:1024].rearrange("p (g w) -> p g w", w=256)
            b4 = it2[:, 0:1024].rearrange("p (g w) -> p g w", w=256)
            plan.inc("citw0", v.tensor_tensor(
                a4, ptv[:, :, 0:256],
                s["itr2_4"][:].rearrange("p (g w) -> p g w", w=256), op=ALU.mult))
            n = plan.inc("vself", v.tensor_tensor(
                b4, ptv[:, :, 0:256],
                s["iti2_4"][:].rearrange("p (g w) -> p g w", w=256), op=ALU.mult))
            v.wait_ge(plan.sem("vself"), n)
            plan.inc("citw0", v.tensor_tensor(ppr[:, :, 0:128], a4[:, :, 0:128],
                                              b4[:, :, 128:256], op=ALU.subtract))
            plan.inc("citw", v.tensor_tensor(ppr[:, :, 128:256], a4[:, :, 128:256],
                                             b4[:, :, 0:128], op=ALU.add))
            v.wait_ge(plan.sem("cpe_is1"), g_i + 1)
            if g_i >= 2:
                v.wait_ge(plan.sem("cgel"), g_i - 1)
            plan.inc("cepi", v.scalar_tensor_tensor(
                hs4[:, gp * 512:(gp + 1) * 512],
                xn[:, (4 * g_i % 8) * 128:((4 * g_i % 8) + 4) * 128], dcol[:],
                psA[:, 512 * (6 + gp):512 * (7 + gp)], op0=ALU.mult, op1=ALU.add))

    @block.tensor
    def _(te):
        te.wait_ge(plan.sem("cl"), plan.val("cl"))
        plan.inc("cdp", te.matmul(psA[0:128, 0:1], s["ones32r"][0:1, :],
                                  s["p_Dv"][0:1, 0:1], start=True, stop=True,
                                  skip_group_check=True))
        for g_i in range(NG):
            gp = g_i % 2
            for k in range(4):
                sq = 4 * g_i + k
                te.wait_ge(plan.sem("cln"), sq // 4 + 1)
                if sq >= 2:
                    te.wait_ge(plan.sem("ctw"), 3 * (sq // 2))
                if sq == 0:
                    te.wait_ge(plan.sem("cdv"), 1)
                te.ldweights(xn[:, (sq % 8) * 128:(sq % 8 + 1) * 128])
                plan.inc("cpe_s1", te.matmul(psA[:, 512 * (sq % 2):512 * (sq % 2) + 258],
                                             xn[:, (sq % 8) * 128:(sq % 8 + 1) * 128],
                                             s["f1"][:], start=True, stop=True,
                                             skip_group_check=True))
            for half in range(2):
                te.wait_ge(plan.sem("ctw"), 3 * (2 * g_i + half + 1))
                for nm, rhs_lo, rhs_hi, out_lo, acc in [
                        ("f2re", 0, 258, 0, "start"),
                        ("f2imn", 129, 258, 0, "mid"),
                        ("f2im", 0, 129, 129, "stop")]:
                    te.ldweights(s[nm][:])
                    for k in range(2 * half, 2 * half + 2):
                        sq = 4 * g_i + k
                        if nm == "f2re" and sq >= 2:
                            te.wait_ge(plan.sem("cpw"), sq // 2)
                        slot = (sq % 8) * 258
                        cveiw = c4[:, slot + rhs_lo: slot + rhs_hi]
                        _mm = te.matmul(psA[:, 512 * (2 + sq % 2) + out_lo:
                                            512 * (2 + sq % 2) + out_lo + (rhs_hi - rhs_lo)],
                                 s[nm][:], cveiw,
                                 start=(nm == "f2re"), stop=(nm == "f2im"),
                                 skip_group_check=True)
                        if nm == "f2im":
                            plan.inc("cpe_s2", _mm)
            for k in range(4):
                sq = 4 * g_i + k
                te.wait_ge(plan.sem("cpw"), 2 * g_i + k // 2 + 1)
                if sq >= 2:
                    te.wait_ge(plan.sem("cet"), sq - 1)
                slot = (sq % 8) * 258
                pts = psA[:, 512 * (4 + sq % 2):512 * (5 + sq % 2)]
                te.ldweights(y4[:, slot:slot + 128])
                te.matmul(pts[:, 0:256], y4[:, slot:slot + 128],
                          s["ra"][:], start=True, stop=False, skip_group_check=True)
                te.ldweights(y4[:, slot + 129:slot + 257])
                te.matmul(pts[:, 0:256], y4[:, slot + 129:slot + 257],
                          s["rb"][:], start=False, stop=True, skip_group_check=True)
                te.ldweights(y4[:, slot + 128:slot + 129])
                te.matmul(pts[0:1, 256:512], y4[:, slot + 128:slot + 129],
                          s["rae"][:], start=True, stop=False, skip_group_check=True)
                te.ldweights(y4[:, slot + 257:slot + 258])
                plan.inc("cpe_is2", te.matmul(pts[0:1, 256:512], y4[:, slot + 257:slot + 258],
                                              s["rbe"][:], start=False, stop=True,
                                              skip_group_check=True))
            te.wait_ge(plan.sem("citw"), g_i + 1)
            if g_i >= 2:
                te.wait_ge(plan.sem("cepi"), g_i - 1)
            hb = psA[:, 512 * (6 + gp):512 * (7 + gp)]
            gpp = g_i % 2
            for nm, src_off, whalf in [("g1re", 0, 0), ("g1imn", 128, 0),
                                       ("g1er", 0, 1), ("g1ein", 128, 1)]:
                te.ldweights(s[nm][:])
                for k in range(4):
                    sq = 4 * g_i + k
                    if whalf == 0:
                        rhsv = pp4[:, gpp * 1024 + (sq % 4) * 256 + src_off:
                                   gpp * 1024 + (sq % 4) * 256 + src_off + 128]
                    else:
                        rhsv = pt4[0:1, gpp * 2048 + (sq % 4) * 512 + 256 + src_off:
                                   gpp * 2048 + (sq % 4) * 512 + 256 + src_off + 128]
                    _mm = te.matmul(hb[:, (sq % 4) * 128:(sq % 4 + 1) * 128],
                                    s[nm][:], rhsv,
                                    start=(nm == "g1re"), stop=(nm == "g1ein"),
                                    skip_group_check=True)
                    if nm == "g1ein" and k == 3:
                        plan.inc("cpe_is1", _mm)

    @block.scalar
    def _(sc):
        for g_i in range(NG):
            gp = g_i % 2
            for k in range(4):
                sq = 4 * g_i + k
                sc.wait_ge(plan.sem("cpe_is2"), sq + 1)
                if sq >= 8:
                    sc.wait_ge(plan.sem("cpe_is1"), (sq - 8) // 4 + 1)
                plan.inc("cet", sc.copy(pt4[:, (sq % 8) * 512:(sq % 8 + 1) * 512],
                                        psA[:, 512 * (4 + sq % 2):512 * (5 + sq % 2)]))
            sc.wait_ge(plan.sem("cepi"), g_i + 1)
            if g_i >= 2:
                sc.wait_ge(plan.sem("cst%d" % (g_i % 2)), 64 * (g_i // 2))
            ng = plan.inc("cgel", sc.activation(sm4[:, gp * 512:(gp + 1) * 512],
                                                hs4[:, gp * 512:(gp + 1) * 512], AF.Gelu))
            sc.wait_ge(plan.sem("cgel"), ng)
            for k in range(4):
                sq = 4 * g_i + k
                chl, b = sq // 2, sq % 2
                dst = pr.d["ssm_agin"][chl, b * L:(b + 1) * L].rearrange(
                    "(a c) -> a c", c=128)
                plan.inc("cst%d" % (g_i % 2), sc.dma_start(
                    out=dst, in_=sm4[:, gp * 512 + k * 128:gp * 512 + (k + 1) * 128]), 16)

    @block.gpsimd
    def _(g):
        g.wait_ge(plan.sem("cst0"), 64 * (NG // 2))
        g.wait_ge(plan.sem("cst1"), 64 * (NG // 2))
        plan.inc("cc_ssm", g.collective_compute(
            "AllGather", ALU.bypass, replica_groups=[list(range(NCORES))],
            ins=[pr.d["ssm_agin"].opt()], outs=[pr.d["ssm_agout"].opt()]))


def _shard_ssm_inputs(core, x, prm):
    c = consts()
    hc = slice(CPC * core, CPC * (core + 1))
    xf = np.ascontiguousarray(x.reshape(T, HD))
    m = {
        "x_full": xf,
        "x_ts": np.ascontiguousarray(xf[TPC * core:TPC * (core + 1)]),
        "x_ch": np.ascontiguousarray(x[:, :, hc].transpose(2, 0, 1)),
        "B_re": prm["B_re"].reshape(NMODE, 1), "B_im": prm["B_im"].reshape(NMODE, 1),
        "P_re": prm["P_re"].reshape(NMODE, 1), "P_im": prm["P_im"].reshape(NMODE, 1),
        "Q_re": prm["Q_re"].reshape(NMODE, 1), "Q_im": prm["Q_im"].reshape(NMODE, 1),
        "diag_re": prm["diag_re"].reshape(NMODE, 1),
        "diag_im": prm["diag_im"].reshape(NMODE, 1),
        "step": prm["step"].reshape(1, 1), "Dv": prm["D"].reshape(1, 1),
        "C_re": np.ascontiguousarray(prm["C_re"][hc]),
        "C_im": np.ascontiguousarray(prm["C_im"][hc]),
        "w_re": c["w_re"][2048 * core:2048 * (core + 1)].reshape(1, 2048),
        "w_im": c["w_im"][2048 * core:2048 * (core + 1)].reshape(1, 2048),
        "tz_re": c["tz_re"][2048 * core:2048 * (core + 1)].reshape(1, 2048),
        "tz_im": c["tz_im"][2048 * core:2048 * (core + 1)].reshape(1, 2048),
    }
    cm = np.zeros((128, CPC), np.float32)
    for r in range(8):
        for j in range(8):
            cm[16 * r + j] = prm["C_re"][hc][:, 8 * r + j]
            cm[16 * r + 8 + j] = prm["C_im"][hc][:, 8 * r + j]
    m["cmix_pre"] = cm
    gi = np.zeros((128, 16), np.uint32)
    p = np.arange(128)
    for j in range(EPC):
        base = (p // 16) * 2048 + (8 * core + j) * 16 + (p % 16)
        gi[:, 2 * j] = base
        gi[:, 2 * j + 1] = base + 1024
    m["gidx"] = gi
    for nm in CONST_NAMES:
        arr = c[nm]
        m["c_" + nm] = np.asarray(arr)
    return m


def _build_ssm_test():
    import contextlib
    nc = bass.Bass("TRN2", target_bir_lowering=False)
    stack = contextlib.ExitStack()
    nc._test_stack = stack
    pr = Prog(nc, stack, 0)
    declare_conv_io(pr)
    out = pr.dram("ssm_out", [CPC, T], BF16, kind="ExternalOutput")
    agpeek = pr.dram("agpeek", [128, 128], BF16, kind="ExternalOutput")
    with nc.Block() as block:
        plan = pr.plan

        @block.gpsimd
        def _(g):
            conv_const_loads(pr, g)

        pr.push()
        emit_stats(pr, block)
        st_fence = [("sv", plan.val("sv")), ("ss2", plan.val("ss2")),
                    ("sve", plan.val("sve")), ("sgi", 16)]
        pr.pop()
        pr.push()
        _phase_wait(block, plan, pr, st_fence, [])
        pr.sbuf("kb_gidx", [128, 16], U32)
        pr.push()
        emit_kg(pr, block)
        pr.pop()
        emit_basis(pr, block)
        emit_cmix(pr, block)
        cm_fence = [("cmp", plan.val("cmp")), ("cme", plan.val("cme")),
                    ("cmw0", plan.val("cmw0")), ("cmw1", plan.val("cmw1")),
                    ("cmw2", plan.val("cmw2")), ("cmw3", plan.val("cmw3")),
                    ("kbe", plan.val("kbe")), ("kbe2", plan.val("kbe2")),
                    ("cmv", 1)]
        pr.pop()
        pr.push()
        _phase_wait(block, plan, pr, cm_fence, [])
        emit_conv(pr, block)
        pr.pop()

        @block.gpsimd
        def _(g):
            g.wait_ge(plan.sem("cc_ssm"), 1)
            dma_in(plan, g, out, pr.d["ssm_agin"], "fin")
            dma_in(plan, g, agpeek, pr.d["ssm_agout"][0:128, 0:128], "fin")
            g.wait_ge(plan.sem("fin"), 32)
    stack.close()
    return nc


def ssm_test_kernel(x, prm):
    nc = _build_ssm_test()
    in_maps = [_shard_ssm_inputs(cc, x, prm) for cc in range(NCORES)]
    res = run_bass_kernel_spmd(nc, in_maps, core_ids=list(range(NCORES)))
    return res


def declare_ffn_io(pr):
    pr.dram("gate_w", [HD, E], F32, kind="ExternalInput")
    pr.dram("shared_gate_w", [HD, 1], F32, kind="ExternalInput")
    pr.dram("sguw", [HD, 2 * ISH], F32, kind="ExternalInput")
    pr.dram("sdw", [ISH, HD], F32, kind="ExternalInput")
    pr.dram("egu_c", [EPC, HD, 2 * IEX], F32, kind="ExternalInput")
    pr.dram("edw_c", [EPC, IEX, HD], F32, kind="ExternalInput")
    pr.dram("rank_off", [1, 1], F32, kind="ExternalInput")
    pr.dram("tri_rk", [8, 1], F32, kind="ExternalInput")
    pr.dram("rank_i", [1, 1], I32, kind="ExternalInput")
    pr.dram("xnfm", [HD, TPC], BF16)
    pr.dram("idwn", [E * CAP, 4], F32)
    pr.dram("idwn_rs", [EPC * CAP, 4], F32)
    pr.dram("cnt_agin", [64, 1], F32)
    pr.dram("cnt_agout", [512, 1], F32)
    pr.dram("moe_out", [EPC * CAP, HD], BF16)
    pr.dram("moe_all", [E * CAP, HD], BF16)
    pr.dram("ssm_my", [HD, TPC], BF16)


def emit_routing(pr, block):
    plan = pr.plan
    s = pr.s
    nt = TPC // 128

    xts = pr.sbuf("rt_x", [128, 2 * HD], F32)
    xnb = pr.sbuf("rt_xn", [128, HD], BF16)
    fmst = pr.sbuf("rt_fm", [128, 2 * HD], BF16)
    lgt = pr.sbuf("rt_lg", [128, 64], F32)
    masked = pr.sbuf("rt_msk", [128, 64], F32)
    chos = pr.sbuf("rt_ch", [128, 64], F32)
    top8v = pr.sbuf("rt_t8v", [128, 8], F32)
    top8i = pr.sbuf("rt_t8i", [128, 8], U32)
    exp8 = pr.sbuf("rt_e8", [128, 8], F32)
    scr = pr.sbuf("rt_scr", [128, 8], F32)
    idx_all = pr.sbuf("rt_idxa", [128, 8 * nt], F32)
    wn_all = pr.sbuf("rt_wna", [128, 8 * nt], F32)
    chT = pr.sbuf("rt_chT", [64, TPC], F32)
    zt64 = pr.sbuf("rt_z64", [64, TPC], BF16)
    gfull = pr.sbuf("rt_gf", [64, TPC], F32)
    gtm = pr.sbuf("rt_gtm", [128, 64], F32)
    gsel = pr.sbuf("rt_gsel", [128, 8], F32)
    dstf = pr.sbuf("rt_dstf", [128, 8], F32)
    intl = pr.sbuf("rt_intl", [128, 64], F32)
    chos8 = pr.sbuf("rt_ch8", [128, 512], F32)
    msk8 = pr.sbuf("rt_mk8", [128, 512], F32)
    iota64 = pr.sbuf("rt_iota64", [128, 64], F32)
    iotap = pr.sbuf("rt_iotap", [128, 1], F32)
    rko = pr.sbuf("rt_rko", [128, 1], F32)
    cnt8 = pr.sbuf("rt_cnt8", [8, 64], F32)
    cntc = pr.sbuf("rt_cntc", [64, 1], F32)
    pcol = pr.sbuf("rt_pcol", [64, 1], F32)
    gwb = pr.sbuf("rt_gw", [128, 8 * 64], BF16)
    sgwb = pr.sbuf("rt_sgw", [128, 8], BF16)
    ztile = pr.sbuf("rt_ztile", [128, 1024], F32)
    rof = pr.sbuf("rt_rof", [1, 1], F32)
    trk = pr.sbuf("rt_trk", [8, 1], F32)
    pbf = pr.psum("rt_pbf", [128, 512], BF16)
    pbf2 = pr.psum("rt_pbf2", [128, 512], BF16)
    psf = pr.psum("rt_psf", [128, 512], F32)
    psg = pr.psum("rt_psg", [128, 64], F32)
    dsti_all = pr.s["dsti_all"]
    sg_all = pr.s["sg_all"]
    VS = "vself"

    @block.vector
    def _(v):
        def vop(instr, k=1):
            n = plan.inc(VS, instr, k)
            v.wait_ge(plan.sem(VS), n)
            return n
        v.wait_ge(plan.sem("ss2"), nt)
        vop(v.memset(ztile[:], 0.0))
        n = vop(v.memset(zt64[:], 0.0))
        pr.marks["rt_zt"] = n
        v.wait_ge(plan.sem("rgp"), 2)
        v.wait_ge(plan.sem("rt_rko_p"), 1)
        vop(v.tensor_copy(rko[:], psf[0:128, 0:1]))
        plan.inc("rt_rko_done", v.nop())
        for i in range(nt):
            v.wait_ge(plan.sem("rxl%d" % (i % 2)), 16 * (i // 2 + 1))
            n = vop(v.tensor_scalar(xnb[:], xts[:, (i % 2) * HD:(i % 2 + 1) * HD],
                                    s["st_out"][:, 2 * i:2 * i + 1],
                                    s["st_out"][:, 2 * i + 1:2 * i + 2],
                                    op0=ALU.subtract, op1=ALU.mult))
            pr.marks["rt_ln_%d" % i] = n
            v.wait_ge(plan.sem("rt_lgev"), i + 1)
            vop(v.max(out=top8v[:], in_=lgt[:]))
            vop(v.max_index(top8i[:], top8v[:], lgt[:]))
            plan.inc(VS, v.tensor_copy(idx_all[:, 8 * i:8 * i + 8], top8i[:]))
            plan.inc(VS, v.tensor_copy(masked[:], lgt[:]))
            n = plan.inc(VS, v.tensor_scalar(scr[:, 3:4], top8v[:, 0:1], -1.0, None,
                                             op0=ALU.mult))
            v.wait_ge(plan.sem(VS), n)
            pr.marks["rt_nm_%d" % i] = n
            vop(v.match_replace(masked[:], in_to_replace=top8v[:], in_values=masked[:],
                                imm_value=-3.0e38))
            v.wait_ge(plan.sem("rt_exp"), i + 1)
            plan.inc(VS, v.reduce_sum(scr[:, 4:5], exp8[:], axis=mybir.AxisListType.X))
            n = plan.inc(VS, v.tensor_tensor(chos[:], lgt[:], masked[:],
                                             op=ALU.not_equal))
            v.wait_ge(plan.sem(VS), n)
            pr.marks["rt_ch_%d" % i] = n
            vop(v.reciprocal(scr[:, 5:6], scr[:, 4:5]))
            vop(v.tensor_scalar(wn_all[:, 8 * i:8 * i + 8], exp8[:], scr[:, 5:6],
                                None, op0=ALU.mult))
            v.wait_ge(plan.sem("rt_chT"), i + 1)
            vop(v.tensor_copy(chT[:, 128 * i:128 * (i + 1)], psf[0:64, 0:128]))
            plan.inc("rt_chTev", v.nop())
        pr.marks["rt_sga"] = plan.val(VS)
        vop(v.tensor_tensor_scan(gfull[:], zt64[:], chT[:], 0.0,
                                 op0=ALU.add, op1=ALU.add))
        n = vop(v.tensor_copy(cntc[:], gfull[:, TPC - 1:TPC]))
        pr.marks["rt_cnt"] = n
        v.wait_ge(plan.sem("rt_pc"), 1)
        vop(v.tensor_copy(pcol[:], psf[0:64, 300:301]))
        vop(v.tensor_scalar(gfull[:], gfull[:], pcol[:], None, op0=ALU.add))
        n = vop(v.tensor_tensor(gfull[:], gfull[:], chT[:], op=ALU.subtract))
        pr.marks["rt_gf"] = n
        for i in range(nt):
            v.wait_ge(plan.sem("rt_gtm"), i + 1)
            if i >= 2:
                v.wait_ge(plan.sem("rsc%s" % ("" if i % 2 == 0 else "1")),
                          16 * TOPK * (i // 2))
            sl = (i % 2) * 32
            iv = intl[:, sl:sl + 32].rearrange("p (k f) -> p f k", f=4)
            plan.inc(VS, v.tensor_copy(gtm[:], psf[0:128, 128:192]))
            for k in range(TOPK):
                plan.inc(VS, v.tensor_scalar(chos8[:, 64 * k:64 * (k + 1)], iota64[:],
                                             idx_all[:, 8 * i + k:8 * i + k + 1], None,
                                             op0=ALU.is_equal))
            plan.inc(VS, v.tensor_scalar(dstf[:], idx_all[:, 8 * i:8 * i + 8], 512.0,
                                         None, op0=ALU.mult))
            plan.inc(VS, v.tensor_scalar(scr[:, 0:1], iotap[:], rko[:],
                                         float(128 * i), op0=ALU.add, op1=ALU.add))
            plan.inc(VS, v.tensor_copy(iv[:, 1, :], wn_all[:, 8 * i:8 * i + 8]))
            plan.inc(VS, v.tensor_copy(iv[:, 2, :],
                     s["st_out"][:, 2 * i:2 * i + 1].to_broadcast([128, 8])))
            nA = plan.inc(VS, v.tensor_copy(iv[:, 3, :],
                          s["st_out"][:, 2 * i + 1:2 * i + 2].to_broadcast([128, 8])))
            v.wait_ge(plan.sem(VS), nA)
            plan.inc("rt_gtmev", v.nop())
            for k in range(TOPK):
                plan.inc(VS, v.tensor_tensor(msk8[:, 64 * k:64 * (k + 1)], gtm[:],
                                             chos8[:, 64 * k:64 * (k + 1)], op=ALU.mult))
            nB = plan.inc(VS, v.tensor_copy(iv[:, 0, :],
                                            scr[:, 0:1].to_broadcast([128, 8])))
            v.wait_ge(plan.sem(VS), nB)
            for k in range(TOPK):
                plan.inc(VS, v.reduce_sum(gsel[:, k:k + 1],
                                          msk8[:, 64 * k:64 * (k + 1)],
                                          axis=mybir.AxisListType.X))
            nC = plan.val(VS)
            v.wait_ge(plan.sem(VS), nC)
            plan.inc(VS, v.tensor_tensor(dstf[:], dstf[:], gsel[:], op=ALU.add))
            nD = plan.inc(VS, v.tensor_scalar(scr[:, 0:8], gsel[:], float(CAP),
                                              40000.0, op0=ALU.is_ge, op1=ALU.mult))
            v.wait_ge(plan.sem(VS), nD)
            nE = plan.inc(VS, v.tensor_tensor(dsti_all[:, 8 * i:8 * i + 8], dstf[:],
                                              scr[:, 0:8], op=ALU.add))
            v.wait_ge(plan.sem(VS), nE)
            pr.marks["rt_intl_%d" % i] = nE
        pr.marks["rt_v_fin"] = plan.val(VS)


    @block.scalar
    def _(sc):
        for i in range(nt):
            for h in range(2):
                sc.wait_ge(plan.sem("rt_tp"), 8 * i + 4 * (h + 1))
                if i >= 2:
                    sc.wait_ge(plan.sem("rfw%d" % (i % 2)), 128 * (i // 2))
                pb = pbf if h == 0 else pbf2
                plan.inc("rt_fmev", sc.copy(
                    fmst[:, (i % 2) * HD + 512 * h:(i % 2) * HD + 512 * (h + 1)],
                    pb[:, 0:512]))
            sc.wait_ge(plan.sem("rt_lg"), i + 1)
            plan.inc("rt_lgev", sc.copy(lgt[:], psg[:, 0:64]))
            sc.wait_ge(plan.sem("rt_sg"), i + 1)
            plan.inc("rt_sgev", sc.copy(sg_all[:, i:i + 1], psf[:, 256:257]))
            sc.wait_ge(plan.sem(VS), pr.marks["rt_nm_%d" % i])
            plan.inc("rt_exp", sc.activation(exp8[:], top8v[:], AF.Exp,
                                             bias=scr[:, 3:4]))
        sc.wait_ge(plan.sem(VS), pr.marks["rt_sga"])
        plan.inc("rt_sig", sc.activation(sg_all[:], sg_all[:], AF.Sigmoid))

    @block.tensor
    def _(te):
        te.wait_ge(plan.sem("cl"), plan.val("cl"))
        te.wait_ge(plan.sem("rgl"), 64)
        plan.inc("rt_rko_p", te.matmul(psf[0:128, 0:1], s["ones32r"][0:1, :],
                                       rof[0:1, 0:1], start=True, stop=True,
                                       skip_group_check=True))
        for i in range(nt):
            te.wait_ge(plan.sem(VS), pr.marks["rt_ln_%d" % i])
            for cch in range(8):
                h = cch // 4
                if cch % 4 == 0 and i >= 1:
                    te.wait_ge(plan.sem("rt_fmev"), 2 * (i - 1) + h + 1)
                pb = pbf if h == 0 else pbf2
                plan.inc("rt_tp", te.transpose(
                    pb[:, 128 * (cch % 4):128 * (cch % 4 + 1)],
                    xnb[:, 128 * cch:128 * (cch + 1)], s["ident"][:]))
            te.wait_ge(plan.sem("rt_fmev"), 2 * (i + 1))
            if i >= 1:
                te.wait_ge(plan.sem("rt_lgev"), i)
                te.wait_ge(plan.sem("rt_sgev"), i)
            for cch in range(8):
                lhs = fmst[:, (i % 2) * HD + 128 * cch:(i % 2) * HD + 128 * (cch + 1)]
                te.ldweights(lhs)
                _m1 = te.matmul(psg[:, 0:64], lhs, gwb[:, 64 * cch:64 * (cch + 1)],
                                start=(cch == 0), stop=(cch == 7),
                                skip_group_check=True)
                _m2 = te.matmul(psf[:, 256:257], lhs, sgwb[:, cch:cch + 1],
                                start=(cch == 0), stop=(cch == 7),
                                skip_group_check=True)
                if cch == 7:
                    plan.inc("rt_lg", _m1)
                    plan.inc("rt_sg", _m2)
            te.wait_ge(plan.sem(VS), pr.marks["rt_ch_%d" % i])
            if i == 0:
                te.wait_ge(plan.sem("rt_rko_done"), 1)
            if i >= 1:
                te.wait_ge(plan.sem("rt_chTev"), i)
            plan.inc("rt_chT", te.transpose(psf[0:64, 0:128], chos[:], s["ident32"][:]))
        te.wait_ge(plan.sem("rcl"), 16)
        plan.inc("rt_pc", te.matmul(psf[0:64, 300:301], cnt8[:], trk[:],
                                    start=True, stop=True, skip_group_check=True))
        te.wait_ge(plan.sem(VS), pr.marks["rt_gf"])
        for i in range(nt):
            if i >= 1:
                te.wait_ge(plan.sem("rt_gtmev"), i)
            plan.inc("rt_gtm", te.transpose(psf[0:128, 128:192],
                                            gfull[:, 128 * i:128 * (i + 1)],
                                            s["ident32"][0:64, 0:64]))

    @block.sync
    def _(sy):
        sy.wait_ge(plan.sem(VS), pr.marks["rt_zt"])
        plan.inc("rzw", sy.dma_start(
            out=pr.d["idwn"].rearrange("(a b) c -> a (b c)", a=128), in_=ztile[:]), 16)
        for i in range(nt):
            if i >= 2:
                sy.wait_ge(plan.sem(VS), pr.marks["rt_ln_%d" % (i - 2)])
            dma_in(plan, sy, xts[:, (i % 2) * HD:(i % 2 + 1) * HD],
                   pr.d["x_ts"][128 * i:128 * (i + 1), :], "rxl%d" % (i % 2))
            sy.wait_ge(plan.sem("rt_fmev"), 2 * (i + 1))
            fv = pr.d["xnfm"].rearrange("(c p) t -> c p t", p=128)
            for cch in range(8):
                plan.inc("rfw%d" % (i % 2), sy.dma_start(
                    out=fv[cch, :, 128 * i:128 * (i + 1)],
                    in_=fmst[:, (i % 2) * HD + 128 * cch:
                             (i % 2) * HD + 128 * (cch + 1)]), 16)
        sy.wait_ge(plan.sem(VS), pr.marks["rt_cnt"])
        dma_in(plan, sy, pr.d["cnt_agin"], cntc[:], "rcw")
        sy.wait_ge(plan.sem("cc_cnt"), 1)
        dma_in(plan, sy, cnt8[:],
               pr.d["cnt_agout"].rearrange("(r e) a -> r (e a)", e=64), "rcl")

    @block.gpsimd
    def _(g):
        g.wait_ge(plan.sem("ss2"), nt)
        plan.inc("rgp", g.iota(iota64[:], pattern=[[1, 64]], base=0,
                               channel_multiplier=0,
                               allow_small_or_imprecise_dtypes=True))
        plan.inc("rgp", g.iota(iotap[:], pattern=[[1, 1]], base=0,
                               channel_multiplier=1,
                               allow_small_or_imprecise_dtypes=True))
        dma_in(plan, g, gwb[:].rearrange("p (c e) -> p c e", e=64),
               pr.d["gate_w"].rearrange("(c p) e -> p c e", p=128), "rgl")
        plan.inc("rgl", g.dma_start(
            out=sgwb[:].rearrange("p (c e) -> p c e", e=1),
            in_=pr.d["shared_gate_w"].rearrange("(c p) e -> p c e", p=128),
            allow_slow_non_contiguous=True), 16)
        dma_in(plan, g, rof[:], pr.d["rank_off"], "rgl")
        dma_in(plan, g, trk[:], pr.d["tri_rk"], "rgl")
        g.wait_ge(plan.sem("rcw"), 16)
        plan.inc("cc_cnt", g.collective_compute(
            "AllGather", ALU.bypass, replica_groups=[list(range(NCORES))],
            ins=[pr.d["cnt_agin"].opt()], outs=[pr.d["cnt_agout"].opt()]))
        g.wait_ge(plan.sem("rzw"), 16)
        for i in range(nt):
            g.wait_ge(plan.sem(VS), pr.marks["rt_intl_%d" % i])
            for k in range(TOPK):
                plan.inc("rsc%s" % ("" if i % 2 == 0 else "1"), g.indirect_dma_start(
                    out=pr.d["idwn"],
                    out_offset=bass.IndirectOffsetOnAxis(
                        ap=dsti_all[:, 8 * i + k:8 * i + k + 1], axis=0),
                    in_=intl[:, (i % 2) * 32 + 4 * k:(i % 2) * 32 + 4 * k + 4],
                    in_offset=None,
                    bounds_check=pr.bnd32767, oob_is_err=False), 16)
        g.wait_ge(plan.sem("rsc"), 16 * TOPK * (nt // 2))
        g.wait_ge(plan.sem("rsc1"), 16 * TOPK * (nt // 2))
        plan.inc("cc_rs", g.collective_compute(
            "ReduceScatter", ALU.add, replica_groups=[list(range(NCORES))],
            ins=[pr.d["idwn"].opt()], outs=[pr.d["idwn_rs"].opt()]))

def emit_moe(pr, block):
    plan = pr.plan
    s = pr.s

    it4 = pr.sbuf("mo_it", [128, 16], F32)
    nmr = pr.sbuf("mo_nmr", [128, 8], F32)
    idxu = pr.sbuf("mo_idxu", [128, 4], U32)
    xrow = pr.sbuf("mo_xrow", [128, 1024], BF16)
    xnr = pr.sbuf("mo_xnr", [128, 1024], BF16)
    xnT = pr.sbuf("mo_xnT", [128, 8 * 512], BF16)
    wegu = pr.sbuf("mo_wegu", [128, 8 * 1024], BF16)
    wedw = pr.sbuf("mo_wedw", [128, 4 * 1024], BF16)
    sil = pr.sbuf("mo_sil", [128, 4 * 512], BF16)
    actT = pr.sbuf("mo_act", [128, 4 * 512], BF16)
    outb = pr.sbuf("mo_out", [128, 1024], BF16)
    pbf = pr.psum("mo_pbf", [128, 128], BF16)
    pbf2 = pr.psum("mo_pbf2", [128, 128], BF16)
    psg1 = pr.psum("mo_ps1", [128, 512], F32)
    psg2 = pr.psum("mo_ps2", [128, 512], F32)
    psd = pr.psum("mo_psd", [128, 1024], F32)
    VS = "vself"

    @block.vector
    def _(v):
        def vop(instr, k=1):
            n = plan.inc(VS, instr, k)
            v.wait_ge(plan.sem(VS), n)
            return n
        for e in range(EPC):
            for q in range(4):
                v.wait_ge(plan.sem("mil%d" % (q % 2)), 16 * ((4 * e + q) // 2 + 1))
                vop(v.tensor_scalar(nmr[:, 0:1], it4[:, 4 * q:4 * q + 1], 0.0, None,
                                    op0=ALU.is_equal))
                vop(v.tensor_scalar(nmr[:, 1:2], it4[:, 4 * q:4 * q + 1], -1.0, None,
                                    op0=ALU.add))
                vop(v.scalar_tensor_tensor(nmr[:, 1:2], nmr[:, 0:1], 50000.0,
                                           nmr[:, 1:2], op0=ALU.mult, op1=ALU.add))
                n = vop(v.tensor_copy(idxu[:, q:q + 1], nmr[:, 1:2]))
                n = vop(v.memset(xrow[:], 0.0))
                pr.marks["mo_gw_%d_%d" % (e, q)] = n
                v.wait_ge(plan.sem("mgl"), 16 * (4 * e + q + 1))
                vop(v.tensor_tensor(nmr[:, 2:3], it4[:, 4 * q + 2:4 * q + 3],
                                    it4[:, 4 * q + 3:4 * q + 4], op=ALU.mult))
                vop(v.tensor_scalar(nmr[:, 3:4], nmr[:, 2:3], -1.0, None, op0=ALU.mult))
                n = vop(v.tensor_scalar(xnr[:], xrow[:],
                                        it4[:, 4 * q + 3:4 * q + 4],
                                        nmr[:, 3:4], op0=ALU.mult, op1=ALU.add))
                pr.marks["mo_ln_%d_%d" % (e, q)] = n
                for cch in range(8):
                    v.wait_ge(plan.sem("mo_tp"), (4 * e + q) * 8 + cch + 1)
                    pb = pbf if cch % 2 == 0 else pbf2
                    n = vop(v.tensor_copy(
                        xnT[:, 512 * cch + 128 * q:512 * cch + 128 * (q + 1)],
                        pb[:, 0:128]))
                    plan.inc("mo_tpev", v.nop())
                pr.marks["mo_xnT_%d_%d" % (e, q)] = plan.val(VS)
            for ip in range(4):
                v.wait_ge(plan.sem("mo_ev"), 8 * e + ip + 5)
                n = vop(v.scalar_tensor_tensor(
                    actT[:, 512 * ip:512 * (ip + 1)],
                    actT[:, 512 * ip:512 * (ip + 1)], 1.0,
                    sil[:, 512 * ip:512 * (ip + 1)], op0=ALU.mult, op1=ALU.mult))
                pr.marks["mo_act_%d_%d" % (e, ip)] = n
            for q in range(4):
                v.wait_ge(plan.sem("mo_dn"), 4 * e + q + 1)
                v.wait_ge(plan.sem("mow"), 16 * (4 * e + q))
                n = vop(v.tensor_scalar(outb[:], psd[:, 0:1024],
                                        it4[:, 4 * q + 1:4 * q + 2], None, op0=ALU.mult))
                plan.inc("mo_dnev", v.nop())
                pr.marks["mo_out_%d_%d" % (e, q)] = n
            for q in range(4):
                pr.marks["mo_xn_%d_%d" % (e + 1, q)] = plan.val(VS)
                pr.marks["mo_idx_%d_%d" % (e + 1, q)] = plan.val(VS)
        pr.marks["mo_v_fin"] = plan.val(VS)

    @block.tensor
    def _(te):
        nmm = 0
        for e in range(EPC):
            for q in range(4):
                te.wait_ge(plan.sem(VS), pr.marks["mo_ln_%d_%d" % (e, q)])
                for cch in range(8):
                    if (4 * e + q) * 8 + cch >= 2:
                        te.wait_ge(plan.sem("mo_tpev"), (4 * e + q) * 8 + cch - 1)
                    pb = pbf if cch % 2 == 0 else pbf2
                    plan.inc("mo_tp", te.transpose(
                        pb[:, 0:128], xnr[:, 128 * cch:128 * (cch + 1)], s["ident"][:]))
            te.wait_ge(plan.sem("mwl%d" % (e % 2)), 16 * 12 * (e // 2 + 1))
            te.wait_ge(plan.sem(VS), pr.marks["mo_xnT_%d_3" % e])
            for fc in range(8):
                psx = psg1 if fc % 2 == 0 else psg2
                if fc >= 2:
                    te.wait_ge(plan.sem("mo_ev"), 8 * e + fc - 1)
                for cc in range(8):
                    lhs = wegu[:, 1024 * cc + 128 * fc:1024 * cc + 128 * (fc + 1)]
                    te.ldweights(lhs)
                    plan.inc("mo_pe", te.matmul(psx[:, 0:512], lhs, xnT[:, 512 * cc:512 * (cc + 1)],
                                                start=(cc == 0), stop=(cc == 7),
                                                skip_group_check=True))
            for q in range(4):
                if q >= 1:
                    te.wait_ge(plan.sem("mo_dnev"), 4 * e + q)
                elif e >= 1:
                    te.wait_ge(plan.sem("mo_dnev"), 4 * e)
                for ic in range(4):
                    te.wait_ge(plan.sem(VS), pr.marks["mo_act_%d_%d" % (e, ic)])
                    lhs = actT[:, 512 * ic + 128 * q:512 * ic + 128 * (q + 1)]
                    te.ldweights(lhs)
                    for hh in range(2):
                        _mm = te.matmul(psd[:, 512 * hh:512 * (hh + 1)], lhs,
                                        wedw[:, 1024 * ic + 512 * hh:1024 * ic + 512 * (hh + 1)],
                                        start=(ic == 0), stop=(ic == 3),
                                        skip_group_check=True)
                    if ic == 3:
                        plan.inc("mo_dn", _mm)
        pr.marks["mo_t_fin"] = plan.val("mo_pe")

    @block.scalar
    def _(sc):
        for e in range(EPC):
            for fc in range(8):
                sc.wait_ge(plan.sem("mo_pe"), 64 * e + 8 * (fc + 1))
                psx = psg1 if fc % 2 == 0 else psg2
                if fc < 4:
                    if e >= 1:
                        sc.wait_ge(plan.sem(VS), pr.marks["mo_act_%d_%d" % (e - 1, fc)])
                    plan.inc("mo_ev", sc.activation(sil[:, 512 * fc:512 * (fc + 1)],
                                                    psx[:, 0:512], AF.Silu))
                else:
                    ip = fc - 4
                    if e >= 1:
                        sc.wait_ge(plan.sem("mo_dn"), 4 * e)
                    plan.inc("mo_ev", sc.copy(actT[:, 512 * ip:512 * (ip + 1)],
                                              psx[:, 0:512]))
            for q in range(4):
                sc.wait_ge(plan.sem(VS), pr.marks["mo_out_%d_%d" % (e, q)])
                plan.inc("mow", sc.dma_start(
                    out=pr.d["moe_out"][e * CAP + q * 128:e * CAP + (q + 1) * 128, :],
                    in_=outb[:]), 16)

    @block.gpsimd
    def _(g):
        g.wait_ge(plan.sem("cc_rs"), 1)
        for e in range(EPC):
            if e > 0:
                g.wait_ge(plan.sem("mo_pe"), 64 * e)
                g.wait_ge(plan.sem("mo_dn"), 4 * e)
            for cc in range(8):
                dma_in(plan, g, wegu[:, 1024 * cc:1024 * (cc + 1)],
                       pr.d["egu_c"][e].rearrange("(c p) f -> c p f", p=128)[cc],
                       "mwl%d" % (e % 2))
            for ic in range(4):
                dma_in(plan, g, wedw[:, 1024 * ic:1024 * (ic + 1)],
                       pr.d["edw_c"][e].rearrange("(c p) f -> c p f", p=128)[ic],
                       "mwl%d" % (e % 2))
            for q in range(4):
                st = e * CAP + q * 128
                g.wait_ge(plan.sem(VS), pr.marks.get("mo_xn_%d_%d" % (e, q), 0))
                dma_in(plan, g, it4[:, 4 * q:4 * (q + 1)],
                       pr.d["idwn_rs"][st:st + 128, :], "mil%d" % (q % 2))
                g.wait_ge(plan.sem(VS), pr.marks.get("mo_idx_%d_%d" % (e, q), 0))
                g.wait_ge(plan.sem(VS), pr.marks["mo_gw_%d_%d" % (e, q)])
                plan.inc("mgl", g.indirect_dma_start(
                    out=xrow[:], out_offset=None,
                    in_=pr.d["x_full"],
                    in_offset=bass.IndirectOffsetOnAxis(ap=idxu[:, q:q + 1], axis=0),
                    bounds_check=pr.bnd32767, oob_is_err=False), 16)

    @block.gpsimd
    def _(g):
        g.wait_ge(plan.sem("mow"), 16 * 4 * EPC)
        plan.inc("cc_moe", g.collective_compute(
            "AllGather", ALU.bypass, replica_groups=[list(range(NCORES))],
            ins=[pr.d["moe_out"].opt()], outs=[pr.d["moe_all"].opt()]))


def emit_final(pr, block):
    plan = pr.plan
    s = pr.s
    NTC = 4
    nt = TPC // 128

    xnfm_b = pr.sbuf("fi_xnfm", [128, 8 * 1024], BF16)
    wsd = pr.sbuf("fi_wsd", [128, 16 * 1024], BF16)
    wsg_g = pr.sbuf("fi_wg", [128, 2 * 1024], BF16)
    wsg_u = pr.sbuf("fi_wu", [128, 2 * 1024], BF16)
    sil = pr.sbuf("fi_sil", [128, 1024], BF16)
    actT = pr.sbuf("fi_act", [128, 16 * 1024], BF16)
    gt8 = pr.sbuf("fi_gt", [128, 2 * 8 * 1024], BF16)
    ymoe = pr.sbuf("fi_ym", [128, 1024], F32)
    ytile = pr.sbuf("fi_y", [128, 2 * 1024], F32)
    xres = pr.sbuf("fi_xr", [128, 2 * 1024], F32)
    ssmt = pr.sbuf("fi_ssm", [128, 2 * 1024], BF16)
    sstg = pr.sbuf("fi_sstg", [128, 4096], BF16)
    psga = pr.psum("fi_pg1", [128, 1024], F32)
    psgb = pr.psum("fi_pg2", [128, 1024], F32)
    psda = pr.psum("fi_pd1", [128, 1024], F32)
    psdb = pr.psum("fi_pd2", [128, 1024], F32)
    VS = "vself"
    SM = {"fxl0": "rxl0", "fxl1": "rxl1", "fsl0": "rfw0", "fsl1": "rfw1",
          "fi_xc": "rcw", "fowp0": "mow", "fowp1": "cmw1",
          "fwg0": "mwl0", "fwg1": "mwl1",
          "fi_gb0": "mil0", "fi_gb1": "mil1", "fi_wl2": "rgl",
          "fi_ev": "mo_ev", "fi_gm": "mo_pe", "fi_up": "mo_tp",
          "fi_dnm": "mo_dn", "fi_actev": "mo_tpev", "fi_dnev": "mo_dnev",
          "fwg0p0": "mwl0", "fwg0p1": "mwl1", "fwg1p0": "mgl", "fwg1p1": "rsc"}
    BASE = {a: plan.val(r) for a, r in SM.items()}

    def fsem(alias):
        return plan.sem(SM[alias])

    def fbase(alias):
        return BASE[alias]

    def finc(alias, instr, k=16):
        return plan.inc(SM[alias], instr, k)

    def fence_eng(eng):
        for nm, cntv in pr.cv_fence:
            eng.wait_ge(plan.sem(nm), cntv)

    @block.vector
    def _(v):
        def vop(instr, k=1):
            n = plan.inc(VS, instr, k)
            v.wait_ge(plan.sem(VS), n)
            return n
        fence_eng(v)
        for tc in range(NTC):
            for ip in range(16):
                v.wait_ge(fsem("fi_ev"), fbase("fi_ev") + 16 * tc + ip + 1)
                v.wait_ge(fsem("fi_up"), fbase("fi_up") + 16 * tc + ip + 1)
                if tc >= 1:
                    v.wait_ge(fsem("fi_dnm"), fbase("fi_dnm") + 8 * tc)
                n = vop(v.scalar_tensor_tensor(
                    actT[:, 1024 * ip:1024 * (ip + 1)], psgb[:, 0:1024], 1.0,
                    sil[:], op0=ALU.mult, op1=ALU.mult))
                finc("fi_actev", v.nop(), 1)
                pr.marks["fi_act_%d_%d" % (tc, ip)] = n
            for tt in range(8):
                i = 8 * tc + tt
                ip2 = i % 2
                if i >= 2:
                    v.wait_ge(plan.sem(VS), pr.marks["fi_tree_%d" % (i - 2)])
                for k in range(8):
                    n = vop(v.memset(gt8[:, (8 * ip2 + k) * 1024:(8 * ip2 + k + 1) * 1024], 0.0))
                pr.marks["fi_gz_%d" % i] = n
                v.wait_ge(fsem("fi_gb%d" % ip2), fbase("fi_gb%d" % ip2) + 16 * 8 * (i // 2 + 1))
                base = 8 * ip2 * 1024
                for k in range(4):
                    vop(v.tensor_tensor(gt8[:, base + 2 * k * 1024:base + (2 * k + 1) * 1024],
                                        gt8[:, base + 2 * k * 1024:base + (2 * k + 1) * 1024],
                                        gt8[:, base + (2 * k + 1) * 1024:base + (2 * k + 2) * 1024],
                                        op=ALU.add))
                vop(v.tensor_tensor(gt8[:, base:base + 1024], gt8[:, base:base + 1024],
                                    gt8[:, base + 2048:base + 3072], op=ALU.add))
                vop(v.tensor_tensor(gt8[:, base + 4096:base + 5120],
                                    gt8[:, base + 4096:base + 5120],
                                    gt8[:, base + 6144:base + 7168], op=ALU.add))
                n = vop(v.tensor_tensor(ymoe[:], gt8[:, base:base + 1024],
                                        gt8[:, base + 4096:base + 5120], op=ALU.add))
                pr.marks["fi_tree_%d" % i] = n
                if i >= 2:
                    pw = "fowp%d" % (i % 2)
                    v.wait_ge(fsem(pw), fbase(pw) + 16 * (i // 2))
                v.wait_ge(fsem("fi_dnm"), fbase("fi_dnm") + i + 1)
                v.wait_ge(plan.sem("rt_sig"), 1)
                psd = psda if tt % 2 == 0 else psdb
                vop(v.scalar_tensor_tensor(ytile[:, ip2 * 1024:(ip2 + 1) * 1024],
                                           psd[:, 0:1024], s["sg_all"][:, i:i + 1],
                                           ymoe[:], op0=ALU.mult, op1=ALU.add))
                finc("fi_dnev", v.nop(), 1)
                v.wait_ge(fsem("fxl%d" % ip2), fbase("fxl%d" % ip2) + 16 * (i // 2 + 1))
                vop(v.tensor_tensor(ytile[:, ip2 * 1024:(ip2 + 1) * 1024],
                                    ytile[:, ip2 * 1024:(ip2 + 1) * 1024],
                                    xres[:, ip2 * 1024:(ip2 + 1) * 1024], op=ALU.add))
                v.wait_ge(fsem("fsl%d" % ip2), fbase("fsl%d" % ip2) + 16 * (i // 2 + 1))
                n = vop(v.tensor_tensor(ytile[:, ip2 * 1024:(ip2 + 1) * 1024],
                                        ytile[:, ip2 * 1024:(ip2 + 1) * 1024],
                                        ssmt[:, ip2 * 1024:(ip2 + 1) * 1024], op=ALU.add))
                pr.marks["fi_y_%d" % i] = n

    @block.tensor
    def _(te):
        fence_eng(te)
        te.wait_ge(fsem("fi_wl2"), fbase("fi_wl2") + 16 * 16)
        for tc in range(NTC):
            te.wait_ge(fsem("fi_xc"), fbase("fi_xc") + 16 * (tc + 1))
            for ip in range(16):
                te.wait_ge(fsem("fwg0p%d" % (ip % 2)),
                           fbase("fwg0p%d" % (ip % 2)) + 16 * ((16 * tc + ip) // 2 + 1))
                if 16 * tc + ip >= 1:
                    te.wait_ge(fsem("fi_ev"), fbase("fi_ev") + 16 * tc + ip)
                for cc in range(8):
                    lhs = wsg_g[:, ((ip % 2) * 8 + cc) * 128:((ip % 2) * 8 + cc + 1) * 128]
                    te.ldweights(lhs)
                    for hh in range(2):
                        _mm = te.matmul(psga[:, 512 * hh:512 * (hh + 1)], lhs,
                                        xnfm_b[:, cc * 1024 + 512 * hh:cc * 1024 + 512 * (hh + 1)],
                                        start=(cc == 0), stop=(cc == 7),
                                        skip_group_check=True)
                    if cc == 7:
                        finc("fi_gm", _mm, 1)
                te.wait_ge(fsem("fwg1p%d" % (ip % 2)),
                           fbase("fwg1p%d" % (ip % 2)) + 16 * ((16 * tc + ip) // 2 + 1))
                if 16 * tc + ip >= 1:
                    te.wait_ge(fsem("fi_actev"), fbase("fi_actev") + 16 * tc + ip)
                for cc in range(8):
                    lhs = wsg_u[:, ((ip % 2) * 8 + cc) * 128:((ip % 2) * 8 + cc + 1) * 128]
                    te.ldweights(lhs)
                    for hh in range(2):
                        _mm = te.matmul(psgb[:, 512 * hh:512 * (hh + 1)], lhs,
                                        xnfm_b[:, cc * 1024 + 512 * hh:cc * 1024 + 512 * (hh + 1)],
                                        start=(cc == 0), stop=(cc == 7),
                                        skip_group_check=True)
                    if cc == 7:
                        finc("fi_up", _mm, 1)
            for tt in range(8):
                i = 8 * tc + tt
                psd = psda if tt % 2 == 0 else psdb
                if i >= 2:
                    te.wait_ge(fsem("fi_dnev"), fbase("fi_dnev") + i - 1)
                for ip in range(16):
                    te.wait_ge(plan.sem(VS), pr.marks["fi_act_%d_%d" % (tc, ip)])
                    lhs = actT[:, 1024 * ip + 128 * tt:1024 * ip + 128 * (tt + 1)]
                    te.ldweights(lhs)
                    for hh in range(2):
                        _mm = te.matmul(psd[:, 512 * hh:512 * (hh + 1)], lhs,
                                        wsd[:, 1024 * ip + 512 * hh:1024 * ip + 512 * (hh + 1)],
                                        start=(ip == 0), stop=(ip == 15),
                                        skip_group_check=True)
                    if ip == 15:
                        finc("fi_dnm", _mm, 1)

    @block.scalar
    def _(sc):
        fence_eng(sc)
        for tc in range(NTC):
            for ip in range(16):
                sc.wait_ge(fsem("fi_gm"), fbase("fi_gm") + 16 * tc + ip + 1)
                if 16 * tc + ip >= 1:
                    sc.wait_ge(plan.sem(VS), pr.marks["fi_act_%d_%d" %
                               ((16 * tc + ip - 1) // 16, (16 * tc + ip - 1) % 16)])
                finc("fi_ev", sc.activation(sil[:], psga[:, 0:1024], AF.Silu), 1)
            for tt in range(8):
                i = 8 * tc + tt
                sc.wait_ge(plan.sem(VS), pr.marks["fi_y_%d" % i])
                finc("fowp%d" % (i % 2), sc.dma_start(
                    out=pr.d["y_out"][128 * i:128 * (i + 1), :],
                    in_=ytile[:, (i % 2) * 1024:(i % 2 + 1) * 1024]))
        sc.wait_ge(fsem("fowp0"), fbase("fowp0") + 16 * (nt // 2))
        sc.wait_ge(fsem("fowp1"), fbase("fowp1") + 16 * (nt // 2))

    @block.sync
    def _(sy):
        fence_eng(sy)
        dma_in(plan, sy,
               xnfm_b[:].rearrange("p (c t) -> p c t", t=1024),
               pr.d["xnfm"].rearrange("(c p) t -> p c t", p=128)
               [:, :, 0:1024], SM["fi_xc"])
        for i0 in range(2):
            dma_in(plan, sy, xres[:, i0 * 1024:(i0 + 1) * 1024],
                   pr.d["x_ts"][128 * i0:128 * (i0 + 1), :], SM["fxl%d" % i0])
        with sy.register("srk") as srk:
            sy.load(srk, pr.s["rk_i_sb"][0:1, 0:1])
            off = sy.snap(srk)
            sy.wait_ge(plan.sem("cc_ssm"), 1)
            bb = plan.val("kgw")
            for cc in range(8):
                if cc >= 1:
                    sy.wait_ge(plan.sem("kgw"), bb + 16 * (2 * cc))
                plan.inc("kgw", sy.dma_start(
                    out=sstg[:],
                    in_=pr.d["ssm_agout"][128 * cc:128 * (cc + 1), bass.ds(off, TPC)]), 16)
                sy.wait_ge(plan.sem("kgw"), bb + 16 * (2 * cc + 1))
                plan.inc("kgw", sy.dma_start(
                    out=pr.d["ssm_my"][128 * cc:128 * (cc + 1), :],
                    in_=sstg[:]), 16)
            sy.wait_ge(plan.sem("kgw"), bb + 16 * 16)
            for tc in range(NTC):
                if tc >= 1:
                    sy.wait_ge(fsem("fi_up"), fbase("fi_up") + 16 * tc)
                    dma_in(plan, sy,
                           xnfm_b[:].rearrange("p (c t) -> p c t", t=1024),
                           pr.d["xnfm"].rearrange("(c p) t -> p c t", p=128)
                           [:, :, 1024 * tc:1024 * (tc + 1)], SM["fi_xc"])
                for tt in range(8):
                    i = 8 * tc + tt
                    ip2 = i % 2
                    if i >= 2:
                        sy.wait_ge(plan.sem(VS), pr.marks["fi_y_%d" % (i - 2)])
                        dma_in(plan, sy, xres[:, ip2 * 1024:(ip2 + 1) * 1024],
                               pr.d["x_ts"][128 * i:128 * (i + 1), :], SM["fxl%d" % ip2])
                    finc("fsl%d" % ip2, sy.dma_start(
                        out=ssmt[:, ip2 * 1024:(ip2 + 1) * 1024],
                        in_=pr.d["ssm_my"][:, 128 * i:128 * (i + 1)],
                        transpose=True))

    @block.gpsimd
    def _(g):
        fence_eng(g)
        for ic in range(16):
            dma_in(plan, g, wsd[:, 1024 * ic:1024 * (ic + 1)],
                   pr.d["sdw"].rearrange("(c p) f -> c p f", p=128)[ic], SM["fi_wl2"])
        for tc in range(NTC):
            for ip in range(16):
                if 16 * tc + ip >= 2:
                    g.wait_ge(fsem("fi_gm"), fbase("fi_gm") + 16 * tc + ip - 1)
                    g.wait_ge(fsem("fi_up"), fbase("fi_up") + 16 * tc + ip - 1)
                dma_in(plan, g,
                       wsg_g[:, (ip % 2) * 1024:(ip % 2 + 1) * 1024].rearrange(
                           "p (c f) -> p c f", f=128),
                       pr.d["sguw"][:, 128 * ip:128 * (ip + 1)].rearrange(
                           "(c p) f -> p c f", p=128), SM["fwg0p%d" % (ip % 2)])
                dma_in(plan, g,
                       wsg_u[:, (ip % 2) * 1024:(ip % 2 + 1) * 1024].rearrange(
                           "p (c f) -> p c f", f=128),
                       pr.d["sguw"][:, ISH + 128 * ip:ISH + 128 * (ip + 1)].rearrange(
                           "(c p) f -> p c f", p=128), SM["fwg1p%d" % (ip % 2)])
            for tt in range(8):
                i = 8 * tc + tt
                ip2 = i % 2
                if i == 0:
                    g.wait_ge(plan.sem("cc_moe"), 1)
                g.wait_ge(plan.sem(VS), pr.marks["fi_gz_%d" % i])
                for k in range(8):
                    finc("fi_gb%d" % ip2, g.indirect_dma_start(
                        out=gt8[:, (8 * ip2 + k) * 1024:(8 * ip2 + k + 1) * 1024],
                        out_offset=None,
                        in_=pr.d["moe_all"],
                        in_offset=bass.IndirectOffsetOnAxis(
                            ap=pr.s["dsti_all"][:, 8 * i + k:8 * i + k + 1], axis=0),
                        bounds_check=pr.bnd32767, oob_is_err=False))


def _phase_wait(block, plan, pr, sems_counts, vs_marks):
    for engname in ["vector", "scalar", "tensor", "sync", "gpsimd"]:
        dec = getattr(block, engname)

        @dec
        def _(eng):
            for nm, cntv in sems_counts:
                eng.wait_ge(plan.sem(nm), cntv)
            for nm, mk in vs_marks:
                eng.wait_ge(plan.sem(nm), mk)


def build_full():
    import contextlib
    nc = bass.Bass("TRN2", target_bir_lowering=False)
    stack = contextlib.ExitStack()
    pr = Prog(nc, stack, 0)
    declare_conv_io(pr)
    declare_ffn_io(pr)
    pr.dram("y_out", [TPC, HD], F32, kind="ExternalOutput")

    with nc.Block() as block:
        plan = pr.plan
        pr.sbuf("dsti_all", [128, 8 * (TPC // 128)], U32)
        pr.sbuf("sg_all", [128, TPC // 128], F32)
        pr.sbuf("rk_i_sb", [1, 1], I32)

        @block.gpsimd
        def _(g):
            conv_const_loads(pr, g)
            dma_in(plan, g, pr.s["rk_i_sb"][:], pr.d["rank_i"], "cl")

        pr.push()
        emit_stats(pr, block)
        st_fence = [("sv", plan.val("sv")), ("ss2", plan.val("ss2")),
                    ("sve", plan.val("sve")), ("sgi", 16)]
        pr.pop()

        pr.push()
        _phase_wait(block, plan, pr, st_fence, [])
        emit_routing(pr, block)
        rt_fence = [("rvs", plan.val("rvs")), ("rt_exp", plan.val("rt_exp")),
                    ("rt_sig", 1), ("rsc", plan.val("rsc")),
                    ("rsc1", plan.val("rsc1")),
                    ("rt_chT", plan.val("rt_chT")), ("rt_gtm", plan.val("rt_gtm")),
                    ("rfw0", plan.val("rfw0")), ("rfw1", plan.val("rfw1")),
                    ("rzw", 16), ("rcw", 16)]
        pr.pop()

        pr.push()
        _phase_wait(block, plan, pr, rt_fence, [])
        pr.sbuf("kb_gidx", [128, 16], U32)
        pr.push()
        emit_kg(pr, block)
        pr.pop()
        emit_basis(pr, block)
        emit_cmix(pr, block)
        cm_fence = [("cmp", plan.val("cmp")), ("cme", plan.val("cme")),
                    ("cmw0", plan.val("cmw0")), ("cmw1", plan.val("cmw1")),
                    ("cmw2", plan.val("cmw2")), ("cmw3", plan.val("cmw3")),
                    ("kbv", plan.val("kbv")), ("kbv2", plan.val("kbv2")),
                    ("kbe", plan.val("kbe")), ("kbe2", plan.val("kbe2")),
                    ("cmv", 1)]
        pr.pop()

        pr.push()
        _phase_wait(block, plan, pr, cm_fence, [])
        emit_moe(pr, block)
        mo_fence = [("mvs", plan.val("mvs")), ("mo_pe", plan.val("mo_pe")),
                    ("mo_dn", plan.val("mo_dn")), ("mo_ev", plan.val("mo_ev")),
                    ("mow", plan.val("mow")), ("mgl", plan.val("mgl")),
                    ("mo_tp", plan.val("mo_tp"))]
        pr.pop()

        pr.push()
        _phase_wait(block, plan, pr, mo_fence, [])
        c0 = dict(plan.cnt)
        emit_conv(pr, block)
        pr.cv_fence = [(nm, v) for nm, v in plan.cnt.items()
                       if v != c0.get(nm, 0) and nm != "cc_ssm"]
        pr.pop()

        pr.push()
        emit_final(pr, block)
        pr.pop()

    stack.close()
    return nc


_NC_CACHE = None


def kernel(**inputs):
    global _NC_CACHE
    x = np.asarray(inputs["L0_attn_r"])
    prm = dict(B_re=np.asarray(inputs["L0_attn_B_re"]),
               B_im=np.asarray(inputs["L0_attn_B_im"]),
               C_re=np.asarray(inputs["L0_attn_C_re"]),
               C_im=np.asarray(inputs["L0_attn_C_im"]),
               P_re=np.asarray(inputs["L0_attn_P_re"]),
               P_im=np.asarray(inputs["L0_attn_P_im"]),
               Q_re=np.asarray(inputs["L0_attn_Q_re"]),
               Q_im=np.asarray(inputs["L0_attn_Q_im"]),
               diag_re=np.asarray(inputs["L0_attn_diag_re"]),
               diag_im=np.asarray(inputs["L0_attn_diag_im"]),
               step=np.asarray(inputs["L0_attn_step"]),
               D=np.asarray(inputs["L0_attn_D"]))
    if _NC_CACHE is None:
        _NC_CACHE = build_full()
    nc = _NC_CACHE
    in_maps = []
    for c in range(NCORES):
        m = _shard_ssm_inputs(c, x, prm)
        m["gate_w"] = np.asarray(inputs["L0_ffn_gate_w"])
        m["shared_gate_w"] = np.asarray(inputs["L0_ffn_shared_gate_w"])
        m["sguw"] = np.asarray(inputs["L0_ffn_shared_gate_up_w"])
        m["sdw"] = np.asarray(inputs["L0_ffn_shared_down_w"])
        m["egu_c"] = np.ascontiguousarray(
            np.asarray(inputs["L0_ffn_expert_gate_up_w"])[EPC * c:EPC * (c + 1)])
        m["edw_c"] = np.ascontiguousarray(
            np.asarray(inputs["L0_ffn_expert_down_w"])[EPC * c:EPC * (c + 1)])
        m["rank_off"] = np.array([[4096.0 * c + 1.0]], np.float32)
        m["tri_rk"] = (np.arange(8) < c).astype(np.float32).reshape(8, 1)
        m["rank_i"] = np.array([[4096 * c]], np.int32)
        in_maps.append(m)
    res = run_bass_kernel_spmd(nc, in_maps, core_ids=list(range(NCORES)))
    out = np.concatenate([res.results[c]["y_out"] for c in range(NCORES)], axis=0)
    return out.reshape(B, L, HD)


# version 07
# speedup vs baseline: 1.3136x; 1.0856x over previous
import numpy as np
import ml_dtypes
import concourse.bass as bass
import concourse.mybir as mybir
from concourse.bass_utils import run_bass_kernel_spmd

F32 = mybir.dt.float32
BF16 = mybir.dt.bfloat16
U32 = mybir.dt.uint32
I32 = mybir.dt.int32
AF = mybir.ActivationFunctionType
ALU = mybir.AluOpType
BF = ml_dtypes.bfloat16

NCORES = 8
B = 2
L = 16384
NF = 2 * L
HD = 1024
T = B * L
TPC = T // NCORES
CPC = HD // NCORES
NSEQ = B * CPC
NMODE = 64
E = 64
EPC = E // NCORES
TOPK = 8
CAP = 512
ISH = 2048
IEX = 512
EPS = 1e-5
K1H = 129


def _host_consts():
    c = {}
    l1 = np.arange(128)
    k1 = np.arange(K1H)
    l2 = np.arange(128)
    k2 = np.arange(128)

    F1 = np.exp(-2j * np.pi * np.outer(l1, k1) / 256.0)
    c["f1"] = np.concatenate([F1.real, F1.imag], axis=1).astype(BF)

    TWm = np.exp(-2j * np.pi * np.outer(l2, k1) / NF)
    c["twr2"] = np.concatenate([TWm.real, TWm.real], axis=1).astype(BF)
    c["twi2"] = np.concatenate([TWm.imag, TWm.imag], axis=1).astype(BF)

    ang = 2 * np.pi * np.outer(l2, k2) / 128.0
    c["f2re"] = np.cos(ang).astype(BF)
    c["f2im"] = (-np.sin(ang)).astype(BF)
    c["f2imn"] = (np.sin(ang)).astype(BF)

    F2i_r = np.cos(ang)
    F2i_i = np.sin(ang)
    c["ra"] = np.concatenate([F2i_r, F2i_i], axis=1).astype(BF)
    c["rb"] = np.concatenate([-F2i_i, F2i_r], axis=1).astype(BF)
    itwe = np.exp(+2j * np.pi * l2 * 128.0 / NF)
    era_r = F2i_r * itwe.real[None, :] - F2i_i * itwe.imag[None, :]
    era_i = F2i_r * itwe.imag[None, :] + F2i_i * itwe.real[None, :]
    c["rae"] = np.concatenate([era_r, era_i], axis=1).astype(BF)
    c["rbe"] = np.concatenate([-era_i, era_r], axis=1).astype(BF)

    IT = np.exp(+2j * np.pi * np.outer(k1[:128], l2) / NF)
    c["itr2"] = np.concatenate([IT.real, IT.real], axis=1).astype(BF)
    c["iti2"] = np.concatenate([IT.imag, IT.imag], axis=1).astype(BF)

    cc = np.ones(K1H)
    cc[0] = 0.5
    cc[128] = 0.5
    G1 = (2.0 * cc[:, None] / NF) * np.exp(+2j * np.pi * np.outer(k1, l1) / 256.0)
    c["g1re"] = np.ascontiguousarray(G1.real[:128]).astype(BF)
    c["g1imn"] = np.ascontiguousarray(-G1.imag[:128]).astype(BF)
    c["g1er"] = np.ascontiguousarray(G1.real[128:129]).astype(BF)
    c["g1ein"] = np.ascontiguousarray(-G1.imag[128:129]).astype(BF)

    lf = np.arange(L, dtype=np.float32)
    ang2 = (np.complex64(-2j * np.pi) * lf.astype(np.complex64) / np.float32(L)).astype(np.complex64)
    z = np.exp(ang2).astype(np.complex64)
    one = np.complex64(1.0)
    w = ((one - z) / (one + z)).astype(np.complex64)
    tz = (np.complex64(2.0) / (one + z)).astype(np.complex64)
    c["w_re"] = w.real.astype(np.float32)
    c["w_im"] = w.imag.astype(np.float32)
    c["tz_re"] = tz.real.astype(np.float32)
    c["tz_im"] = tz.imag.astype(np.float32)

    m1 = np.arange(128)
    m2 = np.arange(128)
    FA = np.exp(+2j * np.pi * np.outer(m1, l2) / 128.0) / 128.0
    c["ifab"] = np.concatenate([FA.real, FA.imag], axis=1).astype(BF)
    c["ifbb"] = np.concatenate([-FA.imag, FA.real], axis=1).astype(BF)
    ITW2 = np.exp(+2j * np.pi * np.outer(m2, l2) / L)
    c["iftr2"] = np.concatenate([ITW2.real, ITW2.real], axis=1).astype(BF)
    c["ifti2"] = np.concatenate([ITW2.imag, ITW2.imag], axis=1).astype(BF)
    FB = np.exp(+2j * np.pi * np.outer(m2, l1) / 128.0) / 128.0
    c["ifbre"] = FB.real.astype(BF)
    c["ifbim"] = FB.imag.astype(BF)
    c["ifbimn"] = (-FB.imag).astype(BF)

    c["ident"] = np.eye(128, dtype=np.float32).astype(BF)
    c["ident32"] = np.eye(128, dtype=np.float32)
    c["ones1x128"] = np.ones((1, 128), dtype=np.float32).astype(BF)
    c["ones32r"] = np.ones((1, 128), dtype=np.float32)
    c["ones64c"] = np.ones((64, 1), dtype=np.float32)
    sgn = np.ones((128, 1), dtype=np.float32)
    for r in range(8):
        sgn[16 * r + 8:16 * r + 16] = -1.0
    c["sign_col"] = sgn
    return c


_CONSTS = None

def consts():
    global _CONSTS
    if _CONSTS is None:
        _CONSTS = _host_consts()
    return _CONSTS


class Plan:

    def __init__(self, nc, stack):
        self.nc = nc
        self.stack = stack
        self.sems = {}
        self.cnt = {}

    def sem(self, name):
        if name not in self.sems:
            self.sems[name] = self.stack.enter_context(self.nc.semaphore(name))
            self.cnt[name] = 0
        return self.sems[name]

    def inc(self, name, instr, k=1):
        instr.then_inc(self.sem(name), k)
        self.cnt[name] += k
        return self.cnt[name]

    def val(self, name):
        self.sem(name)
        return self.cnt[name]


def dma_in(plan, eng, out, in_, sem="ld"):
    return plan.inc(sem, eng.dma_start(out=out, in_=in_), 16)


CONST_NAMES = ["f1", "twr2", "twi2", "f2re", "f2im", "f2imn", "ra", "rb", "rae",
               "rbe", "itr2", "iti2", "g1re", "g1imn", "g1er", "g1ein",
               "ifab", "ifbb", "iftr2", "ifti2", "ifbre", "ifbim", "ifbimn",
               "ident", "ident32", "ones1x128", "ones32r", "ones64c", "sign_col"]


class Prog:

    def __init__(self, nc, stack, rank):
        self.nc = nc
        self.stack = stack
        self.scopes = [stack]
        self.rank = rank
        self.plan = Plan(nc, stack)
        self.d = {}
        self.s = {}
        self.p = {}
        self.marks = {}

    def push(self):
        import contextlib
        st = contextlib.ExitStack()
        self.scopes.append(st)
        return st

    def pop(self):
        self.scopes.pop().close()

    def dram(self, name, shape, dtype, kind=None):
        if kind is None:
            t = self.nc.dram_tensor(name, shape, dtype)
        else:
            t = self.nc.dram_tensor(name, shape, dtype, kind=kind)
        self.d[name] = t.ap()
        return self.d[name]

    def sbuf(self, name, shape, dtype):
        t = self.scopes[-1].enter_context(self.nc.sbuf_tensor(name, shape, dtype))
        self.s[name] = t
        return t

    def sbuf_global(self, name, shape, dtype):
        t = self.scopes[0].enter_context(
            self.nc.sbuf_tensor(name, shape, dtype, side="right"))
        self.s[name] = t
        return t

    def psum(self, name, shape, dtype=F32):
        t = self.scopes[-1].enter_context(self.nc.psum_tensor(name, shape, dtype))
        self.p[name] = t
        return t


def declare_conv_io(pr, with_inputs=True):
    c = consts()
    pr.dram("x_full", [T, HD], F32, kind="ExternalInput")
    pr.dram("x_ch", [CPC, B, L], F32, kind="ExternalInput")
    for nm in ["B_re", "B_im", "P_re", "P_im", "Q_re", "Q_im", "diag_re", "diag_im"]:
        pr.dram(nm, [NMODE, 1], F32, kind="ExternalInput")
    pr.dram("step", [1, 1], F32, kind="ExternalInput")
    pr.dram("Dv", [1, 1], F32, kind="ExternalInput")
    pr.dram("C_re", [CPC, NMODE], F32, kind="ExternalInput")
    pr.dram("C_im", [CPC, NMODE], F32, kind="ExternalInput")
    pr.dram("w_re", [1, 2048], F32, kind="ExternalInput")
    pr.dram("w_im", [1, 2048], F32, kind="ExternalInput")
    pr.dram("tz_re", [1, 2048], F32, kind="ExternalInput")
    pr.dram("tz_im", [1, 2048], F32, kind="ExternalInput")
    for nm in CONST_NAMES:
        arr = c[nm]
        dt = BF16 if arr.dtype == BF else F32
        pr.dram("c_" + nm, list(arr.shape), dt, kind="ExternalInput")
    pr.dram("x_ts", [TPC, HD], F32, kind="ExternalInput")
    pr.dram("cmix_pre", [128, CPC], F32, kind="ExternalInput")
    pr.dram("gidx", [128, 16], U32, kind="ExternalInput")
    pr.dram("g_agin", [2048, 128], F32)
    pr.dram("g_agout", [16384, 128], F32)
    pr.dram("bas_agin", [16, 66048], BF16)
    pr.dram("bas_agout", [128, 66048], BF16)
    pr.dram("kf", [CPC, 66048], BF16)
    pr.dram("st_agin", [64, 128], F32)
    pr.dram("st_agout", [512, 128], F32)
    pr.dram("ssm_agin", [CPC, T], BF16)
    pr.dram("ssm_agout", [HD, T], BF16)
    pr.dram("kg_rows", [8, 2048], F32)


def conv_const_loads(pr, g):
    plan = pr.plan
    c = consts()
    pr.bnd32767 = g.alloc_register()
    g.reg_mov(pr.bnd32767, 32767)
    pr.bnd16383 = g.alloc_register()
    g.reg_mov(pr.bnd16383, 16383)
    for nm, w in [("twr2", 258), ("twi2", 258)]:
        tile = pr.sbuf(nm + "_4", [128, 4 * w], BF16)
        for j in range(4):
            dma_in(plan, g, tile[:, j * w:(j + 1) * w], pr.d["c_" + nm], "cl")
    for nm, w in [("itr2", 256), ("iti2", 256)]:
        tile = pr.sbuf(nm + "_4", [128, 4 * w], BF16)
        for j in range(4):
            dma_in(plan, g, tile[:, j * w:(j + 1) * w], pr.d["c_" + nm], "cl")
    for nm in ["f1", "f2re", "f2im", "f2imn", "ra", "rb", "rae", "rbe",
               "g1re", "g1imn", "g1er", "g1ein", "ident", "ones1x128",
               "ifab", "ifbb", "iftr2", "ifti2", "ifbre", "ifbim", "ifbimn"]:
        arr = c[nm]
        tile = pr.sbuf(nm, list(arr.shape), BF16)
        dma_in(plan, g, tile[:], pr.d["c_" + nm], "cl")
    for nm in ["ones32r", "ones64c", "sign_col", "ident32"]:
        arr = c[nm]
        tile = pr.sbuf(nm, list(arr.shape), F32)
        dma_in(plan, g, tile[:], pr.d["c_" + nm], "cl")
    for nm in ["B_re", "B_im", "P_re", "P_im", "Q_re", "Q_im", "diag_re", "diag_im"]:
        tile = pr.sbuf("p_" + nm, [NMODE, 1], F32)
        dma_in(plan, g, tile[:], pr.d[nm], "cl")
    for nm in ["step", "Dv"]:
        tile = pr.sbuf("p_" + nm, [1, 1], F32)
        dma_in(plan, g, tile[:], pr.d[nm], "cl")
    for nm in ["w_re", "w_im", "tz_re", "tz_im"]:
        tile = pr.sbuf("r_" + nm, [1, 2048], F32)
        dma_in(plan, g, tile[:], pr.d[nm], "cl")
    for nm in ["C_re", "C_im"]:
        tile = pr.sbuf("s" + nm, [CPC, NMODE], F32)
        dma_in(plan, g, tile[:], pr.d[nm], "cl")
    tile = pr.sbuf("scmix_pre", [128, CPC], F32)
    dma_in(plan, g, tile[:], pr.d["cmix_pre"], "cl")
    return plan.val("cl")


def emit_stats(pr, block):
    plan = pr.plan
    nt = TPC // 128

    x2 = pr.sbuf("st_x", [128, 2 * HD], F32)
    sq_scr = pr.sbuf("st_sqscr", [128, HD], BF16)
    s1c = pr.sbuf("st_s1", [128, nt], F32)
    sqc = pr.sbuf("st_sq", [128, nt], F32)
    stats = pr.sbuf_global("st_out", [128, 64], F32)
    scr = pr.sbuf("st_scr", [128, 4], F32)
    tp = pr.sbuf("st_tp", [64, 128], F32)
    ps = pr.psum("st_ps", [128, 512], F32)

    @block.sync
    def _(sy):
        for i in range(nt):
            if i >= 2:
                sy.wait_ge(plan.sem("sv"), 6 * (i - 2) + 1)
                sy.wait_ge(plan.sem("ss1"), i - 1)
            dma_in(plan, sy, x2[:, (i % 2) * HD:(i % 2 + 1) * HD],
                   pr.d["x_ts"][128 * i:128 * (i + 1), :], "sl%d" % (i % 2))

    @block.scalar
    def _(sc):
        for i in range(nt):
            sc.wait_ge(plan.sem("sl%d" % (i % 2)), 16 * (i // 2 + 1))
            plan.inc("ss1", sc.activation(sq_scr[:], x2[:, (i % 2) * HD:(i % 2 + 1) * HD],
                                          AF.Square, accum_out=sqc[:, i:i + 1]))
            sc.wait_ge(plan.sem("sv"), 6 * i + 6)
            plan.inc("ss2", sc.activation(stats[:, 2 * i + 1:2 * i + 2],
                                          scr[:, (i % 2) * 2 + 1:(i % 2) * 2 + 2], AF.Sqrt))

    @block.vector
    def _(v):
        for i in range(nt):
            v.wait_ge(plan.sem("sl%d" % (i % 2)), 16 * (i // 2 + 1))
            n0 = plan.inc("sv", v.reduce_sum(
                s1c[:, i:i + 1], x2[:, (i % 2) * HD:(i % 2 + 1) * HD],
                axis=mybir.AxisListType.X))
            v.wait_ge(plan.sem("sv"), n0)
            v.wait_ge(plan.sem("ss1"), i + 1)
            if i >= 2:
                v.wait_ge(plan.sem("ss2"), i - 1)
            n1 = plan.inc("sv", v.tensor_scalar(
                stats[:, 2 * i:2 * i + 1], s1c[:, i:i + 1], 1.0 / HD, None, op0=ALU.mult))
            v.wait_ge(plan.sem("sv"), n1)
            n2 = plan.inc("sv", v.tensor_tensor(
                scr[:, (i % 2) * 2:(i % 2) * 2 + 1], stats[:, 2 * i:2 * i + 1],
                stats[:, 2 * i:2 * i + 1], op=ALU.mult))
            v.wait_ge(plan.sem("sv"), n2)
            n3 = plan.inc("sv", v.tensor_scalar(
                scr[:, (i % 2) * 2 + 1:(i % 2) * 2 + 2], sqc[:, i:i + 1],
                1.0 / HD, EPS, op0=ALU.mult, op1=ALU.add))
            v.wait_ge(plan.sem("sv"), n3)
            n4 = plan.inc("sv", v.tensor_tensor(
                scr[:, (i % 2) * 2 + 1:(i % 2) * 2 + 2],
                scr[:, (i % 2) * 2 + 1:(i % 2) * 2 + 2],
                scr[:, (i % 2) * 2:(i % 2) * 2 + 1], op=ALU.subtract))
            v.wait_ge(plan.sem("sv"), n4)
            plan.inc("sv", v.reciprocal(
                scr[:, (i % 2) * 2 + 1:(i % 2) * 2 + 2],
                scr[:, (i % 2) * 2 + 1:(i % 2) * 2 + 2]))
        v.wait_ge(plan.sem("stp"), 1)
        plan.inc("sve", v.tensor_copy(tp[:], ps[0:64, 0:128]))

    @block.tensor
    def _(te):
        te.wait_ge(plan.sem("ss2"), nt)
        te.wait_ge(plan.sem("cl"), plan.val("cl"))
        plan.inc("stp", te.transpose(ps[0:64, 0:128], stats[:], pr.s["ident32"][:]))

    @block.gpsimd
    def _(g):
        g.wait_ge(plan.sem("sve"), 1)
        dma_in(plan, g, pr.d["st_agin"], tp[:], "sgi")
        g.wait_ge(plan.sem("sgi"), 16)
        plan.inc("cc_st", g.collective_compute(
            "AllGather", ALU.bypass, replica_groups=[list(range(NCORES))],
            ins=[pr.d["st_agin"].opt()], outs=[pr.d["st_agout"].opt()]))


def emit_kg(pr, block):
    plan = pr.plan
    s = pr.s
    NCH = 4
    W = 512

    wb = {}
    for nm in ["wre", "wim", "tzre", "tzim", "qr", "qi"]:
        wb[nm] = pr.sbuf("kg_" + nm + "_b", [64, W], F32)
    cols = pr.sbuf("kg_cols", [64, 16], F32)
    rows = {}
    for nm in ["k10r", "k10i", "k11r", "k11i", "qrr", "qir", "scr1", "scr2"]:
        rows[nm] = pr.sbuf("kg_r_" + nm, [1, W], F32)
    t_a = pr.sbuf("kg_ta", [64, W], F32)
    t_b = pr.sbuf("kg_tb", [64, W], F32)
    t_c = pr.sbuf("kg_tc", [64, W], F32)
    dr = pr.sbuf("kg_dr", [64, W], F32)
    di = pr.sbuf("kg_di", [64, W], F32)
    cre = pr.sbuf("kg_cre", [64, W], F32)
    cim = pr.sbuf("kg_cim", [64, W], F32)
    gre = pr.sbuf("kg_gre", [64, W], F32)
    gim = pr.sbuf("kg_gim", [64, W], F32)
    smallc = pr.sbuf("kg_smallc", [1, 8], F32)
    ps0 = pr.psum("kg_ps0", [128, 512], F32)
    ps1 = pr.psum("kg_ps1", [128, 512], F32)

    VS = "vself"
    PS = "kps"

    @block.vector
    def _(v):
        def vop(instr, k=1):
            n = plan.inc(VS, instr, k)
            v.wait_ge(plan.sem(VS), n)
            return n
        v.wait_ge(plan.sem("cl"), plan.val("cl"))
        vop(v.tensor_scalar(smallc[0:1, 0:1], s["p_step"][0:1, 0:1], 1e-6, None, op0=ALU.max))
        vop(v.reciprocal(smallc[0:1, 1:2], smallc[0:1, 0:1]))
        n = vop(v.tensor_scalar(smallc[0:1, 2:3], smallc[0:1, 1:2], 2.0, None, op0=ALU.mult))
        pr.marks["kg_r2"] = n
        vop(v.tensor_tensor(cols[:, 0:1], s["p_Q_re"][:], s["p_B_re"][:], op=ALU.mult))
        vop(v.tensor_tensor(t_a[:, 0:1], s["p_Q_im"][:], s["p_B_im"][:], op=ALU.mult))
        vop(v.tensor_tensor(cols[:, 0:1], cols[:, 0:1], t_a[:, 0:1], op=ALU.subtract))
        vop(v.tensor_tensor(cols[:, 1:2], s["p_Q_re"][:], s["p_B_im"][:], op=ALU.mult))
        vop(v.tensor_tensor(t_a[:, 0:1], s["p_Q_im"][:], s["p_B_re"][:], op=ALU.mult))
        vop(v.tensor_tensor(cols[:, 1:2], cols[:, 1:2], t_a[:, 0:1], op=ALU.add))
        vop(v.tensor_scalar(cols[:, 2:3], cols[:, 1:2], -1.0, None, op0=ALU.mult))
        vop(v.tensor_tensor(cols[:, 3:4], s["p_Q_re"][:], s["p_P_re"][:], op=ALU.mult))
        vop(v.tensor_tensor(t_a[:, 0:1], s["p_Q_im"][:], s["p_P_im"][:], op=ALU.mult))
        vop(v.tensor_tensor(cols[:, 3:4], cols[:, 3:4], t_a[:, 0:1], op=ALU.subtract))
        vop(v.tensor_tensor(cols[:, 4:5], s["p_Q_re"][:], s["p_P_im"][:], op=ALU.mult))
        vop(v.tensor_tensor(t_a[:, 0:1], s["p_Q_im"][:], s["p_P_re"][:], op=ALU.mult))
        vop(v.tensor_tensor(cols[:, 4:5], cols[:, 4:5], t_a[:, 0:1], op=ALU.add))
        vop(v.tensor_scalar(cols[:, 5:6], cols[:, 4:5], -1.0, None, op0=ALU.mult))
        n = vop(v.tensor_scalar(cols[:, 6:7], s["p_P_im"][:], -1.0, None, op0=ALU.mult))
        pr.marks["kg_cols"] = n
        v.wait_ge(plan.sem(PS), 1)
        vop(v.tensor_copy(cols[:, 7:8], ps0[0:64, 0:1]))
        plan.inc("kev", v.nop())
        for ci in range(NCH):
            pb = 1 + ci * 10
            for bi, nm in enumerate(["wre", "wim", "tzre", "tzim"]):
                v.wait_ge(plan.sem(PS), pb + bi + 1)
                vop(v.tensor_copy(wb[nm][:], ps1[0:64, 0:W]))
                plan.inc("kev", v.nop())
            vop(v.tensor_scalar(dr[:], wb["wre"][:], cols[:, 7:8], s["p_diag_re"][:],
                                op0=ALU.mult, op1=ALU.subtract))
            vop(v.tensor_scalar(di[:], wb["wim"][:], cols[:, 7:8], s["p_diag_im"][:],
                                op0=ALU.mult, op1=ALU.subtract))
            vop(v.tensor_tensor(t_a[:], dr[:], dr[:], op=ALU.mult))
            vop(v.tensor_tensor(t_b[:], di[:], di[:], op=ALU.mult))
            vop(v.tensor_tensor(t_a[:], t_a[:], t_b[:], op=ALU.add))
            vop(v.reciprocal(t_a[:], t_a[:]))
            vop(v.tensor_tensor(cre[:], dr[:], t_a[:], op=ALU.mult))
            vop(v.scalar_tensor_tensor(cim[:], di[:], -1.0, t_a[:], op0=ALU.mult, op1=ALU.mult))
            for term, (c1, c2) in enumerate([(0, 2), (1, 0), (3, 5), (4, 3)]):
                vop(v.tensor_scalar(t_b[:], cre[:], cols[:, c1:c1 + 1], None, op0=ALU.mult))
                n = vop(v.scalar_tensor_tensor(t_c[:], cim[:], cols[:, c2:c2 + 1], t_b[:],
                                               op0=ALU.mult, op1=ALU.add))
                pr.marks["kterm_%d_%d" % (ci, term)] = n
                v.wait_ge(plan.sem(PS), pb + 4 + term + 1)
                vop(v.tensor_copy(rows[["k10r", "k10i", "k11r", "k11i"][term]][:],
                                  ps0[0:1, 0:W]))
                plan.inc("kev2", v.nop())
            vop(v.tensor_scalar(rows["scr1"][:], rows["k11r"][:], 1.0, None, op0=ALU.add))
            vop(v.tensor_tensor(rows["scr2"][:], rows["scr1"][:], rows["scr1"][:], op=ALU.mult))
            vop(v.tensor_tensor(rows["qrr"][:], rows["k11i"][:], rows["k11i"][:], op=ALU.mult))
            vop(v.tensor_tensor(rows["scr2"][:], rows["scr2"][:], rows["qrr"][:], op=ALU.add))
            vop(v.reciprocal(rows["scr2"][:], rows["scr2"][:]))
            vop(v.tensor_tensor(rows["qrr"][:], rows["k10r"][:], rows["scr1"][:], op=ALU.mult))
            vop(v.tensor_tensor(rows["qir"][:], rows["k10i"][:], rows["k11i"][:], op=ALU.mult))
            vop(v.tensor_tensor(rows["qrr"][:], rows["qrr"][:], rows["qir"][:], op=ALU.add))
            vop(v.tensor_tensor(rows["qrr"][:], rows["qrr"][:], rows["scr2"][:], op=ALU.mult))
            vop(v.tensor_tensor(rows["qir"][:], rows["k10i"][:], rows["scr1"][:], op=ALU.mult))
            vop(v.tensor_tensor(rows["scr1"][:], rows["k10r"][:], rows["k11i"][:], op=ALU.mult))
            vop(v.tensor_tensor(rows["qir"][:], rows["qir"][:], rows["scr1"][:], op=ALU.subtract))
            n = vop(v.tensor_tensor(rows["qir"][:], rows["qir"][:], rows["scr2"][:], op=ALU.mult))
            pr.marks["kqrow_%d" % ci] = n
            for bi, nm in enumerate(["qr", "qi"]):
                v.wait_ge(plan.sem(PS), pb + 8 + bi + 1)
                vop(v.tensor_copy(wb[nm][:], ps1[0:64, 0:W]))
                plan.inc("kev", v.nop())
            vop(v.tensor_scalar(t_a[:], wb["qr"][:], s["p_P_re"][:], None, op0=ALU.mult))
            vop(v.scalar_tensor_tensor(t_a[:], wb["qi"][:], cols[:, 6:7], t_a[:],
                                       op0=ALU.mult, op1=ALU.add))
            vop(v.tensor_scalar(t_a[:], t_a[:], -1.0, s["p_B_re"][:], op0=ALU.mult, op1=ALU.add))
            vop(v.tensor_scalar(t_b[:], wb["qi"][:], s["p_P_re"][:], None, op0=ALU.mult))
            vop(v.scalar_tensor_tensor(t_b[:], wb["qr"][:], s["p_P_im"][:], t_b[:],
                                       op0=ALU.mult, op1=ALU.add))
            vop(v.tensor_scalar(t_b[:], t_b[:], -1.0, s["p_B_im"][:], op0=ALU.mult, op1=ALU.add))
            if ci > 0:
                v.wait_ge(plan.sem("kgw"), 32 * ci)
            vop(v.tensor_tensor(t_c[:], cre[:], t_a[:], op=ALU.mult))
            vop(v.tensor_tensor(gre[:], cim[:], t_b[:], op=ALU.mult))
            vop(v.tensor_tensor(t_c[:], t_c[:], gre[:], op=ALU.subtract))
            vop(v.tensor_tensor(gim[:], cre[:], t_b[:], op=ALU.mult))
            vop(v.tensor_tensor(gre[:], cim[:], t_a[:], op=ALU.mult))
            vop(v.tensor_tensor(gim[:], gim[:], gre[:], op=ALU.add))
            vop(v.tensor_tensor(gre[:], wb["tzre"][:], t_c[:], op=ALU.mult))
            vop(v.tensor_tensor(t_a[:], wb["tzim"][:], gim[:], op=ALU.mult))
            vop(v.tensor_tensor(gre[:], gre[:], t_a[:], op=ALU.subtract))
            vop(v.tensor_tensor(gim[:], wb["tzre"][:], gim[:], op=ALU.mult))
            vop(v.tensor_tensor(t_a[:], wb["tzim"][:], t_c[:], op=ALU.mult))
            n = vop(v.tensor_tensor(gim[:], gim[:], t_a[:], op=ALU.add))
            pr.marks["kg_done_%d" % ci] = n

    @block.tensor
    def _(te):
        te.wait_ge(plan.sem("cl"), plan.val("cl"))
        te.wait_ge(plan.sem(VS), pr.marks["kg_r2"])
        plan.inc(PS, te.matmul(ps0[0:64, 0:1], s["ones32r"][0:1, 0:64],
                               smallc[0:1, 2:3], start=True, stop=True,
                               skip_group_check=True))
        nev = 0
        nev2 = 0
        for ci in range(NCH):
            for bi, nm in enumerate(["r_w_re", "r_w_im", "r_tz_re", "r_tz_im"]):
                nev += 1
                te.wait_ge(plan.sem("kev"), nev)
                plan.inc(PS, te.matmul(ps1[0:64, 0:W], s["ones32r"][0:1, 0:64],
                                       s[nm][0:1, W * ci:W * (ci + 1)],
                                       start=True, stop=True, skip_group_check=True))
            for term in range(4):
                te.wait_ge(plan.sem(VS), pr.marks["kterm_%d_%d" % (ci, term)])
                nev2 += 1
                if nev2 > 1:
                    te.wait_ge(plan.sem("kev2"), nev2 - 1)
                plan.inc(PS, te.matmul(ps0[0:1, 0:W], s["ones64c"][:], pr.s["kg_tc"][:],
                                       start=True, stop=True, skip_group_check=True))
            te.wait_ge(plan.sem(VS), pr.marks["kqrow_%d" % ci])
            for src in [pr.s["kg_r_qrr"], pr.s["kg_r_qir"]]:
                nev += 1
                te.wait_ge(plan.sem("kev"), nev)
                plan.inc(PS, te.matmul(ps1[0:64, 0:W], s["ones32r"][0:1, 0:64],
                                       src[0:1, 0:W], start=True, stop=True,
                                       skip_group_check=True))

    @block.sync
    def _(sy):
        gv = pr.d["g_agin"].rearrange("(n x) c -> n (x c)", x=16)
        for ci in range(NCH):
            sy.wait_ge(plan.sem(VS), pr.marks["kg_done_%d" % ci])
            dma_in(plan, sy, gv[0:64, W * ci:W * (ci + 1)], pr.s["kg_gre"][:], "kgw")
            dma_in(plan, sy, gv[64:128, W * ci:W * (ci + 1)], pr.s["kg_gim"][:], "kgw")

    @block.gpsimd
    def _(g):
        g.wait_ge(plan.sem("kgw"), 32 * NCH)
        plan.inc("cc_g", g.collective_compute(
            "AllGather", ALU.bypass, replica_groups=[list(range(NCORES))],
            ins=[pr.d["g_agin"].opt()], outs=[pr.d["g_agout"].opt()]))


def emit_basis(pr, block):
    plan = pr.plan
    s = pr.s

    gret = pr.sbuf("kb_gre", [128, 128], BF16)
    gimt = pr.sbuf("kb_gim", [128, 128], BF16)
    a1s = pr.sbuf("kb_a1", [128, 256], BF16)
    a2s = pr.sbuf("kb_a2", [128, 256], BF16)
    krs = pr.sbuf("kb_kr", [128, 128], BF16)
    kis = pr.sbuf("kb_ki", [128, 128], BF16)
    bsb = pr.sbuf("kb_bs", [128, 258], BF16)
    cb = pr.sbuf("kb_c", [128, 258], BF16)
    xb = pr.sbuf("kb_x", [128, 258], BF16)
    gidx = pr.s["kb_gidx"]
    ps2 = pr.psum("kb_ps2", [128, 512], F32)
    ps3 = pr.psum("kb_ps3", [128, 512], F32)
    ps4 = pr.psum("kb_ps4", [128, 512], F32)
    ps5 = pr.psum("kb_ps5", [128, 512], F32)

    @block.gpsimd
    def _(g):
        ngi = dma_in(plan, g, gidx[:], pr.d["gidx"], "cl")
        g.wait_ge(plan.sem("cl"), ngi)
        g.wait_ge(plan.sem("cc_g"), 1)
        for j in range(EPC):
            if j > 0:
                g.wait_ge(plan.sem("kbp"), 14 * (j - 1) + 2)
            plan.inc("kbg", g.indirect_dma_start(
                out=gret[:], out_offset=None, in_=pr.d["g_agout"],
                in_offset=bass.IndirectOffsetOnAxis(ap=gidx[:, 2 * j:2 * j + 1], axis=0),
                bounds_check=pr.bnd16383), 16)
            plan.inc("kbg", g.indirect_dma_start(
                out=gimt[:], out_offset=None, in_=pr.d["g_agout"],
                in_offset=bass.IndirectOffsetOnAxis(ap=gidx[:, 2 * j + 1:2 * j + 2], axis=0),
                bounds_check=pr.bnd16383), 16)

    @block.tensor
    def _(te):
        for j in range(EPC):
            te.wait_ge(plan.sem("kbg"), 32 * (j + 1))
            if j > 0:
                te.wait_ge(plan.sem("kbe"), 5 * (j - 1) + 3)
            te.ldweights(gret[:])
            plan.inc("kbp", te.matmul(ps2[:, 0:256], gret[:], s["ifab"][:],
                                      start=True, stop=False, skip_group_check=True))
            te.ldweights(gimt[:])
            plan.inc("kbp", te.matmul(ps2[:, 0:256], gimt[:], s["ifbb"][:],
                                      start=False, stop=True, skip_group_check=True))
            te.wait_ge(plan.sem("kbv"), 4 * j + 4)
            te.ldweights(s["ifbre"][:])
            plan.inc("kbp", te.matmul(ps3[:, 0:128], s["ifbre"][:], a2s[:, 0:128],
                                      start=True, stop=False, skip_group_check=True))
            te.ldweights(s["ifbimn"][:])
            plan.inc("kbp", te.matmul(ps3[:, 0:128], s["ifbimn"][:], a2s[:, 128:256],
                                      start=False, stop=True, skip_group_check=True))
            te.ldweights(s["ifbre"][:])
            plan.inc("kbp", te.matmul(ps3[:, 128:256], s["ifbre"][:], a2s[:, 128:256],
                                      start=True, stop=False, skip_group_check=True))
            te.ldweights(s["ifbim"][:])
            plan.inc("kbp", te.matmul(ps3[:, 128:256], s["ifbim"][:], a2s[:, 0:128],
                                      start=False, stop=True, skip_group_check=True))
            for r in range(2):
                src = krs if r == 0 else kis
                te.wait_ge(plan.sem("kbe"), 5 * j + 2 + r)
                te.wait_ge(plan.sem("kbe"), (5 * j + 4) if r == 1 else (5 * j if j > 0 else 0))
                te.wait_ge(plan.sem("kbe2"), 2 * j + r)
                te.ldweights(src[:])
                plan.inc("kbp", te.matmul(ps4[:, 0:258], src[:], s["f1"][:],
                                          start=True, stop=True, skip_group_check=True))
                te.wait_ge(plan.sem("kbv2"), (2 * j + r) * 4 + 4)
                te.ldweights(s["f2re"][:])
                plan.inc("kbp", te.matmul(ps5[:, 0:258], s["f2re"][:], cb[:, 0:258],
                                          start=True, stop=False, skip_group_check=True))
                te.ldweights(s["f2imn"][:])
                plan.inc("kbp", te.matmul(ps5[:, 0:129], s["f2imn"][:], cb[:, 129:258],
                                          start=False, stop=False, skip_group_check=True))
                te.ldweights(s["f2im"][:])
                plan.inc("kbp", te.matmul(ps5[:, 129:258], s["f2im"][:], cb[:, 0:129],
                                          start=False, stop=True, skip_group_check=True))

    @block.scalar
    def _(sc):
        for j in range(EPC):
            sc.wait_ge(plan.sem("kbp"), 14 * j + 2)
            sc.wait_ge(plan.sem("kbv"), 4 * j)
            plan.inc("kbe", sc.copy(a1s[:], ps2[:, 0:256]))
            sc.wait_ge(plan.sem("kbp"), 14 * j + 4)
            plan.inc("kbe", sc.copy(krs[:], ps3[:, 0:128]))
            sc.wait_ge(plan.sem("kbp"), 14 * j + 6)
            plan.inc("kbe", sc.copy(kis[:], ps3[:, 128:256]))
            for r in range(2):
                sc.wait_ge(plan.sem("kbp"), 14 * j + 7 + 4 * r)
                sc.wait_ge(plan.sem("kbv2"), (2 * j + r) * 4)
                plan.inc("kbe", sc.copy(bsb[:], ps4[:, 0:258]))
                sc.wait_ge(plan.sem("kbp"), 14 * j + 10 + 4 * r)
                sc.wait_ge(plan.sem("kbw"), 64 * (2 * j + r))
                plan.inc("kbe2", sc.copy(xb[:], ps5[:, 0:258]))

    cs1 = pr.sbuf("kb_cs1", [128, 258], BF16)
    cs2 = pr.sbuf("kb_cs2", [128, 258], BF16)

    def cmul(v, plan, sem, dst, src, tr, ti, w, selfsem="vself"):
        a, b = cs1, cs2
        plan.inc(sem, v.tensor_tensor(a[:, 0:2 * w], src, tr, op=ALU.mult))
        n = plan.inc(selfsem, v.tensor_tensor(b[:, 0:2 * w], src, ti, op=ALU.mult))
        plan.inc(sem, v.nop())
        v.wait_ge(plan.sem(selfsem), n)
        plan.inc(sem, v.tensor_tensor(dst[:, 0:w], a[:, 0:w], b[:, w:2 * w],
                                      op=ALU.subtract))
        plan.inc(sem, v.tensor_tensor(dst[:, w:2 * w], a[:, w:2 * w], b[:, 0:w],
                                      op=ALU.add))

    @block.vector
    def _(v):
        for j in range(EPC):
            v.wait_ge(plan.sem("kbe"), 5 * j + 1)
            if j > 0:
                v.wait_ge(plan.sem("kbp"), 14 * (j - 1) + 6)
            cmul(v, plan, "kbv", a2s, a1s[:, 0:256], s["iftr2"][:], s["ifti2"][:], 128)
            for r in range(2):
                v.wait_ge(plan.sem("kbe"), 5 * j + 4 + r)
                if 2 * j + r > 0:
                    pi = 14 * j + 10 if r == 1 else 14 * (j - 1) + 14
                    v.wait_ge(plan.sem("kbp"), pi)
                cmul(v, plan, "kbv2", cb, bsb[:, 0:258], s["twr2_4"][:, 0:258],
                     s["twi2_4"][:, 0:258], 129)

    @block.sync
    def _(sy):
        for j in range(EPC):
            for r in range(2):
                row = j if r == 0 else 8 + j
                sy.wait_ge(plan.sem("kbe2"), 2 * j + r + 1)
                bv = pr.d["bas_agin"][row:row + 1, :].rearrange("a (k w) -> (a k) w", w=516)
                dma_in(plan, sy, bv[:, 0:129], xb[:, 0:129], "kbw")
                dma_in(plan, sy, bv[:, 129:258], xb[:, 0:129], "kbw")
                dma_in(plan, sy, bv[:, 258:387], xb[:, 129:258], "kbw")
                dma_in(plan, sy, bv[:, 387:516], xb[:, 129:258], "kbw")

    @block.gpsimd
    def _(g):
        g.wait_ge(plan.sem("kbw"), 64 * 16)
        plan.inc("cc_b", g.collective_compute(
            "AllGather", ALU.bypass, replica_groups=[list(range(NCORES))],
            ins=[pr.d["bas_agin"].opt()], outs=[pr.d["bas_agout"].opt()]))


def emit_cmix(pr, block):
    plan = pr.plan
    s = pr.s
    NCW = 65
    TOT = 129 * 512

    def cw(c):
        return min(1024 * (c + 1), TOT) - 1024 * c

    cmixb = pr.sbuf("cm_lhs", [128, 128], BF16)
    rhs = pr.sbuf("cm_rhs", [128, 4 * 1024], BF16)
    stg = pr.sbuf("cm_stg", [128, 4 * 1024], BF16)
    cps = pr.psum("cm_ps", [128, 2048], F32)

    @block.vector
    def _(v):
        v.wait_ge(plan.sem("cl"), plan.val("cl"))
        plan.inc("cmv", v.tensor_scalar(cmixb[:], s["scmix_pre"][:], s["sign_col"][:],
                                        None, op0=ALU.mult))

    @block.sync
    def _(sy):
        def store(j):
            sy.wait_ge(plan.sem("cme"), j + 1)
            plan.inc("cmw%d" % (j % 4), sy.dma_start(
                out=pr.d["kf"][:, 1024 * j:1024 * j + cw(j)],
                in_=stg[:, (j % 4) * 1024:(j % 4) * 1024 + cw(j)]), 16)

        sy.wait_ge(plan.sem("cc_b"), 1)
        for c in range(0, NCW, 2):
            if c >= 4:
                sy.wait_ge(plan.sem("cmp"), c - 3)
            dma_in(plan, sy, rhs[:, (c % 4) * 1024:(c % 4) * 1024 + cw(c)],
                   pr.d["bas_agout"][:, 1024 * c:1024 * c + cw(c)], "cml0")
            if c >= 6:
                store(c - 6)
                store(c - 5)
        for j in range(NCW - 5, NCW):
            store(j)

    @block.gpsimd
    def _(g):
        g.wait_ge(plan.sem("cc_b"), 1)
        for c in range(1, NCW, 2):
            if c >= 4:
                g.wait_ge(plan.sem("cmp"), c - 3)
            plan.inc("cmlg", g.dma_start(
                out=rhs[:, (c % 4) * 1024:(c % 4) * 1024 + cw(c)],
                in_=pr.d["bas_agout"][:, 1024 * c:1024 * c + cw(c)]), 16)

    @block.tensor
    def _(te):
        te.wait_ge(plan.sem("cmv"), 1)
        te.ldweights(cmixb[:])
        for c in range(NCW):
            te.wait_ge(plan.sem("cml0" if c % 2 == 0 else "cmlg"),
                       16 * min(c // 2 + 2, 33 if c % 2 == 0 else 32))
            if c >= 2:
                te.wait_ge(plan.sem("cme"), c - 1)
            for hh in range(0, cw(c), 512):
                _mm = te.matmul(cps[:, (c % 2) * 1024 + hh:(c % 2) * 1024 + hh + 512],
                                cmixb[:],
                                rhs[:, (c % 4) * 1024 + hh:(c % 4) * 1024 + hh + 512],
                                start=True, stop=True, skip_group_check=True)
            plan.inc("cmp", _mm)

    @block.scalar
    def _(sc):
        for c in range(NCW):
            sc.wait_ge(plan.sem("cmp"), c + 1)
            if c >= 4:
                sc.wait_ge(plan.sem("cmw%d" % (c % 4)), 16 * ((c - 4) // 4 + 1))
            plan.inc("cme", sc.copy(stg[:, (c % 4) * 1024:(c % 4) * 1024 + cw(c)],
                                    cps[:, (c % 2) * 1024:(c % 2) * 1024 + cw(c)]))


def emit_conv(pr, block):
    plan = pr.plan
    s = pr.s
    NG = NSEQ // 4

    xin = pr.sbuf("cv_xin", [128, 4 * 128], BF16)
    xn = pr.sbuf("cv_xn", [128, 8 * 128], BF16)
    c4 = pr.sbuf("cv_c", [128, 8 * 258], BF16)
    y4 = pr.sbuf("cv_y", [128, 8 * 258], BF16)
    kf4 = pr.sbuf("cv_kf", [128, 8 * 516], BF16)
    pt4 = pr.sbuf("cv_pt", [128, 8 * 512], BF16)
    pp4 = pr.sbuf("cv_pp", [128, 8 * 256], BF16)
    hs4 = pr.sbuf("cv_hs", [128, 2 * 512], F32)
    sm4 = pr.sbuf("cv_sm", [128, 2 * 512], BF16)
    mus = pr.sbuf("cv_mu", [128, 4 * 128], BF16)
    muq = pr.sbuf("cv_muq", [128, 512], BF16)
    rsq = pr.sbuf("cv_rsq", [128, 512], BF16)
    dcol = pr.sbuf("cv_dcol", [128, 1], F32)
    psA = pr.psum("cv_ps", [128, 4096], F32)

    tw1 = pr.sbuf("cv_tw1", [128, 2 * 516], BF16)
    tw2 = pr.sbuf("cv_tw2", [128, 2 * 516], BF16)
    pw1 = pr.sbuf("cv_pw1", [128, 2 * 516], BF16)
    pw2 = pr.sbuf("cv_pw2", [128, 2 * 516], BF16)
    it1 = pr.sbuf("cv_it1", [128, 2 * 512], BF16)
    it2 = pr.sbuf("cv_it2", [128, 2 * 512], BF16)

    @block.gpsimd
    def _(g):
        g.wait_ge(plan.sem("cc_st"), 1)
        stv = pr.d["st_agout"].rearrange("(q w) c -> q w c", w=2)
        for b in range(B):
            dma_in(plan, g, mus[:, b * 128:(b + 1) * 128],
                   stv[128 * b:128 * (b + 1), 0, :], "cmu")
            dma_in(plan, g, mus[:, (2 + b) * 128:(3 + b) * 128],
                   stv[128 * b:128 * (b + 1), 1, :], "cmu")
        for sq in range(NSEQ):
            chl, b = sq // 2, sq % 2
            if sq >= 4:
                g.wait_ge(plan.sem("cln"), sq // 4)
            xv = pr.d["x_ch"][chl, b, :].rearrange("(a c) -> a c", c=128)
            dma_in(plan, g, xin[:, (sq % 4) * 128:(sq % 4 + 1) * 128], xv,
                   "cxlp%d" % ((sq // 2) % 2))

    @block.sync
    def _(sy):
        sy.wait_ge(plan.sem("cmw0"), 16 * 17)
        sy.wait_ge(plan.sem("cmw1"), 16 * 16)
        sy.wait_ge(plan.sem("cmw2"), 16 * 16)
        sy.wait_ge(plan.sem("cmw3"), 16 * 16)
        for sq in range(NSEQ):
            chl = sq // 2
            if sq >= 8:
                sy.wait_ge(plan.sem("cpw"), (sq - 8) // 2 + 1)
            dma_in(plan, sy, kf4[:, (sq % 8) * 516:(sq % 8 + 1) * 516],
                   pr.d["kf"][chl:chl + 1, :].rearrange("a (k w) -> (a k) w", w=516),
                   "cklp%d" % ((sq // 4) % 2))

    @block.vector
    def _(v):
        v.wait_ge(plan.sem("cmu"), 64)
        plan.inc("vself", v.tensor_copy(muq[:, 0:256], mus[:, 0:256]))
        plan.inc("vself", v.tensor_copy(muq[:, 256:512], mus[:, 0:256]))
        plan.inc("vself", v.tensor_copy(rsq[:, 0:256], mus[:, 256:512]))
        n = plan.inc("vself", v.tensor_copy(rsq[:, 256:512], mus[:, 256:512]))
        v.wait_ge(plan.sem("vself"), n)
        v.wait_ge(plan.sem("kbe2"), 16)
        plan.inc("vself", v.memset(psA[:, 2048 + 256:2048 + 512], 0.0))
        plan.inc("vself", v.memset(psA[:, 2560 + 256:2560 + 512], 0.0))
        v.wait_ge(plan.sem("cdp"), 1)
        n = plan.inc("vself", v.tensor_copy(dcol[:], psA[0:128, 0:1]))
        v.wait_ge(plan.sem("vself"), n)
        plan.inc("cdv", v.nop())
        B1v = psA[:, 0:1024].rearrange("p (s w) -> p s w", w=512)
        Xv = psA[:, 1024:2048].rearrange("p (s w) -> p s w", w=512)
        PTv = psA[:, 2048:3072].rearrange("p (s w) -> p s w", w=512)
        t_r = s["twr2_4"][:, 0:516].rearrange("p (s w) -> p s w", w=258)
        t_i = s["twi2_4"][:, 0:516].rearrange("p (s w) -> p s w", w=258)
        it_r = s["itr2_4"][:, 0:512].rearrange("p (s w) -> p s w", w=256)
        it_i = s["iti2_4"][:, 0:512].rearrange("p (s w) -> p s w", w=256)
        for g_i in range(NG):
            gp = g_i % 2
            v.wait_ge(plan.sem("cxlp0"), 32 * (g_i + 1))
            v.wait_ge(plan.sem("cxlp1"), 32 * (g_i + 1))
            if g_i >= 2:
                v.wait_ge(plan.sem("cpe_s1"), 4 * (g_i - 1))
                v.wait_ge(plan.sem("cepi"), g_i - 1)
            xnv4 = xn[:, (4 * g_i % 8) * 128:((4 * g_i % 8) + 4) * 128]
            nn = plan.inc("vself", v.tensor_tensor(xnv4, xin[:], muq[:],
                                                   op=ALU.subtract))
            v.wait_ge(plan.sem("vself"), nn)
            plan.inc("cln", v.tensor_tensor(xnv4, xnv4, rsq[:], op=ALU.mult))
            for h in range(2):
                cs = (4 * g_i + 2 * h) % 8
                v.wait_ge(plan.sem("cpe_s1"), 4 * g_i + 2 * h + 2)
                if g_i >= 2:
                    v.wait_ge(plan.sem("cpe_s2"), 4 * (g_i - 2) + 2 * h + 2)
                dst = c4[:, cs * 258:(cs + 2) * 258].rearrange("p (s w) -> p s w", w=258)
                a_v = tw1[:, h * 516:(h + 1) * 516].rearrange("p (s w) -> p s w", w=258)
                b_v = tw2[:, h * 516:(h + 1) * 516].rearrange("p (s w) -> p s w", w=258)
                plan.inc("ctw", v.tensor_tensor(a_v, B1v[:, :, 0:258], t_r, op=ALU.mult))
                n = plan.inc("vself", v.tensor_tensor(b_v, B1v[:, :, 0:258], t_i,
                                                      op=ALU.mult))
                v.wait_ge(plan.sem("vself"), n)
                plan.inc("ctw", v.tensor_tensor(dst[:, :, 0:129], a_v[:, :, 0:129],
                                                b_v[:, :, 129:258], op=ALU.subtract))
                plan.inc("ctw", v.tensor_tensor(dst[:, :, 129:258], a_v[:, :, 129:258],
                                                b_v[:, :, 0:129], op=ALU.add))
            for h in range(2):
                cs = (4 * g_i + 2 * h) % 8
                v.wait_ge(plan.sem("cpe_s2"), 4 * g_i + 2 * h + 2)
                v.wait_ge(plan.sem("cklp%d" % (g_i % 2)), 64 * (g_i // 2 + 1))
                if g_i >= 2:
                    v.wait_ge(plan.sem("cpe_is2"), 4 * (g_i - 2) + 2 * h + 2)
                kfp = kf4[:, cs * 516:(cs + 2) * 516].rearrange("p (s w) -> p s w", w=516)
                yp = y4[:, cs * 258:(cs + 2) * 258].rearrange("p (s w) -> p s w", w=258)
                a_v = pw1[:, h * 516:(h + 1) * 516].rearrange("p (s w) -> p s w", w=258)
                b_v = pw2[:, h * 516:(h + 1) * 516].rearrange("p (s w) -> p s w", w=258)
                plan.inc("cpw0", v.tensor_tensor(a_v, Xv[:, :, 0:258],
                                                 kfp[:, :, 0:258], op=ALU.mult))
                n = plan.inc("vself", v.tensor_tensor(b_v, Xv[:, :, 0:258],
                                                      kfp[:, :, 258:516], op=ALU.mult))
                v.wait_ge(plan.sem("vself"), n)
                plan.inc("cpw0", v.tensor_tensor(yp[:, :, 0:129], a_v[:, :, 0:129],
                                                 b_v[:, :, 129:258], op=ALU.subtract))
                plan.inc("cpw", v.tensor_tensor(yp[:, :, 129:258], a_v[:, :, 129:258],
                                                b_v[:, :, 0:129], op=ALU.add))
            v.wait_ge(plan.sem("cet"), 4 * g_i + 4)
            if g_i >= 2:
                v.wait_ge(plan.sem("cpe_is1"), g_i - 1)
            ptv = pt4[:, gp * 2048:(gp + 1) * 2048].rearrange("p (g w) -> p g w", w=512)
            ppr = pp4[:, gp * 1024:(gp + 1) * 1024].rearrange("p (g w) -> p g w", w=256)
            a4 = it1[:, 0:1024].rearrange("p (g w) -> p g w", w=256)
            b4 = it2[:, 0:1024].rearrange("p (g w) -> p g w", w=256)
            plan.inc("citw0", v.tensor_tensor(
                a4, ptv[:, :, 0:256],
                s["itr2_4"][:].rearrange("p (g w) -> p g w", w=256), op=ALU.mult))
            n = plan.inc("vself", v.tensor_tensor(
                b4, ptv[:, :, 0:256],
                s["iti2_4"][:].rearrange("p (g w) -> p g w", w=256), op=ALU.mult))
            v.wait_ge(plan.sem("vself"), n)
            plan.inc("citw0", v.tensor_tensor(ppr[:, :, 0:128], a4[:, :, 0:128],
                                              b4[:, :, 128:256], op=ALU.subtract))
            plan.inc("citw", v.tensor_tensor(ppr[:, :, 128:256], a4[:, :, 128:256],
                                             b4[:, :, 0:128], op=ALU.add))
            v.wait_ge(plan.sem("cpe_is1"), g_i + 1)
            if g_i >= 2:
                v.wait_ge(plan.sem("cgel"), g_i - 1)
            plan.inc("cepi", v.scalar_tensor_tensor(
                hs4[:, gp * 512:(gp + 1) * 512],
                xn[:, (4 * g_i % 8) * 128:((4 * g_i % 8) + 4) * 128], dcol[:],
                psA[:, 512 * (6 + gp):512 * (7 + gp)], op0=ALU.mult, op1=ALU.add))

    @block.tensor
    def _(te):
        te.wait_ge(plan.sem("cl"), plan.val("cl"))
        plan.inc("cdp", te.matmul(psA[0:128, 0:1], s["ones32r"][0:1, :],
                                  s["p_Dv"][0:1, 0:1], start=True, stop=True,
                                  skip_group_check=True))
        for g_i in range(NG):
            gp = g_i % 2
            for k in range(4):
                sq = 4 * g_i + k
                te.wait_ge(plan.sem("cln"), sq // 4 + 1)
                if sq >= 2:
                    te.wait_ge(plan.sem("ctw"), 3 * (sq // 2))
                if sq == 0:
                    te.wait_ge(plan.sem("cdv"), 1)
                te.ldweights(xn[:, (sq % 8) * 128:(sq % 8 + 1) * 128])
                plan.inc("cpe_s1", te.matmul(psA[:, 512 * (sq % 2):512 * (sq % 2) + 258],
                                             xn[:, (sq % 8) * 128:(sq % 8 + 1) * 128],
                                             s["f1"][:], start=True, stop=True,
                                             skip_group_check=True))
            for half in range(2):
                te.wait_ge(plan.sem("ctw"), 3 * (2 * g_i + half + 1))
                for nm, rhs_lo, rhs_hi, out_lo, acc in [
                        ("f2re", 0, 258, 0, "start"),
                        ("f2imn", 129, 258, 0, "mid"),
                        ("f2im", 0, 129, 129, "stop")]:
                    te.ldweights(s[nm][:])
                    for k in range(2 * half, 2 * half + 2):
                        sq = 4 * g_i + k
                        if nm == "f2re" and sq >= 2:
                            te.wait_ge(plan.sem("cpw"), sq // 2)
                        slot = (sq % 8) * 258
                        cveiw = c4[:, slot + rhs_lo: slot + rhs_hi]
                        _mm = te.matmul(psA[:, 512 * (2 + sq % 2) + out_lo:
                                            512 * (2 + sq % 2) + out_lo + (rhs_hi - rhs_lo)],
                                 s[nm][:], cveiw,
                                 start=(nm == "f2re"), stop=(nm == "f2im"),
                                 skip_group_check=True)
                        if nm == "f2im":
                            plan.inc("cpe_s2", _mm)
            for k in range(4):
                sq = 4 * g_i + k
                te.wait_ge(plan.sem("cpw"), 2 * g_i + k // 2 + 1)
                if sq >= 2:
                    te.wait_ge(plan.sem("cet"), sq - 1)
                slot = (sq % 8) * 258
                pts = psA[:, 512 * (4 + sq % 2):512 * (5 + sq % 2)]
                te.ldweights(y4[:, slot:slot + 128])
                te.matmul(pts[:, 0:256], y4[:, slot:slot + 128],
                          s["ra"][:], start=True, stop=False, skip_group_check=True)
                te.ldweights(y4[:, slot + 129:slot + 257])
                te.matmul(pts[:, 0:256], y4[:, slot + 129:slot + 257],
                          s["rb"][:], start=False, stop=True, skip_group_check=True)
                te.ldweights(y4[:, slot + 128:slot + 129])
                te.matmul(pts[0:1, 256:512], y4[:, slot + 128:slot + 129],
                          s["rae"][:], start=True, stop=False, skip_group_check=True)
                te.ldweights(y4[:, slot + 257:slot + 258])
                plan.inc("cpe_is2", te.matmul(pts[0:1, 256:512], y4[:, slot + 257:slot + 258],
                                              s["rbe"][:], start=False, stop=True,
                                              skip_group_check=True))
            te.wait_ge(plan.sem("citw"), g_i + 1)
            if g_i >= 2:
                te.wait_ge(plan.sem("cepi"), g_i - 1)
            hb = psA[:, 512 * (6 + gp):512 * (7 + gp)]
            gpp = g_i % 2
            for nm, src_off, whalf in [("g1re", 0, 0), ("g1imn", 128, 0),
                                       ("g1er", 0, 1), ("g1ein", 128, 1)]:
                te.ldweights(s[nm][:])
                for k in range(4):
                    sq = 4 * g_i + k
                    if whalf == 0:
                        rhsv = pp4[:, gpp * 1024 + (sq % 4) * 256 + src_off:
                                   gpp * 1024 + (sq % 4) * 256 + src_off + 128]
                    else:
                        rhsv = pt4[0:1, gpp * 2048 + (sq % 4) * 512 + 256 + src_off:
                                   gpp * 2048 + (sq % 4) * 512 + 256 + src_off + 128]
                    _mm = te.matmul(hb[:, (sq % 4) * 128:(sq % 4 + 1) * 128],
                                    s[nm][:], rhsv,
                                    start=(nm == "g1re"), stop=(nm == "g1ein"),
                                    skip_group_check=True)
                    if nm == "g1ein" and k == 3:
                        plan.inc("cpe_is1", _mm)

    @block.scalar
    def _(sc):
        for g_i in range(NG):
            gp = g_i % 2
            for k in range(4):
                sq = 4 * g_i + k
                sc.wait_ge(plan.sem("cpe_is2"), sq + 1)
                if sq >= 8:
                    sc.wait_ge(plan.sem("cpe_is1"), (sq - 8) // 4 + 1)
                plan.inc("cet", sc.copy(pt4[:, (sq % 8) * 512:(sq % 8 + 1) * 512],
                                        psA[:, 512 * (4 + sq % 2):512 * (5 + sq % 2)]))
            sc.wait_ge(plan.sem("cepi"), g_i + 1)
            if g_i >= 2:
                sc.wait_ge(plan.sem("cst%d" % (g_i % 2)), 64 * (g_i // 2))
            ng = plan.inc("cgel", sc.activation(sm4[:, gp * 512:(gp + 1) * 512],
                                                hs4[:, gp * 512:(gp + 1) * 512], AF.Gelu))
            sc.wait_ge(plan.sem("cgel"), ng)
            for k in range(4):
                sq = 4 * g_i + k
                chl, b = sq // 2, sq % 2
                dst = pr.d["ssm_agin"][chl, b * L:(b + 1) * L].rearrange(
                    "(a c) -> a c", c=128)
                plan.inc("cst%d" % (g_i % 2), sc.dma_start(
                    out=dst, in_=sm4[:, gp * 512 + k * 128:gp * 512 + (k + 1) * 128]), 16)

    @block.gpsimd
    def _(g):
        g.wait_ge(plan.sem("cst0"), 64 * (NG // 2))
        g.wait_ge(plan.sem("cst1"), 64 * (NG // 2))
        plan.inc("cc_ssm", g.collective_compute(
            "AllGather", ALU.bypass, replica_groups=[list(range(NCORES))],
            ins=[pr.d["ssm_agin"].opt()], outs=[pr.d["ssm_agout"].opt()]))


def _shard_ssm_inputs(core, x, prm):
    c = consts()
    hc = slice(CPC * core, CPC * (core + 1))
    xf = np.ascontiguousarray(x.reshape(T, HD))
    m = {
        "x_full": xf,
        "x_ts": np.ascontiguousarray(xf[TPC * core:TPC * (core + 1)]),
        "x_ch": np.ascontiguousarray(x[:, :, hc].transpose(2, 0, 1)),
        "B_re": prm["B_re"].reshape(NMODE, 1), "B_im": prm["B_im"].reshape(NMODE, 1),
        "P_re": prm["P_re"].reshape(NMODE, 1), "P_im": prm["P_im"].reshape(NMODE, 1),
        "Q_re": prm["Q_re"].reshape(NMODE, 1), "Q_im": prm["Q_im"].reshape(NMODE, 1),
        "diag_re": prm["diag_re"].reshape(NMODE, 1),
        "diag_im": prm["diag_im"].reshape(NMODE, 1),
        "step": prm["step"].reshape(1, 1), "Dv": prm["D"].reshape(1, 1),
        "C_re": np.ascontiguousarray(prm["C_re"][hc]),
        "C_im": np.ascontiguousarray(prm["C_im"][hc]),
        "w_re": c["w_re"][2048 * core:2048 * (core + 1)].reshape(1, 2048),
        "w_im": c["w_im"][2048 * core:2048 * (core + 1)].reshape(1, 2048),
        "tz_re": c["tz_re"][2048 * core:2048 * (core + 1)].reshape(1, 2048),
        "tz_im": c["tz_im"][2048 * core:2048 * (core + 1)].reshape(1, 2048),
    }
    cm = np.zeros((128, CPC), np.float32)
    for r in range(8):
        for j in range(8):
            cm[16 * r + j] = prm["C_re"][hc][:, 8 * r + j]
            cm[16 * r + 8 + j] = prm["C_im"][hc][:, 8 * r + j]
    m["cmix_pre"] = cm
    gi = np.zeros((128, 16), np.uint32)
    p = np.arange(128)
    for j in range(EPC):
        base = (p // 16) * 2048 + (8 * core + j) * 16 + (p % 16)
        gi[:, 2 * j] = base
        gi[:, 2 * j + 1] = base + 1024
    m["gidx"] = gi
    for nm in CONST_NAMES:
        arr = c[nm]
        m["c_" + nm] = np.asarray(arr)
    return m


def _build_ssm_test():
    import contextlib
    nc = bass.Bass("TRN2", target_bir_lowering=False)
    stack = contextlib.ExitStack()
    nc._test_stack = stack
    pr = Prog(nc, stack, 0)
    declare_conv_io(pr)
    out = pr.dram("ssm_out", [CPC, T], BF16, kind="ExternalOutput")
    agpeek = pr.dram("agpeek", [128, 128], BF16, kind="ExternalOutput")
    with nc.Block() as block:
        plan = pr.plan

        @block.gpsimd
        def _(g):
            conv_const_loads(pr, g)

        pr.push()
        emit_stats(pr, block)
        st_fence = [("sv", plan.val("sv")), ("ss2", plan.val("ss2")),
                    ("sve", plan.val("sve")), ("sgi", 16)]
        pr.pop()
        pr.push()
        _phase_wait(block, plan, pr, st_fence, [])
        pr.sbuf("kb_gidx", [128, 16], U32)
        pr.push()
        emit_kg(pr, block)
        pr.pop()
        emit_basis(pr, block)
        emit_cmix(pr, block)
        cm_fence = [("cmp", plan.val("cmp")), ("cme", plan.val("cme")),
                    ("cmw0", plan.val("cmw0")), ("cmw1", plan.val("cmw1")),
                    ("cmw2", plan.val("cmw2")), ("cmw3", plan.val("cmw3")),
                    ("kbe", plan.val("kbe")), ("kbe2", plan.val("kbe2")),
                    ("cmv", 1)]
        pr.pop()
        pr.push()
        _phase_wait(block, plan, pr, cm_fence, [])
        emit_conv(pr, block)
        pr.pop()

        @block.gpsimd
        def _(g):
            g.wait_ge(plan.sem("cc_ssm"), 1)
            dma_in(plan, g, out, pr.d["ssm_agin"], "fin")
            dma_in(plan, g, agpeek, pr.d["ssm_agout"][0:128, 0:128], "fin")
            g.wait_ge(plan.sem("fin"), 32)
    stack.close()
    return nc


def ssm_test_kernel(x, prm):
    nc = _build_ssm_test()
    in_maps = [_shard_ssm_inputs(cc, x, prm) for cc in range(NCORES)]
    res = run_bass_kernel_spmd(nc, in_maps, core_ids=list(range(NCORES)))
    return res


def declare_ffn_io(pr):
    pr.dram("gate_w", [HD, E], F32, kind="ExternalInput")
    pr.dram("shared_gate_w", [HD, 1], F32, kind="ExternalInput")
    pr.dram("sguw", [HD, 2 * ISH], F32, kind="ExternalInput")
    pr.dram("sdw", [ISH, HD], F32, kind="ExternalInput")
    pr.dram("egu_c", [EPC, HD, 2 * IEX], F32, kind="ExternalInput")
    pr.dram("edw_c", [EPC, IEX, HD], F32, kind="ExternalInput")
    pr.dram("rank_off", [1, 1], F32, kind="ExternalInput")
    pr.dram("tri_rk", [8, 1], F32, kind="ExternalInput")
    pr.dram("rank_i", [1, 1], I32, kind="ExternalInput")
    pr.dram("xnfm", [HD, TPC], BF16)
    pr.dram("idwn", [E * CAP, 4], F32)
    pr.dram("idwn_rs", [EPC * CAP, 4], F32)
    pr.dram("cnt_agin", [64, 1], F32)
    pr.dram("cnt_agout", [512, 1], F32)
    pr.dram("moe_out", [EPC * CAP, HD], BF16)
    pr.dram("moe_all", [E * CAP, HD], BF16)
    pr.dram("ssm_my", [HD, TPC], BF16)


def emit_routing(pr, block):
    plan = pr.plan
    s = pr.s
    nt = TPC // 128

    xts = pr.sbuf("rt_x", [128, 2 * HD], F32)
    xnb = pr.sbuf("rt_xn", [128, HD], BF16)
    fmst = pr.sbuf("rt_fm", [128, 2 * HD], BF16)
    lgt = pr.sbuf("rt_lg", [128, 64], F32)
    masked = pr.sbuf("rt_msk", [128, 64], F32)
    chos = pr.sbuf("rt_ch", [128, 64], F32)
    top8v = pr.sbuf("rt_t8v", [128, 8], F32)
    top8i = pr.sbuf("rt_t8i", [128, 8], U32)
    exp8 = pr.sbuf("rt_e8", [128, 8], F32)
    scr = pr.sbuf("rt_scr", [128, 8], F32)
    idx_all = pr.sbuf("rt_idxa", [128, 8 * nt], F32)
    wn_all = pr.sbuf("rt_wna", [128, 8 * nt], F32)
    chT = pr.sbuf("rt_chT", [64, TPC], F32)
    zt64 = pr.sbuf("rt_z64", [64, TPC], BF16)
    gfull = pr.sbuf("rt_gf", [64, TPC], F32)
    gtm = pr.sbuf("rt_gtm", [128, 64], F32)
    gsel = pr.sbuf("rt_gsel", [128, 8], F32)
    dstf = pr.sbuf("rt_dstf", [128, 8], F32)
    intl = pr.sbuf("rt_intl", [128, 64], F32)
    chos8 = pr.sbuf("rt_ch8", [128, 512], F32)
    msk8 = pr.sbuf("rt_mk8", [128, 512], F32)
    iota64 = pr.sbuf("rt_iota64", [128, 64], F32)
    iotap = pr.sbuf("rt_iotap", [128, 1], F32)
    rko = pr.sbuf("rt_rko", [128, 1], F32)
    cnt8 = pr.sbuf("rt_cnt8", [8, 64], F32)
    cntc = pr.sbuf("rt_cntc", [64, 1], F32)
    pcol = pr.sbuf("rt_pcol", [64, 1], F32)
    gwb = pr.sbuf("rt_gw", [128, 8 * 64], BF16)
    sgwb = pr.sbuf("rt_sgw", [128, 8], BF16)
    ztile = pr.sbuf("rt_ztile", [128, 1024], F32)
    rof = pr.sbuf("rt_rof", [1, 1], F32)
    trk = pr.sbuf("rt_trk", [8, 1], F32)
    pbf = pr.psum("rt_pbf", [128, 512], BF16)
    pbf2 = pr.psum("rt_pbf2", [128, 512], BF16)
    psf = pr.psum("rt_psf", [128, 512], F32)
    psg = pr.psum("rt_psg", [128, 64], F32)
    dsti_all = pr.s["dsti_all"]
    sg_all = pr.s["sg_all"]
    VS = "vself"

    @block.vector
    def _(v):
        def vop(instr, k=1):
            n = plan.inc(VS, instr, k)
            v.wait_ge(plan.sem(VS), n)
            return n
        v.wait_ge(plan.sem("ss2"), nt)
        vop(v.memset(ztile[:], 0.0))
        n = vop(v.memset(zt64[:], 0.0))
        pr.marks["rt_zt"] = n
        v.wait_ge(plan.sem("rgp"), 2)
        v.wait_ge(plan.sem("rt_rko_p"), 1)
        vop(v.tensor_copy(rko[:], psf[0:128, 0:1]))
        plan.inc("rt_rko_done", v.nop())
        for i in range(nt):
            v.wait_ge(plan.sem("rxl%d" % (i % 2)), 16 * (i // 2 + 1))
            n = vop(v.tensor_scalar(xnb[:], xts[:, (i % 2) * HD:(i % 2 + 1) * HD],
                                    s["st_out"][:, 2 * i:2 * i + 1],
                                    s["st_out"][:, 2 * i + 1:2 * i + 2],
                                    op0=ALU.subtract, op1=ALU.mult))
            pr.marks["rt_ln_%d" % i] = n
            v.wait_ge(plan.sem("rt_lgev"), i + 1)
            vop(v.max(out=top8v[:], in_=lgt[:]))
            vop(v.max_index(top8i[:], top8v[:], lgt[:]))
            plan.inc(VS, v.tensor_copy(idx_all[:, 8 * i:8 * i + 8], top8i[:]))
            plan.inc(VS, v.tensor_copy(masked[:], lgt[:]))
            n = plan.inc(VS, v.tensor_scalar(scr[:, 3:4], top8v[:, 0:1], -1.0, None,
                                             op0=ALU.mult))
            v.wait_ge(plan.sem(VS), n)
            pr.marks["rt_nm_%d" % i] = n
            vop(v.match_replace(masked[:], in_to_replace=top8v[:], in_values=masked[:],
                                imm_value=-3.0e38))
            v.wait_ge(plan.sem("rt_exp"), i + 1)
            plan.inc(VS, v.reduce_sum(scr[:, 4:5], exp8[:], axis=mybir.AxisListType.X))
            n = plan.inc(VS, v.tensor_tensor(chos[:], lgt[:], masked[:],
                                             op=ALU.not_equal))
            v.wait_ge(plan.sem(VS), n)
            pr.marks["rt_ch_%d" % i] = n
            vop(v.reciprocal(scr[:, 5:6], scr[:, 4:5]))
            vop(v.tensor_scalar(wn_all[:, 8 * i:8 * i + 8], exp8[:], scr[:, 5:6],
                                None, op0=ALU.mult))
            v.wait_ge(plan.sem("rt_chT"), i + 1)
            vop(v.tensor_copy(chT[:, 128 * i:128 * (i + 1)], psf[0:64, 0:128]))
            plan.inc("rt_chTev", v.nop())
        pr.marks["rt_sga"] = plan.val(VS)
        vop(v.tensor_tensor_scan(gfull[:], zt64[:], chT[:], 0.0,
                                 op0=ALU.add, op1=ALU.add))
        n = vop(v.tensor_copy(cntc[:], gfull[:, TPC - 1:TPC]))
        pr.marks["rt_cnt"] = n
        v.wait_ge(plan.sem("rt_pc"), 1)
        vop(v.tensor_copy(pcol[:], psf[0:64, 300:301]))
        vop(v.tensor_scalar(gfull[:], gfull[:], pcol[:], None, op0=ALU.add))
        n = vop(v.tensor_tensor(gfull[:], gfull[:], chT[:], op=ALU.subtract))
        pr.marks["rt_gf"] = n
        for i in range(nt):
            v.wait_ge(plan.sem("rt_gtm"), i + 1)
            if i >= 2:
                v.wait_ge(plan.sem("rsc%s" % ("" if i % 2 == 0 else "1")),
                          16 * TOPK * (i // 2))
            sl = (i % 2) * 32
            iv = intl[:, sl:sl + 32].rearrange("p (k f) -> p f k", f=4)
            plan.inc(VS, v.tensor_copy(gtm[:], psf[0:128, 128:192]))
            for k in range(TOPK):
                plan.inc(VS, v.tensor_scalar(chos8[:, 64 * k:64 * (k + 1)], iota64[:],
                                             idx_all[:, 8 * i + k:8 * i + k + 1], None,
                                             op0=ALU.is_equal))
            plan.inc(VS, v.tensor_scalar(dstf[:], idx_all[:, 8 * i:8 * i + 8], 512.0,
                                         None, op0=ALU.mult))
            plan.inc(VS, v.tensor_scalar(scr[:, 0:1], iotap[:], rko[:],
                                         float(128 * i), op0=ALU.add, op1=ALU.add))
            plan.inc(VS, v.tensor_copy(iv[:, 1, :], wn_all[:, 8 * i:8 * i + 8]))
            plan.inc(VS, v.tensor_copy(iv[:, 2, :],
                     s["st_out"][:, 2 * i:2 * i + 1].to_broadcast([128, 8])))
            nA = plan.inc(VS, v.tensor_copy(iv[:, 3, :],
                          s["st_out"][:, 2 * i + 1:2 * i + 2].to_broadcast([128, 8])))
            v.wait_ge(plan.sem(VS), nA)
            plan.inc("rt_gtmev", v.nop())
            for k in range(TOPK):
                plan.inc(VS, v.tensor_tensor(msk8[:, 64 * k:64 * (k + 1)], gtm[:],
                                             chos8[:, 64 * k:64 * (k + 1)], op=ALU.mult))
            nB = plan.inc(VS, v.tensor_copy(iv[:, 0, :],
                                            scr[:, 0:1].to_broadcast([128, 8])))
            v.wait_ge(plan.sem(VS), nB)
            for k in range(TOPK):
                plan.inc(VS, v.reduce_sum(gsel[:, k:k + 1],
                                          msk8[:, 64 * k:64 * (k + 1)],
                                          axis=mybir.AxisListType.X))
            nC = plan.val(VS)
            v.wait_ge(plan.sem(VS), nC)
            plan.inc(VS, v.tensor_tensor(dstf[:], dstf[:], gsel[:], op=ALU.add))
            nD = plan.inc(VS, v.tensor_scalar(scr[:, 0:8], gsel[:], float(CAP),
                                              40000.0, op0=ALU.is_ge, op1=ALU.mult))
            v.wait_ge(plan.sem(VS), nD)
            nE = plan.inc(VS, v.tensor_tensor(dsti_all[:, 8 * i:8 * i + 8], dstf[:],
                                              scr[:, 0:8], op=ALU.add))
            v.wait_ge(plan.sem(VS), nE)
            pr.marks["rt_intl_%d" % i] = nE
        pr.marks["rt_v_fin"] = plan.val(VS)


    @block.scalar
    def _(sc):
        for i in range(nt):
            for h in range(2):
                sc.wait_ge(plan.sem("rt_tp"), 8 * i + 4 * (h + 1))
                if i >= 2:
                    sc.wait_ge(plan.sem("rfw%d" % (i % 2)), 128 * (i // 2))
                pb = pbf if h == 0 else pbf2
                plan.inc("rt_fmev", sc.copy(
                    fmst[:, (i % 2) * HD + 512 * h:(i % 2) * HD + 512 * (h + 1)],
                    pb[:, 0:512]))
            sc.wait_ge(plan.sem("rt_lg"), i + 1)
            plan.inc("rt_lgev", sc.copy(lgt[:], psg[:, 0:64]))
            sc.wait_ge(plan.sem("rt_sg"), i + 1)
            plan.inc("rt_sgev", sc.copy(sg_all[:, i:i + 1], psf[:, 256:257]))
            sc.wait_ge(plan.sem(VS), pr.marks["rt_nm_%d" % i])
            plan.inc("rt_exp", sc.activation(exp8[:], top8v[:], AF.Exp,
                                             bias=scr[:, 3:4]))
        sc.wait_ge(plan.sem(VS), pr.marks["rt_sga"])
        plan.inc("rt_sig", sc.activation(sg_all[:], sg_all[:], AF.Sigmoid))

    @block.tensor
    def _(te):
        te.wait_ge(plan.sem("cl"), plan.val("cl"))
        te.wait_ge(plan.sem("rgl"), 64)
        plan.inc("rt_rko_p", te.matmul(psf[0:128, 0:1], s["ones32r"][0:1, :],
                                       rof[0:1, 0:1], start=True, stop=True,
                                       skip_group_check=True))
        for i in range(nt):
            te.wait_ge(plan.sem(VS), pr.marks["rt_ln_%d" % i])
            for cch in range(8):
                h = cch // 4
                if cch % 4 == 0 and i >= 1:
                    te.wait_ge(plan.sem("rt_fmev"), 2 * (i - 1) + h + 1)
                pb = pbf if h == 0 else pbf2
                plan.inc("rt_tp", te.transpose(
                    pb[:, 128 * (cch % 4):128 * (cch % 4 + 1)],
                    xnb[:, 128 * cch:128 * (cch + 1)], s["ident"][:]))
            te.wait_ge(plan.sem("rt_fmev"), 2 * (i + 1))
            if i >= 1:
                te.wait_ge(plan.sem("rt_lgev"), i)
                te.wait_ge(plan.sem("rt_sgev"), i)
            for cch in range(8):
                lhs = fmst[:, (i % 2) * HD + 128 * cch:(i % 2) * HD + 128 * (cch + 1)]
                te.ldweights(lhs)
                _m1 = te.matmul(psg[:, 0:64], lhs, gwb[:, 64 * cch:64 * (cch + 1)],
                                start=(cch == 0), stop=(cch == 7),
                                skip_group_check=True)
                _m2 = te.matmul(psf[:, 256:257], lhs, sgwb[:, cch:cch + 1],
                                start=(cch == 0), stop=(cch == 7),
                                skip_group_check=True)
                if cch == 7:
                    plan.inc("rt_lg", _m1)
                    plan.inc("rt_sg", _m2)
            te.wait_ge(plan.sem(VS), pr.marks["rt_ch_%d" % i])
            if i == 0:
                te.wait_ge(plan.sem("rt_rko_done"), 1)
            if i >= 1:
                te.wait_ge(plan.sem("rt_chTev"), i)
            plan.inc("rt_chT", te.transpose(psf[0:64, 0:128], chos[:], s["ident32"][:]))
        te.wait_ge(plan.sem("rcl"), 16)
        plan.inc("rt_pc", te.matmul(psf[0:64, 300:301], cnt8[:], trk[:],
                                    start=True, stop=True, skip_group_check=True))
        te.wait_ge(plan.sem(VS), pr.marks["rt_gf"])
        for i in range(nt):
            if i >= 1:
                te.wait_ge(plan.sem("rt_gtmev"), i)
            plan.inc("rt_gtm", te.transpose(psf[0:128, 128:192],
                                            gfull[:, 128 * i:128 * (i + 1)],
                                            s["ident32"][0:64, 0:64]))

    @block.sync
    def _(sy):
        sy.wait_ge(plan.sem(VS), pr.marks["rt_zt"])
        plan.inc("rzw", sy.dma_start(
            out=pr.d["idwn"].rearrange("(a b) c -> a (b c)", a=128), in_=ztile[:]), 16)
        for i in range(nt):
            if i >= 2:
                sy.wait_ge(plan.sem(VS), pr.marks["rt_ln_%d" % (i - 2)])
            dma_in(plan, sy, xts[:, (i % 2) * HD:(i % 2 + 1) * HD],
                   pr.d["x_ts"][128 * i:128 * (i + 1), :], "rxl%d" % (i % 2))
            sy.wait_ge(plan.sem("rt_fmev"), 2 * (i + 1))
            fv = pr.d["xnfm"].rearrange("(c p) t -> c p t", p=128)
            for cch in range(8):
                plan.inc("rfw%d" % (i % 2), sy.dma_start(
                    out=fv[cch, :, 128 * i:128 * (i + 1)],
                    in_=fmst[:, (i % 2) * HD + 128 * cch:
                             (i % 2) * HD + 128 * (cch + 1)]), 16)
        sy.wait_ge(plan.sem(VS), pr.marks["rt_cnt"])
        dma_in(plan, sy, pr.d["cnt_agin"], cntc[:], "rcw")
        sy.wait_ge(plan.sem("cc_cnt"), 1)
        dma_in(plan, sy, cnt8[:],
               pr.d["cnt_agout"].rearrange("(r e) a -> r (e a)", e=64), "rcl")

    @block.gpsimd
    def _(g):
        g.wait_ge(plan.sem("ss2"), nt)
        plan.inc("rgp", g.iota(iota64[:], pattern=[[1, 64]], base=0,
                               channel_multiplier=0,
                               allow_small_or_imprecise_dtypes=True))
        plan.inc("rgp", g.iota(iotap[:], pattern=[[1, 1]], base=0,
                               channel_multiplier=1,
                               allow_small_or_imprecise_dtypes=True))
        dma_in(plan, g, gwb[:].rearrange("p (c e) -> p c e", e=64),
               pr.d["gate_w"].rearrange("(c p) e -> p c e", p=128), "rgl")
        plan.inc("rgl", g.dma_start(
            out=sgwb[:].rearrange("p (c e) -> p c e", e=1),
            in_=pr.d["shared_gate_w"].rearrange("(c p) e -> p c e", p=128),
            allow_slow_non_contiguous=True), 16)
        dma_in(plan, g, rof[:], pr.d["rank_off"], "rgl")
        dma_in(plan, g, trk[:], pr.d["tri_rk"], "rgl")
        g.wait_ge(plan.sem("rcw"), 16)
        plan.inc("cc_cnt", g.collective_compute(
            "AllGather", ALU.bypass, replica_groups=[list(range(NCORES))],
            ins=[pr.d["cnt_agin"].opt()], outs=[pr.d["cnt_agout"].opt()]))
        g.wait_ge(plan.sem("rzw"), 16)
        for i in range(nt):
            g.wait_ge(plan.sem(VS), pr.marks["rt_intl_%d" % i])
            for k in range(TOPK):
                plan.inc("rsc%s" % ("" if i % 2 == 0 else "1"), g.indirect_dma_start(
                    out=pr.d["idwn"],
                    out_offset=bass.IndirectOffsetOnAxis(
                        ap=dsti_all[:, 8 * i + k:8 * i + k + 1], axis=0),
                    in_=intl[:, (i % 2) * 32 + 4 * k:(i % 2) * 32 + 4 * k + 4],
                    in_offset=None,
                    bounds_check=pr.bnd32767, oob_is_err=False), 16)
        g.wait_ge(plan.sem("rsc"), 16 * TOPK * (nt // 2))
        g.wait_ge(plan.sem("rsc1"), 16 * TOPK * (nt // 2))
        plan.inc("cc_rs", g.collective_compute(
            "ReduceScatter", ALU.add, replica_groups=[list(range(NCORES))],
            ins=[pr.d["idwn"].opt()], outs=[pr.d["idwn_rs"].opt()]))

def emit_moe(pr, block):
    plan = pr.plan
    s = pr.s

    it4 = pr.sbuf("mo_it", [128, 16], F32)
    nmr = pr.sbuf("mo_nmr", [128, 8], F32)
    idxu = pr.sbuf("mo_idxu", [128, 4], U32)
    xrow = pr.sbuf("mo_xrow", [128, 1024], BF16)
    xnr = pr.sbuf("mo_xnr", [128, 1024], BF16)
    xnT = pr.sbuf("mo_xnT", [128, 8 * 512], BF16)
    wegu = pr.sbuf("mo_wegu", [128, 8 * 1024], BF16)
    wedw = pr.sbuf("mo_wedw", [128, 4 * 1024], BF16)
    sil = pr.sbuf("mo_sil", [128, 4 * 512], BF16)
    actT = pr.sbuf("mo_act", [128, 4 * 512], BF16)
    outb = pr.sbuf("mo_out", [128, 1024], BF16)
    pbf = pr.psum("mo_pbf", [128, 128], BF16)
    pbf2 = pr.psum("mo_pbf2", [128, 128], BF16)
    psg1 = pr.psum("mo_ps1", [128, 512], F32)
    psg2 = pr.psum("mo_ps2", [128, 512], F32)
    psd = pr.psum("mo_psd", [128, 1024], F32)
    VS = "vself"

    @block.vector
    def _(v):
        def vop(instr, k=1):
            n = plan.inc(VS, instr, k)
            v.wait_ge(plan.sem(VS), n)
            return n
        for e in range(EPC):
            for q in range(4):
                v.wait_ge(plan.sem("mil%d" % (q % 2)), 16 * ((4 * e + q) // 2 + 1))
                vop(v.tensor_scalar(nmr[:, 0:1], it4[:, 4 * q:4 * q + 1], 0.0, None,
                                    op0=ALU.is_equal))
                vop(v.tensor_scalar(nmr[:, 1:2], it4[:, 4 * q:4 * q + 1], -1.0, None,
                                    op0=ALU.add))
                vop(v.scalar_tensor_tensor(nmr[:, 1:2], nmr[:, 0:1], 50000.0,
                                           nmr[:, 1:2], op0=ALU.mult, op1=ALU.add))
                n = vop(v.tensor_copy(idxu[:, q:q + 1], nmr[:, 1:2]))
                n = vop(v.memset(xrow[:], 0.0))
                pr.marks["mo_gw_%d_%d" % (e, q)] = n
                v.wait_ge(plan.sem("mgl"), 16 * (4 * e + q + 1))
                vop(v.tensor_tensor(nmr[:, 2:3], it4[:, 4 * q + 2:4 * q + 3],
                                    it4[:, 4 * q + 3:4 * q + 4], op=ALU.mult))
                vop(v.tensor_scalar(nmr[:, 3:4], nmr[:, 2:3], -1.0, None, op0=ALU.mult))
                n = vop(v.tensor_scalar(xnr[:], xrow[:],
                                        it4[:, 4 * q + 3:4 * q + 4],
                                        nmr[:, 3:4], op0=ALU.mult, op1=ALU.add))
                pr.marks["mo_ln_%d_%d" % (e, q)] = n
                for cch in range(8):
                    v.wait_ge(plan.sem("mo_tp"), (4 * e + q) * 8 + cch + 1)
                    pb = pbf if cch % 2 == 0 else pbf2
                    n = vop(v.tensor_copy(
                        xnT[:, 512 * cch + 128 * q:512 * cch + 128 * (q + 1)],
                        pb[:, 0:128]))
                    plan.inc("mo_tpev", v.nop())
                pr.marks["mo_xnT_%d_%d" % (e, q)] = plan.val(VS)
            for ip in range(4):
                v.wait_ge(plan.sem("mo_ev"), 8 * e + ip + 5)
                n = vop(v.scalar_tensor_tensor(
                    actT[:, 512 * ip:512 * (ip + 1)],
                    actT[:, 512 * ip:512 * (ip + 1)], 1.0,
                    sil[:, 512 * ip:512 * (ip + 1)], op0=ALU.mult, op1=ALU.mult))
                pr.marks["mo_act_%d_%d" % (e, ip)] = n
            for q in range(4):
                v.wait_ge(plan.sem("mo_dn"), 4 * e + q + 1)
                v.wait_ge(plan.sem("mow"), 16 * (4 * e + q))
                n = vop(v.tensor_scalar(outb[:], psd[:, 0:1024],
                                        it4[:, 4 * q + 1:4 * q + 2], None, op0=ALU.mult))
                plan.inc("mo_dnev", v.nop())
                pr.marks["mo_out_%d_%d" % (e, q)] = n
            for q in range(4):
                pr.marks["mo_xn_%d_%d" % (e + 1, q)] = plan.val(VS)
                pr.marks["mo_idx_%d_%d" % (e + 1, q)] = plan.val(VS)
        pr.marks["mo_v_fin"] = plan.val(VS)

    @block.tensor
    def _(te):
        nmm = 0
        for e in range(EPC):
            for q in range(4):
                te.wait_ge(plan.sem(VS), pr.marks["mo_ln_%d_%d" % (e, q)])
                for cch in range(8):
                    if (4 * e + q) * 8 + cch >= 2:
                        te.wait_ge(plan.sem("mo_tpev"), (4 * e + q) * 8 + cch - 1)
                    pb = pbf if cch % 2 == 0 else pbf2
                    plan.inc("mo_tp", te.transpose(
                        pb[:, 0:128], xnr[:, 128 * cch:128 * (cch + 1)], s["ident"][:]))
            te.wait_ge(plan.sem("mwl%d" % (e % 2)), 16 * 12 * (e // 2 + 1))
            te.wait_ge(plan.sem(VS), pr.marks["mo_xnT_%d_3" % e])
            for fc in range(8):
                psx = psg1 if fc % 2 == 0 else psg2
                if fc >= 2:
                    te.wait_ge(plan.sem("mo_ev"), 8 * e + fc - 1)
                for cc in range(8):
                    lhs = wegu[:, 1024 * cc + 128 * fc:1024 * cc + 128 * (fc + 1)]
                    te.ldweights(lhs)
                    plan.inc("mo_pe", te.matmul(psx[:, 0:512], lhs, xnT[:, 512 * cc:512 * (cc + 1)],
                                                start=(cc == 0), stop=(cc == 7),
                                                skip_group_check=True))
            for q in range(4):
                if q >= 1:
                    te.wait_ge(plan.sem("mo_dnev"), 4 * e + q)
                elif e >= 1:
                    te.wait_ge(plan.sem("mo_dnev"), 4 * e)
                for ic in range(4):
                    te.wait_ge(plan.sem(VS), pr.marks["mo_act_%d_%d" % (e, ic)])
                    lhs = actT[:, 512 * ic + 128 * q:512 * ic + 128 * (q + 1)]
                    te.ldweights(lhs)
                    for hh in range(2):
                        _mm = te.matmul(psd[:, 512 * hh:512 * (hh + 1)], lhs,
                                        wedw[:, 1024 * ic + 512 * hh:1024 * ic + 512 * (hh + 1)],
                                        start=(ic == 0), stop=(ic == 3),
                                        skip_group_check=True)
                    if ic == 3:
                        plan.inc("mo_dn", _mm)
        pr.marks["mo_t_fin"] = plan.val("mo_pe")

    @block.scalar
    def _(sc):
        for e in range(EPC):
            for fc in range(8):
                sc.wait_ge(plan.sem("mo_pe"), 64 * e + 8 * (fc + 1))
                psx = psg1 if fc % 2 == 0 else psg2
                if fc < 4:
                    if e >= 1:
                        sc.wait_ge(plan.sem(VS), pr.marks["mo_act_%d_%d" % (e - 1, fc)])
                    plan.inc("mo_ev", sc.activation(sil[:, 512 * fc:512 * (fc + 1)],
                                                    psx[:, 0:512], AF.Silu))
                else:
                    ip = fc - 4
                    if e >= 1:
                        sc.wait_ge(plan.sem("mo_dn"), 4 * e)
                    plan.inc("mo_ev", sc.copy(actT[:, 512 * ip:512 * (ip + 1)],
                                              psx[:, 0:512]))
            for q in range(4):
                sc.wait_ge(plan.sem(VS), pr.marks["mo_out_%d_%d" % (e, q)])
                plan.inc("mow", sc.dma_start(
                    out=pr.d["moe_out"][e * CAP + q * 128:e * CAP + (q + 1) * 128, :],
                    in_=outb[:]), 16)

    @block.gpsimd
    def _(g):
        g.wait_ge(plan.sem("cc_rs"), 1)
        for e in range(EPC):
            if e > 0:
                g.wait_ge(plan.sem("mo_pe"), 64 * e)
                g.wait_ge(plan.sem("mo_dn"), 4 * e)
            for cc in range(8):
                dma_in(plan, g, wegu[:, 1024 * cc:1024 * (cc + 1)],
                       pr.d["egu_c"][e].rearrange("(c p) f -> c p f", p=128)[cc],
                       "mwl%d" % (e % 2))
            for ic in range(4):
                dma_in(plan, g, wedw[:, 1024 * ic:1024 * (ic + 1)],
                       pr.d["edw_c"][e].rearrange("(c p) f -> c p f", p=128)[ic],
                       "mwl%d" % (e % 2))
            for q in range(4):
                st = e * CAP + q * 128
                g.wait_ge(plan.sem(VS), pr.marks.get("mo_xn_%d_%d" % (e, q), 0))
                dma_in(plan, g, it4[:, 4 * q:4 * (q + 1)],
                       pr.d["idwn_rs"][st:st + 128, :], "mil%d" % (q % 2))
                g.wait_ge(plan.sem(VS), pr.marks.get("mo_idx_%d_%d" % (e, q), 0))
                g.wait_ge(plan.sem(VS), pr.marks["mo_gw_%d_%d" % (e, q)])
                plan.inc("mgl", g.indirect_dma_start(
                    out=xrow[:], out_offset=None,
                    in_=pr.d["x_full"],
                    in_offset=bass.IndirectOffsetOnAxis(ap=idxu[:, q:q + 1], axis=0),
                    bounds_check=pr.bnd32767, oob_is_err=False), 16)

    @block.gpsimd
    def _(g):
        g.wait_ge(plan.sem("mow"), 16 * 4 * EPC)
        plan.inc("cc_moe", g.collective_compute(
            "AllGather", ALU.bypass, replica_groups=[list(range(NCORES))],
            ins=[pr.d["moe_out"].opt()], outs=[pr.d["moe_all"].opt()]))


def emit_final(pr, block):
    plan = pr.plan
    s = pr.s
    NTC = 4
    nt = TPC // 128

    xnfm_b = pr.sbuf("fi_xnfm", [128, 8 * 1024], BF16)
    wsd = pr.sbuf("fi_wsd", [128, 16 * 1024], BF16)
    wsg_g = pr.sbuf("fi_wg", [128, 2 * 1024], BF16)
    wsg_u = pr.sbuf("fi_wu", [128, 2 * 1024], BF16)
    sil = pr.sbuf("fi_sil", [128, 1024], BF16)
    actT = pr.sbuf("fi_act", [128, 16 * 1024], BF16)
    gt8 = pr.sbuf("fi_gt", [128, 2 * 8 * 1024], BF16)
    ymoe = pr.sbuf("fi_ym", [128, 1024], F32)
    ytile = pr.sbuf("fi_y", [128, 2 * 1024], F32)
    xres = pr.sbuf("fi_xr", [128, 2 * 1024], F32)
    ssmt = pr.sbuf("fi_ssm", [128, 2 * 1024], BF16)
    sstg = pr.sbuf("fi_sstg", [128, 4096], BF16)
    psga = pr.psum("fi_pg1", [128, 1024], F32)
    psgb = pr.psum("fi_pg2", [128, 1024], F32)
    psda = pr.psum("fi_pd1", [128, 1024], F32)
    psdb = pr.psum("fi_pd2", [128, 1024], F32)
    VS = "vself"
    SM = {"fxl0": "rxl0", "fxl1": "rxl1", "fsl0": "rfw0", "fsl1": "rfw1",
          "fi_xc": "rcw", "fowp0": "mow", "fowp1": "cmw1",
          "fwg0": "mwl0", "fwg1": "mwl1",
          "fi_gb0": "mil0", "fi_gb1": "mil1", "fi_wl2": "rgl",
          "fi_ev": "mo_ev", "fi_gm": "mo_pe", "fi_up": "mo_tp",
          "fi_dnm": "mo_dn", "fi_actev": "mo_tpev", "fi_dnev": "mo_dnev",
          "fwg0p0": "mwl0", "fwg0p1": "mwl1", "fwg1p0": "mgl", "fwg1p1": "rsc"}
    BASE = {a: plan.val(r) for a, r in SM.items()}

    def fsem(alias):
        return plan.sem(SM[alias])

    def fbase(alias):
        return BASE[alias]

    def finc(alias, instr, k=16):
        return plan.inc(SM[alias], instr, k)

    def fence_eng(eng):
        for nm, cntv in pr.cv_fence:
            eng.wait_ge(plan.sem(nm), cntv)

    @block.vector
    def _(v):
        def vop(instr, k=1):
            n = plan.inc(VS, instr, k)
            v.wait_ge(plan.sem(VS), n)
            return n
        fence_eng(v)
        for tc in range(NTC):
            for ip in range(16):
                v.wait_ge(fsem("fi_ev"), fbase("fi_ev") + 16 * tc + ip + 1)
                v.wait_ge(fsem("fi_up"), fbase("fi_up") + 16 * tc + ip + 1)
                if tc >= 1:
                    v.wait_ge(fsem("fi_dnm"), fbase("fi_dnm") + 8 * tc)
                n = vop(v.scalar_tensor_tensor(
                    actT[:, 1024 * ip:1024 * (ip + 1)], psgb[:, 0:1024], 1.0,
                    sil[:], op0=ALU.mult, op1=ALU.mult))
                finc("fi_actev", v.nop(), 1)
                pr.marks["fi_act_%d_%d" % (tc, ip)] = n
            for tt in range(8):
                i = 8 * tc + tt
                ip2 = i % 2
                v.wait_ge(fsem("fi_gb%d" % ip2), fbase("fi_gb%d" % ip2) + 16 * 8 * (i // 2 + 1))
                base = 8 * ip2 * 1024
                for k in range(4):
                    vop(v.tensor_tensor(gt8[:, base + 2 * k * 1024:base + (2 * k + 1) * 1024],
                                        gt8[:, base + 2 * k * 1024:base + (2 * k + 1) * 1024],
                                        gt8[:, base + (2 * k + 1) * 1024:base + (2 * k + 2) * 1024],
                                        op=ALU.add))
                vop(v.tensor_tensor(gt8[:, base:base + 1024], gt8[:, base:base + 1024],
                                    gt8[:, base + 2048:base + 3072], op=ALU.add))
                vop(v.tensor_tensor(gt8[:, base + 4096:base + 5120],
                                    gt8[:, base + 4096:base + 5120],
                                    gt8[:, base + 6144:base + 7168], op=ALU.add))
                n = vop(v.tensor_tensor(ymoe[:], gt8[:, base:base + 1024],
                                        gt8[:, base + 4096:base + 5120], op=ALU.add))
                pr.marks["fi_tree_%d" % i] = n
                if i >= 2:
                    pw = "fowp%d" % (i % 2)
                    v.wait_ge(fsem(pw), fbase(pw) + 16 * (i // 2))
                v.wait_ge(fsem("fi_dnm"), fbase("fi_dnm") + i + 1)
                v.wait_ge(plan.sem("rt_sig"), 1)
                psd = psda if tt % 2 == 0 else psdb
                vop(v.scalar_tensor_tensor(ytile[:, ip2 * 1024:(ip2 + 1) * 1024],
                                           psd[:, 0:1024], s["sg_all"][:, i:i + 1],
                                           ymoe[:], op0=ALU.mult, op1=ALU.add))
                finc("fi_dnev", v.nop(), 1)
                v.wait_ge(fsem("fxl%d" % ip2), fbase("fxl%d" % ip2) + 16 * (i // 2 + 1))
                vop(v.tensor_tensor(ytile[:, ip2 * 1024:(ip2 + 1) * 1024],
                                    ytile[:, ip2 * 1024:(ip2 + 1) * 1024],
                                    xres[:, ip2 * 1024:(ip2 + 1) * 1024], op=ALU.add))
                v.wait_ge(fsem("fsl%d" % ip2), fbase("fsl%d" % ip2) + 16 * (i // 2 + 1))
                n = vop(v.tensor_tensor(ytile[:, ip2 * 1024:(ip2 + 1) * 1024],
                                        ytile[:, ip2 * 1024:(ip2 + 1) * 1024],
                                        ssmt[:, ip2 * 1024:(ip2 + 1) * 1024], op=ALU.add))
                pr.marks["fi_y_%d" % i] = n

    @block.tensor
    def _(te):
        fence_eng(te)
        te.wait_ge(fsem("fi_wl2"), fbase("fi_wl2") + 16 * 16)
        for tc in range(NTC):
            te.wait_ge(fsem("fi_xc"), fbase("fi_xc") + 16 * (tc + 1))
            for ip in range(16):
                te.wait_ge(fsem("fwg0p%d" % (ip % 2)),
                           fbase("fwg0p%d" % (ip % 2)) + 16 * ((16 * tc + ip) // 2 + 1))
                if 16 * tc + ip >= 1:
                    te.wait_ge(fsem("fi_ev"), fbase("fi_ev") + 16 * tc + ip)
                for cc in range(8):
                    lhs = wsg_g[:, ((ip % 2) * 8 + cc) * 128:((ip % 2) * 8 + cc + 1) * 128]
                    te.ldweights(lhs)
                    for hh in range(2):
                        _mm = te.matmul(psga[:, 512 * hh:512 * (hh + 1)], lhs,
                                        xnfm_b[:, cc * 1024 + 512 * hh:cc * 1024 + 512 * (hh + 1)],
                                        start=(cc == 0), stop=(cc == 7),
                                        skip_group_check=True)
                    if cc == 7:
                        finc("fi_gm", _mm, 1)
                te.wait_ge(fsem("fwg1p%d" % (ip % 2)),
                           fbase("fwg1p%d" % (ip % 2)) + 16 * ((16 * tc + ip) // 2 + 1))
                if 16 * tc + ip >= 1:
                    te.wait_ge(fsem("fi_actev"), fbase("fi_actev") + 16 * tc + ip)
                for cc in range(8):
                    lhs = wsg_u[:, ((ip % 2) * 8 + cc) * 128:((ip % 2) * 8 + cc + 1) * 128]
                    te.ldweights(lhs)
                    for hh in range(2):
                        _mm = te.matmul(psgb[:, 512 * hh:512 * (hh + 1)], lhs,
                                        xnfm_b[:, cc * 1024 + 512 * hh:cc * 1024 + 512 * (hh + 1)],
                                        start=(cc == 0), stop=(cc == 7),
                                        skip_group_check=True)
                    if cc == 7:
                        finc("fi_up", _mm, 1)
            for tt in range(8):
                i = 8 * tc + tt
                psd = psda if tt % 2 == 0 else psdb
                if i >= 2:
                    te.wait_ge(fsem("fi_dnev"), fbase("fi_dnev") + i - 1)
                for ip in range(16):
                    te.wait_ge(plan.sem(VS), pr.marks["fi_act_%d_%d" % (tc, ip)])
                    lhs = actT[:, 1024 * ip + 128 * tt:1024 * ip + 128 * (tt + 1)]
                    te.ldweights(lhs)
                    for hh in range(2):
                        _mm = te.matmul(psd[:, 512 * hh:512 * (hh + 1)], lhs,
                                        wsd[:, 1024 * ip + 512 * hh:1024 * ip + 512 * (hh + 1)],
                                        start=(ip == 0), stop=(ip == 15),
                                        skip_group_check=True)
                    if ip == 15:
                        finc("fi_dnm", _mm, 1)

    @block.scalar
    def _(sc):
        fence_eng(sc)
        for tc in range(NTC):
            for ip in range(16):
                sc.wait_ge(fsem("fi_gm"), fbase("fi_gm") + 16 * tc + ip + 1)
                if 16 * tc + ip >= 1:
                    sc.wait_ge(plan.sem(VS), pr.marks["fi_act_%d_%d" %
                               ((16 * tc + ip - 1) // 16, (16 * tc + ip - 1) % 16)])
                finc("fi_ev", sc.activation(sil[:], psga[:, 0:1024], AF.Silu), 1)
            for tt in range(8):
                i = 8 * tc + tt
                sc.wait_ge(plan.sem(VS), pr.marks["fi_y_%d" % i])
                finc("fowp%d" % (i % 2), sc.dma_start(
                    out=pr.d["y_out"][128 * i:128 * (i + 1), :],
                    in_=ytile[:, (i % 2) * 1024:(i % 2 + 1) * 1024]))
        sc.wait_ge(fsem("fowp0"), fbase("fowp0") + 16 * (nt // 2))
        sc.wait_ge(fsem("fowp1"), fbase("fowp1") + 16 * (nt // 2))

    @block.sync
    def _(sy):
        fence_eng(sy)
        dma_in(plan, sy,
               xnfm_b[:].rearrange("p (c t) -> p c t", t=1024),
               pr.d["xnfm"].rearrange("(c p) t -> p c t", p=128)
               [:, :, 0:1024], SM["fi_xc"])
        for i0 in range(2):
            dma_in(plan, sy, xres[:, i0 * 1024:(i0 + 1) * 1024],
                   pr.d["x_ts"][128 * i0:128 * (i0 + 1), :], SM["fxl%d" % i0])
        with sy.register("srk") as srk:
            sy.load(srk, pr.s["rk_i_sb"][0:1, 0:1])
            off = sy.snap(srk)
            sy.wait_ge(plan.sem("cc_ssm"), 1)
            bb = plan.val("kgw")
            for cc in range(8):
                if cc >= 1:
                    sy.wait_ge(plan.sem("kgw"), bb + 16 * (2 * cc))
                plan.inc("kgw", sy.dma_start(
                    out=sstg[:],
                    in_=pr.d["ssm_agout"][128 * cc:128 * (cc + 1), bass.ds(off, TPC)]), 16)
                sy.wait_ge(plan.sem("kgw"), bb + 16 * (2 * cc + 1))
                plan.inc("kgw", sy.dma_start(
                    out=pr.d["ssm_my"][128 * cc:128 * (cc + 1), :],
                    in_=sstg[:]), 16)
            sy.wait_ge(plan.sem("kgw"), bb + 16 * 16)
            for tc in range(NTC):
                if tc >= 1:
                    sy.wait_ge(fsem("fi_up"), fbase("fi_up") + 16 * tc)
                    dma_in(plan, sy,
                           xnfm_b[:].rearrange("p (c t) -> p c t", t=1024),
                           pr.d["xnfm"].rearrange("(c p) t -> p c t", p=128)
                           [:, :, 1024 * tc:1024 * (tc + 1)], SM["fi_xc"])
                for tt in range(8):
                    i = 8 * tc + tt
                    ip2 = i % 2
                    if i >= 2:
                        sy.wait_ge(plan.sem(VS), pr.marks["fi_y_%d" % (i - 2)])
                        dma_in(plan, sy, xres[:, ip2 * 1024:(ip2 + 1) * 1024],
                               pr.d["x_ts"][128 * i:128 * (i + 1), :], SM["fxl%d" % ip2])
                    finc("fsl%d" % ip2, sy.dma_start(
                        out=ssmt[:, ip2 * 1024:(ip2 + 1) * 1024],
                        in_=pr.d["ssm_my"][:, 128 * i:128 * (i + 1)],
                        transpose=True))

    @block.gpsimd
    def _(g):
        fence_eng(g)
        for ic in range(16):
            dma_in(plan, g, wsd[:, 1024 * ic:1024 * (ic + 1)],
                   pr.d["sdw"].rearrange("(c p) f -> c p f", p=128)[ic], SM["fi_wl2"])
        for tc in range(NTC):
            for ip in range(16):
                if 16 * tc + ip >= 2:
                    g.wait_ge(fsem("fi_gm"), fbase("fi_gm") + 16 * tc + ip - 1)
                    g.wait_ge(fsem("fi_up"), fbase("fi_up") + 16 * tc + ip - 1)
                dma_in(plan, g,
                       wsg_g[:, (ip % 2) * 1024:(ip % 2 + 1) * 1024].rearrange(
                           "p (c f) -> p c f", f=128),
                       pr.d["sguw"][:, 128 * ip:128 * (ip + 1)].rearrange(
                           "(c p) f -> p c f", p=128), SM["fwg0p%d" % (ip % 2)])
                dma_in(plan, g,
                       wsg_u[:, (ip % 2) * 1024:(ip % 2 + 1) * 1024].rearrange(
                           "p (c f) -> p c f", f=128),
                       pr.d["sguw"][:, ISH + 128 * ip:ISH + 128 * (ip + 1)].rearrange(
                           "(c p) f -> p c f", p=128), SM["fwg1p%d" % (ip % 2)])
            for tt in range(8):
                i = 8 * tc + tt
                ip2 = i % 2
                if i == 0:
                    g.wait_ge(plan.sem("cc_moe"), 1)
                if i >= 2:
                    g.wait_ge(plan.sem(VS), pr.marks["fi_tree_%d" % (i - 2)])
                for k in range(8):
                    plan.inc("gms", g.memset(
                        gt8[:, (8 * ip2 + k) * 1024:(8 * ip2 + k + 1) * 1024], 0.0))
                g.wait_ge(plan.sem("gms"), plan.val("gms"))
                for k in range(8):
                    finc("fi_gb%d" % ip2, g.indirect_dma_start(
                        out=gt8[:, (8 * ip2 + k) * 1024:(8 * ip2 + k + 1) * 1024],
                        out_offset=None,
                        in_=pr.d["moe_all"],
                        in_offset=bass.IndirectOffsetOnAxis(
                            ap=pr.s["dsti_all"][:, 8 * i + k:8 * i + k + 1], axis=0),
                        bounds_check=pr.bnd32767, oob_is_err=False))


def _phase_wait(block, plan, pr, sems_counts, vs_marks):
    for engname in ["vector", "scalar", "tensor", "sync", "gpsimd"]:
        dec = getattr(block, engname)

        @dec
        def _(eng):
            for nm, cntv in sems_counts:
                eng.wait_ge(plan.sem(nm), cntv)
            for nm, mk in vs_marks:
                eng.wait_ge(plan.sem(nm), mk)


def build_full():
    import contextlib
    nc = bass.Bass("TRN2", target_bir_lowering=False)
    stack = contextlib.ExitStack()
    pr = Prog(nc, stack, 0)
    declare_conv_io(pr)
    declare_ffn_io(pr)
    pr.dram("y_out", [TPC, HD], F32, kind="ExternalOutput")

    with nc.Block() as block:
        plan = pr.plan
        pr.sbuf("dsti_all", [128, 8 * (TPC // 128)], U32)
        pr.sbuf("sg_all", [128, TPC // 128], F32)
        pr.sbuf("rk_i_sb", [1, 1], I32)

        @block.gpsimd
        def _(g):
            conv_const_loads(pr, g)
            dma_in(plan, g, pr.s["rk_i_sb"][:], pr.d["rank_i"], "cl")

        pr.push()
        emit_stats(pr, block)
        st_fence = [("sv", plan.val("sv")), ("ss2", plan.val("ss2")),
                    ("sve", plan.val("sve")), ("sgi", 16)]
        pr.pop()

        pr.push()
        _phase_wait(block, plan, pr, st_fence, [])
        emit_routing(pr, block)
        rt_fence = [("rvs", plan.val("rvs")), ("rt_exp", plan.val("rt_exp")),
                    ("rt_sig", 1), ("rsc", plan.val("rsc")),
                    ("rsc1", plan.val("rsc1")),
                    ("rt_chT", plan.val("rt_chT")), ("rt_gtm", plan.val("rt_gtm")),
                    ("rfw0", plan.val("rfw0")), ("rfw1", plan.val("rfw1")),
                    ("rzw", 16), ("rcw", 16)]
        pr.pop()

        pr.push()
        _phase_wait(block, plan, pr, rt_fence, [])
        pr.sbuf("kb_gidx", [128, 16], U32)
        pr.push()
        emit_kg(pr, block)
        pr.pop()
        emit_basis(pr, block)
        emit_cmix(pr, block)
        cm_fence = [("cmp", plan.val("cmp")), ("cme", plan.val("cme")),
                    ("cmw0", plan.val("cmw0")), ("cmw1", plan.val("cmw1")),
                    ("cmw2", plan.val("cmw2")), ("cmw3", plan.val("cmw3")),
                    ("kbv", plan.val("kbv")), ("kbv2", plan.val("kbv2")),
                    ("kbe", plan.val("kbe")), ("kbe2", plan.val("kbe2")),
                    ("cmv", 1)]
        pr.pop()

        pr.push()
        _phase_wait(block, plan, pr, cm_fence, [])
        emit_moe(pr, block)
        mo_fence = [("mvs", plan.val("mvs")), ("mo_pe", plan.val("mo_pe")),
                    ("mo_dn", plan.val("mo_dn")), ("mo_ev", plan.val("mo_ev")),
                    ("mow", plan.val("mow")), ("mgl", plan.val("mgl")),
                    ("mo_tp", plan.val("mo_tp"))]
        pr.pop()

        pr.push()
        _phase_wait(block, plan, pr, mo_fence, [])
        c0 = dict(plan.cnt)
        emit_conv(pr, block)
        pr.cv_fence = [(nm, v) for nm, v in plan.cnt.items()
                       if v != c0.get(nm, 0) and nm != "cc_ssm"]
        pr.pop()

        pr.push()
        emit_final(pr, block)
        pr.pop()

    stack.close()
    return nc


_NC_CACHE = None


def kernel(**inputs):
    global _NC_CACHE
    x = np.asarray(inputs["L0_attn_r"])
    prm = dict(B_re=np.asarray(inputs["L0_attn_B_re"]),
               B_im=np.asarray(inputs["L0_attn_B_im"]),
               C_re=np.asarray(inputs["L0_attn_C_re"]),
               C_im=np.asarray(inputs["L0_attn_C_im"]),
               P_re=np.asarray(inputs["L0_attn_P_re"]),
               P_im=np.asarray(inputs["L0_attn_P_im"]),
               Q_re=np.asarray(inputs["L0_attn_Q_re"]),
               Q_im=np.asarray(inputs["L0_attn_Q_im"]),
               diag_re=np.asarray(inputs["L0_attn_diag_re"]),
               diag_im=np.asarray(inputs["L0_attn_diag_im"]),
               step=np.asarray(inputs["L0_attn_step"]),
               D=np.asarray(inputs["L0_attn_D"]))
    if _NC_CACHE is None:
        _NC_CACHE = build_full()
    nc = _NC_CACHE
    in_maps = []
    for c in range(NCORES):
        m = _shard_ssm_inputs(c, x, prm)
        m["gate_w"] = np.asarray(inputs["L0_ffn_gate_w"])
        m["shared_gate_w"] = np.asarray(inputs["L0_ffn_shared_gate_w"])
        m["sguw"] = np.asarray(inputs["L0_ffn_shared_gate_up_w"])
        m["sdw"] = np.asarray(inputs["L0_ffn_shared_down_w"])
        m["egu_c"] = np.ascontiguousarray(
            np.asarray(inputs["L0_ffn_expert_gate_up_w"])[EPC * c:EPC * (c + 1)])
        m["edw_c"] = np.ascontiguousarray(
            np.asarray(inputs["L0_ffn_expert_down_w"])[EPC * c:EPC * (c + 1)])
        m["rank_off"] = np.array([[4096.0 * c + 1.0]], np.float32)
        m["tri_rk"] = (np.arange(8) < c).astype(np.float32).reshape(8, 1)
        m["rank_i"] = np.array([[4096 * c]], np.int32)
        in_maps.append(m)
    res = run_bass_kernel_spmd(nc, in_maps, core_ids=list(range(NCORES)))
    out = np.concatenate([res.results[c]["y_out"] for c in range(NCORES)], axis=0)
    return out.reshape(B, L, HD)


# version 08
# speedup vs baseline: 1.3189x; 1.0040x over previous
import numpy as np
import ml_dtypes
import concourse.bass as bass
import concourse.mybir as mybir
from concourse.bass_utils import run_bass_kernel_spmd

F32 = mybir.dt.float32
BF16 = mybir.dt.bfloat16
U32 = mybir.dt.uint32
I32 = mybir.dt.int32
AF = mybir.ActivationFunctionType
ALU = mybir.AluOpType
BF = ml_dtypes.bfloat16

NCORES = 8
B = 2
L = 16384
NF = 2 * L
HD = 1024
T = B * L
TPC = T // NCORES
CPC = HD // NCORES
NSEQ = B * CPC
NMODE = 64
E = 64
EPC = E // NCORES
TOPK = 8
CAP = 512
ISH = 2048
IEX = 512
EPS = 1e-5
K1H = 129


def _host_consts():
    c = {}
    l1 = np.arange(128)
    k1 = np.arange(K1H)
    l2 = np.arange(128)
    k2 = np.arange(128)

    F1 = np.exp(-2j * np.pi * np.outer(l1, k1) / 256.0)
    c["f1"] = np.concatenate([F1.real, F1.imag], axis=1).astype(BF)

    TWm = np.exp(-2j * np.pi * np.outer(l2, k1) / NF)
    c["twr2"] = np.concatenate([TWm.real, TWm.real], axis=1).astype(BF)
    c["twi2"] = np.concatenate([TWm.imag, TWm.imag], axis=1).astype(BF)

    ang = 2 * np.pi * np.outer(l2, k2) / 128.0
    c["f2re"] = np.cos(ang).astype(BF)
    c["f2im"] = (-np.sin(ang)).astype(BF)
    c["f2imn"] = (np.sin(ang)).astype(BF)

    F2i_r = np.cos(ang)
    F2i_i = np.sin(ang)
    c["ra"] = np.concatenate([F2i_r, F2i_i], axis=1).astype(BF)
    c["rb"] = np.concatenate([-F2i_i, F2i_r], axis=1).astype(BF)
    itwe = np.exp(+2j * np.pi * l2 * 128.0 / NF)
    era_r = F2i_r * itwe.real[None, :] - F2i_i * itwe.imag[None, :]
    era_i = F2i_r * itwe.imag[None, :] + F2i_i * itwe.real[None, :]
    c["rae"] = np.concatenate([era_r, era_i], axis=1).astype(BF)
    c["rbe"] = np.concatenate([-era_i, era_r], axis=1).astype(BF)

    IT = np.exp(+2j * np.pi * np.outer(k1[:128], l2) / NF)
    c["itr2"] = np.concatenate([IT.real, IT.real], axis=1).astype(BF)
    c["iti2"] = np.concatenate([IT.imag, IT.imag], axis=1).astype(BF)

    cc = np.ones(K1H)
    cc[0] = 0.5
    cc[128] = 0.5
    G1 = (2.0 * cc[:, None] / NF) * np.exp(+2j * np.pi * np.outer(k1, l1) / 256.0)
    c["g1re"] = np.ascontiguousarray(G1.real[:128]).astype(BF)
    c["g1imn"] = np.ascontiguousarray(-G1.imag[:128]).astype(BF)
    c["g1er"] = np.ascontiguousarray(G1.real[128:129]).astype(BF)
    c["g1ein"] = np.ascontiguousarray(-G1.imag[128:129]).astype(BF)

    lf = np.arange(L, dtype=np.float32)
    ang2 = (np.complex64(-2j * np.pi) * lf.astype(np.complex64) / np.float32(L)).astype(np.complex64)
    z = np.exp(ang2).astype(np.complex64)
    one = np.complex64(1.0)
    w = ((one - z) / (one + z)).astype(np.complex64)
    tz = (np.complex64(2.0) / (one + z)).astype(np.complex64)
    c["w_re"] = w.real.astype(np.float32)
    c["w_im"] = w.imag.astype(np.float32)
    c["tz_re"] = tz.real.astype(np.float32)
    c["tz_im"] = tz.imag.astype(np.float32)

    m1 = np.arange(128)
    m2 = np.arange(128)
    FA = np.exp(+2j * np.pi * np.outer(m1, l2) / 128.0) / 128.0
    c["ifab"] = np.concatenate([FA.real, FA.imag], axis=1).astype(BF)
    c["ifbb"] = np.concatenate([-FA.imag, FA.real], axis=1).astype(BF)
    ITW2 = np.exp(+2j * np.pi * np.outer(m2, l2) / L)
    c["iftr2"] = np.concatenate([ITW2.real, ITW2.real], axis=1).astype(BF)
    c["ifti2"] = np.concatenate([ITW2.imag, ITW2.imag], axis=1).astype(BF)
    FB = np.exp(+2j * np.pi * np.outer(m2, l1) / 128.0) / 128.0
    c["ifbre"] = FB.real.astype(BF)
    c["ifbim"] = FB.imag.astype(BF)
    c["ifbimn"] = (-FB.imag).astype(BF)

    c["ident"] = np.eye(128, dtype=np.float32).astype(BF)
    c["ident32"] = np.eye(128, dtype=np.float32)
    c["ones1x128"] = np.ones((1, 128), dtype=np.float32).astype(BF)
    c["ones32r"] = np.ones((1, 128), dtype=np.float32)
    c["ones64c"] = np.ones((64, 1), dtype=np.float32)
    sgn = np.ones((128, 1), dtype=np.float32)
    for r in range(8):
        sgn[16 * r + 8:16 * r + 16] = -1.0
    c["sign_col"] = sgn
    return c


_CONSTS = None

def consts():
    global _CONSTS
    if _CONSTS is None:
        _CONSTS = _host_consts()
    return _CONSTS


class Plan:

    def __init__(self, nc, stack):
        self.nc = nc
        self.stack = stack
        self.sems = {}
        self.cnt = {}

    def sem(self, name):
        if name not in self.sems:
            self.sems[name] = self.stack.enter_context(self.nc.semaphore(name))
            self.cnt[name] = 0
        return self.sems[name]

    def inc(self, name, instr, k=1):
        instr.then_inc(self.sem(name), k)
        self.cnt[name] += k
        return self.cnt[name]

    def val(self, name):
        self.sem(name)
        return self.cnt[name]


def dma_in(plan, eng, out, in_, sem="ld"):
    return plan.inc(sem, eng.dma_start(out=out, in_=in_), 16)


CONST_NAMES = ["f1", "twr2", "twi2", "f2re", "f2im", "f2imn", "ra", "rb", "rae",
               "rbe", "itr2", "iti2", "g1re", "g1imn", "g1er", "g1ein",
               "ifab", "ifbb", "iftr2", "ifti2", "ifbre", "ifbim", "ifbimn",
               "ident", "ident32", "ones1x128", "ones32r", "ones64c", "sign_col"]


class Prog:

    def __init__(self, nc, stack, rank):
        self.nc = nc
        self.stack = stack
        self.scopes = [stack]
        self.rank = rank
        self.plan = Plan(nc, stack)
        self.d = {}
        self.s = {}
        self.p = {}
        self.marks = {}

    def push(self):
        import contextlib
        st = contextlib.ExitStack()
        self.scopes.append(st)
        return st

    def pop(self):
        self.scopes.pop().close()

    def dram(self, name, shape, dtype, kind=None):
        if kind is None:
            t = self.nc.dram_tensor(name, shape, dtype)
        else:
            t = self.nc.dram_tensor(name, shape, dtype, kind=kind)
        self.d[name] = t.ap()
        return self.d[name]

    def sbuf(self, name, shape, dtype):
        t = self.scopes[-1].enter_context(self.nc.sbuf_tensor(name, shape, dtype))
        self.s[name] = t
        return t

    def sbuf_global(self, name, shape, dtype):
        t = self.scopes[0].enter_context(
            self.nc.sbuf_tensor(name, shape, dtype, side="right"))
        self.s[name] = t
        return t

    def psum(self, name, shape, dtype=F32):
        t = self.scopes[-1].enter_context(self.nc.psum_tensor(name, shape, dtype))
        self.p[name] = t
        return t


def declare_conv_io(pr, with_inputs=True):
    c = consts()
    pr.dram("x_full", [T, HD], F32, kind="ExternalInput")
    pr.dram("x_ch", [CPC, B, L], F32, kind="ExternalInput")
    for nm in ["B_re", "B_im", "P_re", "P_im", "Q_re", "Q_im", "diag_re", "diag_im"]:
        pr.dram(nm, [NMODE, 1], F32, kind="ExternalInput")
    pr.dram("step", [1, 1], F32, kind="ExternalInput")
    pr.dram("Dv", [1, 1], F32, kind="ExternalInput")
    pr.dram("C_re", [CPC, NMODE], F32, kind="ExternalInput")
    pr.dram("C_im", [CPC, NMODE], F32, kind="ExternalInput")
    pr.dram("w_re", [1, 2048], F32, kind="ExternalInput")
    pr.dram("w_im", [1, 2048], F32, kind="ExternalInput")
    pr.dram("tz_re", [1, 2048], F32, kind="ExternalInput")
    pr.dram("tz_im", [1, 2048], F32, kind="ExternalInput")
    for nm in CONST_NAMES:
        arr = c[nm]
        dt = BF16 if arr.dtype == BF else F32
        pr.dram("c_" + nm, list(arr.shape), dt, kind="ExternalInput")
    pr.dram("x_ts", [TPC, HD], F32, kind="ExternalInput")
    pr.dram("cmix_pre", [128, CPC], F32, kind="ExternalInput")
    pr.dram("gidx", [128, 16], U32, kind="ExternalInput")
    pr.dram("g_agin", [2048, 128], F32)
    pr.dram("g_agout", [16384, 128], F32)
    pr.dram("bas_agin", [16, 66048], BF16)
    pr.dram("bas_agout", [128, 66048], BF16)
    pr.dram("kf", [CPC, 66048], BF16)
    pr.dram("st_agin", [64, 128], F32)
    pr.dram("st_agout", [512, 128], F32)
    for _j in range(4):
        pr.dram("ssm_agin%d" % _j, [32, T], BF16)
        pr.dram("ssm_ago%d" % _j, [256, T], BF16)
    pr.dram("kg_rows", [8, 2048], F32)


def conv_const_loads(pr, g):
    plan = pr.plan
    c = consts()
    pr.bnd32767 = g.alloc_register()
    g.reg_mov(pr.bnd32767, 32767)
    pr.bnd16383 = g.alloc_register()
    g.reg_mov(pr.bnd16383, 16383)
    for nm, w in [("twr2", 258), ("twi2", 258)]:
        tile = pr.sbuf(nm + "_4", [128, 4 * w], BF16)
        for j in range(4):
            dma_in(plan, g, tile[:, j * w:(j + 1) * w], pr.d["c_" + nm], "cl")
    for nm, w in [("itr2", 256), ("iti2", 256)]:
        tile = pr.sbuf(nm + "_4", [128, 4 * w], BF16)
        for j in range(4):
            dma_in(plan, g, tile[:, j * w:(j + 1) * w], pr.d["c_" + nm], "cl")
    for nm in ["f1", "f2re", "f2im", "f2imn", "ra", "rb", "rae", "rbe",
               "g1re", "g1imn", "g1er", "g1ein", "ident", "ones1x128",
               "ifab", "ifbb", "iftr2", "ifti2", "ifbre", "ifbim", "ifbimn"]:
        arr = c[nm]
        tile = pr.sbuf(nm, list(arr.shape), BF16)
        dma_in(plan, g, tile[:], pr.d["c_" + nm], "cl")
    for nm in ["ones32r", "ones64c", "sign_col", "ident32"]:
        arr = c[nm]
        tile = pr.sbuf(nm, list(arr.shape), F32)
        dma_in(plan, g, tile[:], pr.d["c_" + nm], "cl")
    for nm in ["B_re", "B_im", "P_re", "P_im", "Q_re", "Q_im", "diag_re", "diag_im"]:
        tile = pr.sbuf("p_" + nm, [NMODE, 1], F32)
        dma_in(plan, g, tile[:], pr.d[nm], "cl")
    for nm in ["step", "Dv"]:
        tile = pr.sbuf("p_" + nm, [1, 1], F32)
        dma_in(plan, g, tile[:], pr.d[nm], "cl")
    for nm in ["w_re", "w_im", "tz_re", "tz_im"]:
        tile = pr.sbuf("r_" + nm, [1, 2048], F32)
        dma_in(plan, g, tile[:], pr.d[nm], "cl")
    for nm in ["C_re", "C_im"]:
        tile = pr.sbuf("s" + nm, [CPC, NMODE], F32)
        dma_in(plan, g, tile[:], pr.d[nm], "cl")
    tile = pr.sbuf("scmix_pre", [128, CPC], F32)
    dma_in(plan, g, tile[:], pr.d["cmix_pre"], "cl")
    return plan.val("cl")


def emit_stats(pr, block):
    plan = pr.plan
    nt = TPC // 128

    x2 = pr.sbuf("st_x", [128, 2 * HD], F32)
    sq_scr = pr.sbuf("st_sqscr", [128, HD], BF16)
    s1c = pr.sbuf("st_s1", [128, nt], F32)
    sqc = pr.sbuf("st_sq", [128, nt], F32)
    stats = pr.sbuf_global("st_out", [128, 64], F32)
    scr = pr.sbuf("st_scr", [128, 4], F32)
    tp = pr.sbuf("st_tp", [64, 128], F32)
    ps = pr.psum("st_ps", [128, 512], F32)

    @block.sync
    def _(sy):
        for i in range(nt):
            if i >= 2:
                sy.wait_ge(plan.sem("sv"), 6 * (i - 2) + 1)
                sy.wait_ge(plan.sem("ss1"), i - 1)
            dma_in(plan, sy, x2[:, (i % 2) * HD:(i % 2 + 1) * HD],
                   pr.d["x_ts"][128 * i:128 * (i + 1), :], "sl%d" % (i % 2))

    @block.scalar
    def _(sc):
        for i in range(nt):
            sc.wait_ge(plan.sem("sl%d" % (i % 2)), 16 * (i // 2 + 1))
            plan.inc("ss1", sc.activation(sq_scr[:], x2[:, (i % 2) * HD:(i % 2 + 1) * HD],
                                          AF.Square, accum_out=sqc[:, i:i + 1]))
            sc.wait_ge(plan.sem("sv"), 6 * i + 6)
            plan.inc("ss2", sc.activation(stats[:, 2 * i + 1:2 * i + 2],
                                          scr[:, (i % 2) * 2 + 1:(i % 2) * 2 + 2], AF.Sqrt))

    @block.vector
    def _(v):
        for i in range(nt):
            v.wait_ge(plan.sem("sl%d" % (i % 2)), 16 * (i // 2 + 1))
            n0 = plan.inc("sv", v.reduce_sum(
                s1c[:, i:i + 1], x2[:, (i % 2) * HD:(i % 2 + 1) * HD],
                axis=mybir.AxisListType.X))
            v.wait_ge(plan.sem("sv"), n0)
            v.wait_ge(plan.sem("ss1"), i + 1)
            if i >= 2:
                v.wait_ge(plan.sem("ss2"), i - 1)
            n1 = plan.inc("sv", v.tensor_scalar(
                stats[:, 2 * i:2 * i + 1], s1c[:, i:i + 1], 1.0 / HD, None, op0=ALU.mult))
            v.wait_ge(plan.sem("sv"), n1)
            n2 = plan.inc("sv", v.tensor_tensor(
                scr[:, (i % 2) * 2:(i % 2) * 2 + 1], stats[:, 2 * i:2 * i + 1],
                stats[:, 2 * i:2 * i + 1], op=ALU.mult))
            v.wait_ge(plan.sem("sv"), n2)
            n3 = plan.inc("sv", v.tensor_scalar(
                scr[:, (i % 2) * 2 + 1:(i % 2) * 2 + 2], sqc[:, i:i + 1],
                1.0 / HD, EPS, op0=ALU.mult, op1=ALU.add))
            v.wait_ge(plan.sem("sv"), n3)
            n4 = plan.inc("sv", v.tensor_tensor(
                scr[:, (i % 2) * 2 + 1:(i % 2) * 2 + 2],
                scr[:, (i % 2) * 2 + 1:(i % 2) * 2 + 2],
                scr[:, (i % 2) * 2:(i % 2) * 2 + 1], op=ALU.subtract))
            v.wait_ge(plan.sem("sv"), n4)
            plan.inc("sv", v.reciprocal(
                scr[:, (i % 2) * 2 + 1:(i % 2) * 2 + 2],
                scr[:, (i % 2) * 2 + 1:(i % 2) * 2 + 2]))
        v.wait_ge(plan.sem("stp"), 1)
        plan.inc("sve", v.tensor_copy(tp[:], ps[0:64, 0:128]))

    @block.tensor
    def _(te):
        te.wait_ge(plan.sem("ss2"), nt)
        te.wait_ge(plan.sem("cl"), plan.val("cl"))
        plan.inc("stp", te.transpose(ps[0:64, 0:128], stats[:], pr.s["ident32"][:]))

    @block.gpsimd
    def _(g):
        g.wait_ge(plan.sem("sve"), 1)
        dma_in(plan, g, pr.d["st_agin"], tp[:], "sgi")
        g.wait_ge(plan.sem("sgi"), 16)
        plan.inc("cc_st", g.collective_compute(
            "AllGather", ALU.bypass, replica_groups=[list(range(NCORES))],
            ins=[pr.d["st_agin"].opt()], outs=[pr.d["st_agout"].opt()]))


def emit_kg(pr, block):
    plan = pr.plan
    s = pr.s
    NCH = 4
    W = 512

    wb = {}
    for nm in ["wre", "wim", "tzre", "tzim", "qr", "qi"]:
        wb[nm] = pr.sbuf("kg_" + nm + "_b", [64, W], F32)
    cols = pr.sbuf("kg_cols", [64, 16], F32)
    rows = {}
    for nm in ["k10r", "k10i", "k11r", "k11i", "qrr", "qir", "scr1", "scr2"]:
        rows[nm] = pr.sbuf("kg_r_" + nm, [1, W], F32)
    t_a = pr.sbuf("kg_ta", [64, W], F32)
    t_b = pr.sbuf("kg_tb", [64, W], F32)
    t_c = pr.sbuf("kg_tc", [64, W], F32)
    dr = pr.sbuf("kg_dr", [64, W], F32)
    di = pr.sbuf("kg_di", [64, W], F32)
    cre = pr.sbuf("kg_cre", [64, W], F32)
    cim = pr.sbuf("kg_cim", [64, W], F32)
    gre = pr.sbuf("kg_gre", [64, W], F32)
    gim = pr.sbuf("kg_gim", [64, W], F32)
    smallc = pr.sbuf("kg_smallc", [1, 8], F32)
    ps0 = pr.psum("kg_ps0", [128, 512], F32)
    ps1 = pr.psum("kg_ps1", [128, 512], F32)

    VS = "vself"
    PS = "kps"

    @block.vector
    def _(v):
        def vop(instr, k=1):
            n = plan.inc(VS, instr, k)
            v.wait_ge(plan.sem(VS), n)
            return n
        v.wait_ge(plan.sem("cl"), plan.val("cl"))
        vop(v.tensor_scalar(smallc[0:1, 0:1], s["p_step"][0:1, 0:1], 1e-6, None, op0=ALU.max))
        vop(v.reciprocal(smallc[0:1, 1:2], smallc[0:1, 0:1]))
        n = vop(v.tensor_scalar(smallc[0:1, 2:3], smallc[0:1, 1:2], 2.0, None, op0=ALU.mult))
        pr.marks["kg_r2"] = n
        vop(v.tensor_tensor(cols[:, 0:1], s["p_Q_re"][:], s["p_B_re"][:], op=ALU.mult))
        vop(v.tensor_tensor(t_a[:, 0:1], s["p_Q_im"][:], s["p_B_im"][:], op=ALU.mult))
        vop(v.tensor_tensor(cols[:, 0:1], cols[:, 0:1], t_a[:, 0:1], op=ALU.subtract))
        vop(v.tensor_tensor(cols[:, 1:2], s["p_Q_re"][:], s["p_B_im"][:], op=ALU.mult))
        vop(v.tensor_tensor(t_a[:, 0:1], s["p_Q_im"][:], s["p_B_re"][:], op=ALU.mult))
        vop(v.tensor_tensor(cols[:, 1:2], cols[:, 1:2], t_a[:, 0:1], op=ALU.add))
        vop(v.tensor_scalar(cols[:, 2:3], cols[:, 1:2], -1.0, None, op0=ALU.mult))
        vop(v.tensor_tensor(cols[:, 3:4], s["p_Q_re"][:], s["p_P_re"][:], op=ALU.mult))
        vop(v.tensor_tensor(t_a[:, 0:1], s["p_Q_im"][:], s["p_P_im"][:], op=ALU.mult))
        vop(v.tensor_tensor(cols[:, 3:4], cols[:, 3:4], t_a[:, 0:1], op=ALU.subtract))
        vop(v.tensor_tensor(cols[:, 4:5], s["p_Q_re"][:], s["p_P_im"][:], op=ALU.mult))
        vop(v.tensor_tensor(t_a[:, 0:1], s["p_Q_im"][:], s["p_P_re"][:], op=ALU.mult))
        vop(v.tensor_tensor(cols[:, 4:5], cols[:, 4:5], t_a[:, 0:1], op=ALU.add))
        vop(v.tensor_scalar(cols[:, 5:6], cols[:, 4:5], -1.0, None, op0=ALU.mult))
        n = vop(v.tensor_scalar(cols[:, 6:7], s["p_P_im"][:], -1.0, None, op0=ALU.mult))
        pr.marks["kg_cols"] = n
        v.wait_ge(plan.sem(PS), 1)
        vop(v.tensor_copy(cols[:, 7:8], ps0[0:64, 0:1]))
        plan.inc("kev", v.nop())
        for ci in range(NCH):
            pb = 1 + ci * 10
            for bi, nm in enumerate(["wre", "wim", "tzre", "tzim"]):
                v.wait_ge(plan.sem(PS), pb + bi + 1)
                vop(v.tensor_copy(wb[nm][:], ps1[0:64, 0:W]))
                plan.inc("kev", v.nop())
            vop(v.tensor_scalar(dr[:], wb["wre"][:], cols[:, 7:8], s["p_diag_re"][:],
                                op0=ALU.mult, op1=ALU.subtract))
            vop(v.tensor_scalar(di[:], wb["wim"][:], cols[:, 7:8], s["p_diag_im"][:],
                                op0=ALU.mult, op1=ALU.subtract))
            vop(v.tensor_tensor(t_a[:], dr[:], dr[:], op=ALU.mult))
            vop(v.tensor_tensor(t_b[:], di[:], di[:], op=ALU.mult))
            vop(v.tensor_tensor(t_a[:], t_a[:], t_b[:], op=ALU.add))
            vop(v.reciprocal(t_a[:], t_a[:]))
            vop(v.tensor_tensor(cre[:], dr[:], t_a[:], op=ALU.mult))
            vop(v.scalar_tensor_tensor(cim[:], di[:], -1.0, t_a[:], op0=ALU.mult, op1=ALU.mult))
            for term, (c1, c2) in enumerate([(0, 2), (1, 0), (3, 5), (4, 3)]):
                vop(v.tensor_scalar(t_b[:], cre[:], cols[:, c1:c1 + 1], None, op0=ALU.mult))
                n = vop(v.scalar_tensor_tensor(t_c[:], cim[:], cols[:, c2:c2 + 1], t_b[:],
                                               op0=ALU.mult, op1=ALU.add))
                pr.marks["kterm_%d_%d" % (ci, term)] = n
                v.wait_ge(plan.sem(PS), pb + 4 + term + 1)
                vop(v.tensor_copy(rows[["k10r", "k10i", "k11r", "k11i"][term]][:],
                                  ps0[0:1, 0:W]))
                plan.inc("kev2", v.nop())
            vop(v.tensor_scalar(rows["scr1"][:], rows["k11r"][:], 1.0, None, op0=ALU.add))
            vop(v.tensor_tensor(rows["scr2"][:], rows["scr1"][:], rows["scr1"][:], op=ALU.mult))
            vop(v.tensor_tensor(rows["qrr"][:], rows["k11i"][:], rows["k11i"][:], op=ALU.mult))
            vop(v.tensor_tensor(rows["scr2"][:], rows["scr2"][:], rows["qrr"][:], op=ALU.add))
            vop(v.reciprocal(rows["scr2"][:], rows["scr2"][:]))
            vop(v.tensor_tensor(rows["qrr"][:], rows["k10r"][:], rows["scr1"][:], op=ALU.mult))
            vop(v.tensor_tensor(rows["qir"][:], rows["k10i"][:], rows["k11i"][:], op=ALU.mult))
            vop(v.tensor_tensor(rows["qrr"][:], rows["qrr"][:], rows["qir"][:], op=ALU.add))
            vop(v.tensor_tensor(rows["qrr"][:], rows["qrr"][:], rows["scr2"][:], op=ALU.mult))
            vop(v.tensor_tensor(rows["qir"][:], rows["k10i"][:], rows["scr1"][:], op=ALU.mult))
            vop(v.tensor_tensor(rows["scr1"][:], rows["k10r"][:], rows["k11i"][:], op=ALU.mult))
            vop(v.tensor_tensor(rows["qir"][:], rows["qir"][:], rows["scr1"][:], op=ALU.subtract))
            n = vop(v.tensor_tensor(rows["qir"][:], rows["qir"][:], rows["scr2"][:], op=ALU.mult))
            pr.marks["kqrow_%d" % ci] = n
            for bi, nm in enumerate(["qr", "qi"]):
                v.wait_ge(plan.sem(PS), pb + 8 + bi + 1)
                vop(v.tensor_copy(wb[nm][:], ps1[0:64, 0:W]))
                plan.inc("kev", v.nop())
            vop(v.tensor_scalar(t_a[:], wb["qr"][:], s["p_P_re"][:], None, op0=ALU.mult))
            vop(v.scalar_tensor_tensor(t_a[:], wb["qi"][:], cols[:, 6:7], t_a[:],
                                       op0=ALU.mult, op1=ALU.add))
            vop(v.tensor_scalar(t_a[:], t_a[:], -1.0, s["p_B_re"][:], op0=ALU.mult, op1=ALU.add))
            vop(v.tensor_scalar(t_b[:], wb["qi"][:], s["p_P_re"][:], None, op0=ALU.mult))
            vop(v.scalar_tensor_tensor(t_b[:], wb["qr"][:], s["p_P_im"][:], t_b[:],
                                       op0=ALU.mult, op1=ALU.add))
            vop(v.tensor_scalar(t_b[:], t_b[:], -1.0, s["p_B_im"][:], op0=ALU.mult, op1=ALU.add))
            if ci > 0:
                v.wait_ge(plan.sem("kgw"), 32 * ci)
            vop(v.tensor_tensor(t_c[:], cre[:], t_a[:], op=ALU.mult))
            vop(v.tensor_tensor(gre[:], cim[:], t_b[:], op=ALU.mult))
            vop(v.tensor_tensor(t_c[:], t_c[:], gre[:], op=ALU.subtract))
            vop(v.tensor_tensor(gim[:], cre[:], t_b[:], op=ALU.mult))
            vop(v.tensor_tensor(gre[:], cim[:], t_a[:], op=ALU.mult))
            vop(v.tensor_tensor(gim[:], gim[:], gre[:], op=ALU.add))
            vop(v.tensor_tensor(gre[:], wb["tzre"][:], t_c[:], op=ALU.mult))
            vop(v.tensor_tensor(t_a[:], wb["tzim"][:], gim[:], op=ALU.mult))
            vop(v.tensor_tensor(gre[:], gre[:], t_a[:], op=ALU.subtract))
            vop(v.tensor_tensor(gim[:], wb["tzre"][:], gim[:], op=ALU.mult))
            vop(v.tensor_tensor(t_a[:], wb["tzim"][:], t_c[:], op=ALU.mult))
            n = vop(v.tensor_tensor(gim[:], gim[:], t_a[:], op=ALU.add))
            pr.marks["kg_done_%d" % ci] = n

    @block.tensor
    def _(te):
        te.wait_ge(plan.sem("cl"), plan.val("cl"))
        te.wait_ge(plan.sem(VS), pr.marks["kg_r2"])
        plan.inc(PS, te.matmul(ps0[0:64, 0:1], s["ones32r"][0:1, 0:64],
                               smallc[0:1, 2:3], start=True, stop=True,
                               skip_group_check=True))
        nev = 0
        nev2 = 0
        for ci in range(NCH):
            for bi, nm in enumerate(["r_w_re", "r_w_im", "r_tz_re", "r_tz_im"]):
                nev += 1
                te.wait_ge(plan.sem("kev"), nev)
                plan.inc(PS, te.matmul(ps1[0:64, 0:W], s["ones32r"][0:1, 0:64],
                                       s[nm][0:1, W * ci:W * (ci + 1)],
                                       start=True, stop=True, skip_group_check=True))
            for term in range(4):
                te.wait_ge(plan.sem(VS), pr.marks["kterm_%d_%d" % (ci, term)])
                nev2 += 1
                if nev2 > 1:
                    te.wait_ge(plan.sem("kev2"), nev2 - 1)
                plan.inc(PS, te.matmul(ps0[0:1, 0:W], s["ones64c"][:], pr.s["kg_tc"][:],
                                       start=True, stop=True, skip_group_check=True))
            te.wait_ge(plan.sem(VS), pr.marks["kqrow_%d" % ci])
            for src in [pr.s["kg_r_qrr"], pr.s["kg_r_qir"]]:
                nev += 1
                te.wait_ge(plan.sem("kev"), nev)
                plan.inc(PS, te.matmul(ps1[0:64, 0:W], s["ones32r"][0:1, 0:64],
                                       src[0:1, 0:W], start=True, stop=True,
                                       skip_group_check=True))

    @block.sync
    def _(sy):
        gv = pr.d["g_agin"].rearrange("(n x) c -> n (x c)", x=16)
        for ci in range(NCH):
            sy.wait_ge(plan.sem(VS), pr.marks["kg_done_%d" % ci])
            dma_in(plan, sy, gv[0:64, W * ci:W * (ci + 1)], pr.s["kg_gre"][:], "kgw")
            dma_in(plan, sy, gv[64:128, W * ci:W * (ci + 1)], pr.s["kg_gim"][:], "kgw")

    @block.gpsimd
    def _(g):
        g.wait_ge(plan.sem("kgw"), 32 * NCH)
        plan.inc("cc_g", g.collective_compute(
            "AllGather", ALU.bypass, replica_groups=[list(range(NCORES))],
            ins=[pr.d["g_agin"].opt()], outs=[pr.d["g_agout"].opt()]))


def emit_basis(pr, block):
    plan = pr.plan
    s = pr.s

    gret = pr.sbuf("kb_gre", [128, 128], BF16)
    gimt = pr.sbuf("kb_gim", [128, 128], BF16)
    a1s = pr.sbuf("kb_a1", [128, 256], BF16)
    a2s = pr.sbuf("kb_a2", [128, 256], BF16)
    krs = pr.sbuf("kb_kr", [128, 128], BF16)
    kis = pr.sbuf("kb_ki", [128, 128], BF16)
    bsb = pr.sbuf("kb_bs", [128, 258], BF16)
    cb = pr.sbuf("kb_c", [128, 258], BF16)
    xb = pr.sbuf("kb_x", [128, 258], BF16)
    gidx = pr.s["kb_gidx"]
    ps2 = pr.psum("kb_ps2", [128, 512], F32)
    ps3 = pr.psum("kb_ps3", [128, 512], F32)
    ps4 = pr.psum("kb_ps4", [128, 512], F32)
    ps5 = pr.psum("kb_ps5", [128, 512], F32)

    @block.gpsimd
    def _(g):
        ngi = dma_in(plan, g, gidx[:], pr.d["gidx"], "cl")
        g.wait_ge(plan.sem("cl"), ngi)
        g.wait_ge(plan.sem("cc_g"), 1)
        for j in range(EPC):
            if j > 0:
                g.wait_ge(plan.sem("kbp"), 14 * (j - 1) + 2)
            plan.inc("kbg", g.indirect_dma_start(
                out=gret[:], out_offset=None, in_=pr.d["g_agout"],
                in_offset=bass.IndirectOffsetOnAxis(ap=gidx[:, 2 * j:2 * j + 1], axis=0),
                bounds_check=pr.bnd16383), 16)
            plan.inc("kbg", g.indirect_dma_start(
                out=gimt[:], out_offset=None, in_=pr.d["g_agout"],
                in_offset=bass.IndirectOffsetOnAxis(ap=gidx[:, 2 * j + 1:2 * j + 2], axis=0),
                bounds_check=pr.bnd16383), 16)

    @block.tensor
    def _(te):
        for j in range(EPC):
            te.wait_ge(plan.sem("kbg"), 32 * (j + 1))
            if j > 0:
                te.wait_ge(plan.sem("kbe"), 5 * (j - 1) + 3)
            te.ldweights(gret[:])
            plan.inc("kbp", te.matmul(ps2[:, 0:256], gret[:], s["ifab"][:],
                                      start=True, stop=False, skip_group_check=True))
            te.ldweights(gimt[:])
            plan.inc("kbp", te.matmul(ps2[:, 0:256], gimt[:], s["ifbb"][:],
                                      start=False, stop=True, skip_group_check=True))
            te.wait_ge(plan.sem("kbv"), 4 * j + 4)
            te.ldweights(s["ifbre"][:])
            plan.inc("kbp", te.matmul(ps3[:, 0:128], s["ifbre"][:], a2s[:, 0:128],
                                      start=True, stop=False, skip_group_check=True))
            te.ldweights(s["ifbimn"][:])
            plan.inc("kbp", te.matmul(ps3[:, 0:128], s["ifbimn"][:], a2s[:, 128:256],
                                      start=False, stop=True, skip_group_check=True))
            te.ldweights(s["ifbre"][:])
            plan.inc("kbp", te.matmul(ps3[:, 128:256], s["ifbre"][:], a2s[:, 128:256],
                                      start=True, stop=False, skip_group_check=True))
            te.ldweights(s["ifbim"][:])
            plan.inc("kbp", te.matmul(ps3[:, 128:256], s["ifbim"][:], a2s[:, 0:128],
                                      start=False, stop=True, skip_group_check=True))
            for r in range(2):
                src = krs if r == 0 else kis
                te.wait_ge(plan.sem("kbe"), 5 * j + 2 + r)
                te.wait_ge(plan.sem("kbe"), (5 * j + 4) if r == 1 else (5 * j if j > 0 else 0))
                te.wait_ge(plan.sem("kbe2"), 2 * j + r)
                te.ldweights(src[:])
                plan.inc("kbp", te.matmul(ps4[:, 0:258], src[:], s["f1"][:],
                                          start=True, stop=True, skip_group_check=True))
                te.wait_ge(plan.sem("kbv2"), (2 * j + r) * 4 + 4)
                te.ldweights(s["f2re"][:])
                plan.inc("kbp", te.matmul(ps5[:, 0:258], s["f2re"][:], cb[:, 0:258],
                                          start=True, stop=False, skip_group_check=True))
                te.ldweights(s["f2imn"][:])
                plan.inc("kbp", te.matmul(ps5[:, 0:129], s["f2imn"][:], cb[:, 129:258],
                                          start=False, stop=False, skip_group_check=True))
                te.ldweights(s["f2im"][:])
                plan.inc("kbp", te.matmul(ps5[:, 129:258], s["f2im"][:], cb[:, 0:129],
                                          start=False, stop=True, skip_group_check=True))

    @block.scalar
    def _(sc):
        for j in range(EPC):
            sc.wait_ge(plan.sem("kbp"), 14 * j + 2)
            sc.wait_ge(plan.sem("kbv"), 4 * j)
            plan.inc("kbe", sc.copy(a1s[:], ps2[:, 0:256]))
            sc.wait_ge(plan.sem("kbp"), 14 * j + 4)
            plan.inc("kbe", sc.copy(krs[:], ps3[:, 0:128]))
            sc.wait_ge(plan.sem("kbp"), 14 * j + 6)
            plan.inc("kbe", sc.copy(kis[:], ps3[:, 128:256]))
            for r in range(2):
                sc.wait_ge(plan.sem("kbp"), 14 * j + 7 + 4 * r)
                sc.wait_ge(plan.sem("kbv2"), (2 * j + r) * 4)
                plan.inc("kbe", sc.copy(bsb[:], ps4[:, 0:258]))
                sc.wait_ge(plan.sem("kbp"), 14 * j + 10 + 4 * r)
                sc.wait_ge(plan.sem("kbw"), 64 * (2 * j + r))
                plan.inc("kbe2", sc.copy(xb[:], ps5[:, 0:258]))

    cs1 = pr.sbuf("kb_cs1", [128, 258], BF16)
    cs2 = pr.sbuf("kb_cs2", [128, 258], BF16)

    def cmul(v, plan, sem, dst, src, tr, ti, w, selfsem="vself"):
        a, b = cs1, cs2
        plan.inc(sem, v.tensor_tensor(a[:, 0:2 * w], src, tr, op=ALU.mult))
        n = plan.inc(selfsem, v.tensor_tensor(b[:, 0:2 * w], src, ti, op=ALU.mult))
        plan.inc(sem, v.nop())
        v.wait_ge(plan.sem(selfsem), n)
        plan.inc(sem, v.tensor_tensor(dst[:, 0:w], a[:, 0:w], b[:, w:2 * w],
                                      op=ALU.subtract))
        plan.inc(sem, v.tensor_tensor(dst[:, w:2 * w], a[:, w:2 * w], b[:, 0:w],
                                      op=ALU.add))

    @block.vector
    def _(v):
        for j in range(EPC):
            v.wait_ge(plan.sem("kbe"), 5 * j + 1)
            if j > 0:
                v.wait_ge(plan.sem("kbp"), 14 * (j - 1) + 6)
            cmul(v, plan, "kbv", a2s, a1s[:, 0:256], s["iftr2"][:], s["ifti2"][:], 128)
            for r in range(2):
                v.wait_ge(plan.sem("kbe"), 5 * j + 4 + r)
                if 2 * j + r > 0:
                    pi = 14 * j + 10 if r == 1 else 14 * (j - 1) + 14
                    v.wait_ge(plan.sem("kbp"), pi)
                cmul(v, plan, "kbv2", cb, bsb[:, 0:258], s["twr2_4"][:, 0:258],
                     s["twi2_4"][:, 0:258], 129)

    @block.sync
    def _(sy):
        for j in range(EPC):
            for r in range(2):
                row = j if r == 0 else 8 + j
                sy.wait_ge(plan.sem("kbe2"), 2 * j + r + 1)
                bv = pr.d["bas_agin"][row:row + 1, :].rearrange("a (k w) -> (a k) w", w=516)
                dma_in(plan, sy, bv[:, 0:129], xb[:, 0:129], "kbw")
                dma_in(plan, sy, bv[:, 129:258], xb[:, 0:129], "kbw")
                dma_in(plan, sy, bv[:, 258:387], xb[:, 129:258], "kbw")
                dma_in(plan, sy, bv[:, 387:516], xb[:, 129:258], "kbw")

    @block.gpsimd
    def _(g):
        g.wait_ge(plan.sem("kbw"), 64 * 16)
        plan.inc("cc_b", g.collective_compute(
            "AllGather", ALU.bypass, replica_groups=[list(range(NCORES))],
            ins=[pr.d["bas_agin"].opt()], outs=[pr.d["bas_agout"].opt()]))


def emit_cmix(pr, block):
    plan = pr.plan
    s = pr.s
    NCW = 65
    TOT = 129 * 512

    def cw(c):
        return min(1024 * (c + 1), TOT) - 1024 * c

    cmixb = pr.sbuf("cm_lhs", [128, 128], BF16)
    rhs = pr.sbuf("cm_rhs", [128, 4 * 1024], BF16)
    stg = pr.sbuf("cm_stg", [128, 4 * 1024], BF16)
    cps = pr.psum("cm_ps", [128, 2048], F32)

    @block.vector
    def _(v):
        v.wait_ge(plan.sem("cl"), plan.val("cl"))
        plan.inc("cmv", v.tensor_scalar(cmixb[:], s["scmix_pre"][:], s["sign_col"][:],
                                        None, op0=ALU.mult))

    @block.sync
    def _(sy):
        def store(j):
            sy.wait_ge(plan.sem("cme"), j + 1)
            plan.inc("cmw%d" % (j % 4), sy.dma_start(
                out=pr.d["kf"][:, 1024 * j:1024 * j + cw(j)],
                in_=stg[:, (j % 4) * 1024:(j % 4) * 1024 + cw(j)]), 16)

        sy.wait_ge(plan.sem("cc_b"), 1)
        for c in range(0, NCW, 2):
            if c >= 4:
                sy.wait_ge(plan.sem("cmp"), c - 3)
            dma_in(plan, sy, rhs[:, (c % 4) * 1024:(c % 4) * 1024 + cw(c)],
                   pr.d["bas_agout"][:, 1024 * c:1024 * c + cw(c)], "cml0")
            if c >= 6:
                store(c - 6)
                store(c - 5)
        for j in range(NCW - 5, NCW):
            store(j)

    @block.gpsimd
    def _(g):
        g.wait_ge(plan.sem("cc_b"), 1)
        for c in range(1, NCW, 2):
            if c >= 4:
                g.wait_ge(plan.sem("cmp"), c - 3)
            plan.inc("cmlg", g.dma_start(
                out=rhs[:, (c % 4) * 1024:(c % 4) * 1024 + cw(c)],
                in_=pr.d["bas_agout"][:, 1024 * c:1024 * c + cw(c)]), 16)

    @block.tensor
    def _(te):
        te.wait_ge(plan.sem("cmv"), 1)
        te.ldweights(cmixb[:])
        for c in range(NCW):
            te.wait_ge(plan.sem("cml0" if c % 2 == 0 else "cmlg"),
                       16 * min(c // 2 + 2, 33 if c % 2 == 0 else 32))
            if c >= 2:
                te.wait_ge(plan.sem("cme"), c - 1)
            for hh in range(0, cw(c), 512):
                _mm = te.matmul(cps[:, (c % 2) * 1024 + hh:(c % 2) * 1024 + hh + 512],
                                cmixb[:],
                                rhs[:, (c % 4) * 1024 + hh:(c % 4) * 1024 + hh + 512],
                                start=True, stop=True, skip_group_check=True)
            plan.inc("cmp", _mm)

    @block.scalar
    def _(sc):
        for c in range(NCW):
            sc.wait_ge(plan.sem("cmp"), c + 1)
            if c >= 4:
                sc.wait_ge(plan.sem("cmw%d" % (c % 4)), 16 * ((c - 4) // 4 + 1))
            plan.inc("cme", sc.copy(stg[:, (c % 4) * 1024:(c % 4) * 1024 + cw(c)],
                                    cps[:, (c % 2) * 1024:(c % 2) * 1024 + cw(c)]))


def ag_final(pr, g):
    plan = pr.plan
    g.wait_ge(plan.sem("cst0"), 64 * 32)
    g.wait_ge(plan.sem("cst1"), 64 * 32)
    plan.inc("cc_ssm", g.collective_compute(
        "AllGather", ALU.bypass, replica_groups=[list(range(NCORES))],
        ins=[pr.d["ssm_agin3"].opt()], outs=[pr.d["ssm_ago3"].opt()]))


def emit_conv(pr, block):
    plan = pr.plan
    s = pr.s
    NG = NSEQ // 4

    xin = pr.sbuf("cv_xin", [128, 16 * 128], BF16)
    xn = pr.sbuf("cv_xn", [128, 8 * 128], BF16)
    c4 = pr.sbuf("cv_c", [128, 8 * 258], BF16)
    y4 = pr.sbuf("cv_y", [128, 8 * 258], BF16)
    kf4 = pr.sbuf("cv_kf", [128, 8 * 516], BF16)
    pt4 = pr.sbuf("cv_pt", [128, 8 * 512], BF16)
    pp4 = pr.sbuf("cv_pp", [128, 8 * 256], BF16)
    hs4 = pr.sbuf("cv_hs", [128, 2 * 512], F32)
    sm4 = pr.sbuf("cv_sm", [128, 2 * 512], BF16)
    mus = pr.sbuf("cv_mu", [128, 4 * 128], BF16)
    muq = pr.sbuf("cv_muq", [128, 512], BF16)
    rsq = pr.sbuf("cv_rsq", [128, 512], BF16)
    dcol = pr.sbuf("cv_dcol", [128, 1], F32)
    psA = pr.psum("cv_ps", [128, 4096], F32)

    tw1 = pr.sbuf("cv_tw1", [128, 2 * 516], BF16)
    tw2 = pr.sbuf("cv_tw2", [128, 2 * 516], BF16)
    pw1 = pr.sbuf("cv_pw1", [128, 2 * 516], BF16)
    pw2 = pr.sbuf("cv_pw2", [128, 2 * 516], BF16)
    it1 = pr.sbuf("cv_it1", [128, 2 * 512], BF16)
    it2 = pr.sbuf("cv_it2", [128, 2 * 512], BF16)

    @block.gpsimd
    def _(g):
        g.wait_ge(plan.sem("cc_st"), 1)
        stv = pr.d["st_agout"].rearrange("(q w) c -> q w c", w=2)
        for b in range(B):
            dma_in(plan, g, mus[:, b * 128:(b + 1) * 128],
                   stv[128 * b:128 * (b + 1), 0, :], "cmu")
            dma_in(plan, g, mus[:, (2 + b) * 128:(3 + b) * 128],
                   stv[128 * b:128 * (b + 1), 1, :], "cmu")
        def ag_issue(j):
            g.wait_ge(plan.sem("cst0"), 64 * 8 * (j + 1))
            g.wait_ge(plan.sem("cst1"), 64 * 8 * (j + 1))
            plan.inc("cc_ssm", g.collective_compute(
                "AllGather", ALU.bypass, replica_groups=[list(range(NCORES))],
                ins=[pr.d["ssm_agin%d" % j].opt()],
                outs=[pr.d["ssm_ago%d" % j].opt()]))

        for sq in range(NSEQ):
            chl, b = sq // 2, sq % 2
            if sq >= 16:
                g.wait_ge(plan.sem("cln"), sq // 4 - 3)
            xv = pr.d["x_ch"][chl, b, :].rearrange("(a c) -> a c", c=128)
            dma_in(plan, g, xin[:, (sq % 16) * 128:(sq % 16 + 1) * 128], xv,
                   "cxlp%d" % ((sq // 2) % 2))
            if sq in (80, 144, 208):
                ag_issue(sq // 64 - 1)

    @block.sync
    def _(sy):
        sy.wait_ge(plan.sem("cmw0"), 16 * 17)
        sy.wait_ge(plan.sem("cmw1"), 16 * 16)
        sy.wait_ge(plan.sem("cmw2"), 16 * 16)
        sy.wait_ge(plan.sem("cmw3"), 16 * 16)
        for sq in range(NSEQ):
            chl = sq // 2
            if sq >= 8:
                sy.wait_ge(plan.sem("cpw"), (sq - 8) // 2 + 1)
            dma_in(plan, sy, kf4[:, (sq % 8) * 516:(sq % 8 + 1) * 516],
                   pr.d["kf"][chl:chl + 1, :].rearrange("a (k w) -> (a k) w", w=516),
                   "cklp%d" % ((sq // 4) % 2))

    @block.vector
    def _(v):
        v.wait_ge(plan.sem("cmu"), 64)
        plan.inc("vself", v.tensor_copy(muq[:, 0:256], mus[:, 0:256]))
        plan.inc("vself", v.tensor_copy(muq[:, 256:512], mus[:, 0:256]))
        plan.inc("vself", v.tensor_copy(rsq[:, 0:256], mus[:, 256:512]))
        n = plan.inc("vself", v.tensor_copy(rsq[:, 256:512], mus[:, 256:512]))
        v.wait_ge(plan.sem("vself"), n)
        v.wait_ge(plan.sem("kbe2"), 16)
        plan.inc("vself", v.memset(psA[:, 2048 + 256:2048 + 512], 0.0))
        plan.inc("vself", v.memset(psA[:, 2560 + 256:2560 + 512], 0.0))
        v.wait_ge(plan.sem("cdp"), 1)
        n = plan.inc("vself", v.tensor_copy(dcol[:], psA[0:128, 0:1]))
        v.wait_ge(plan.sem("vself"), n)
        plan.inc("cdv", v.nop())
        B1v = psA[:, 0:1024].rearrange("p (s w) -> p s w", w=512)
        Xv = psA[:, 1024:2048].rearrange("p (s w) -> p s w", w=512)
        PTv = psA[:, 2048:3072].rearrange("p (s w) -> p s w", w=512)
        t_r = s["twr2_4"][:, 0:516].rearrange("p (s w) -> p s w", w=258)
        t_i = s["twi2_4"][:, 0:516].rearrange("p (s w) -> p s w", w=258)
        it_r = s["itr2_4"][:, 0:512].rearrange("p (s w) -> p s w", w=256)
        it_i = s["iti2_4"][:, 0:512].rearrange("p (s w) -> p s w", w=256)
        for g_i in range(NG):
            gp = g_i % 2
            v.wait_ge(plan.sem("cxlp0"), 32 * min(g_i + 4, NG))
            v.wait_ge(plan.sem("cxlp1"), 32 * min(g_i + 4, NG))
            if g_i >= 2:
                v.wait_ge(plan.sem("cpe_s1"), 4 * (g_i - 1))
                v.wait_ge(plan.sem("cepi"), g_i - 1)
            xnv4 = xn[:, (4 * g_i % 8) * 128:((4 * g_i % 8) + 4) * 128]
            nn = plan.inc("vself", v.tensor_tensor(
                xnv4, xin[:, (4 * g_i % 16) * 128:((4 * g_i % 16) + 4) * 128],
                muq[:], op=ALU.subtract))
            v.wait_ge(plan.sem("vself"), nn)
            plan.inc("cln", v.tensor_tensor(xnv4, xnv4, rsq[:], op=ALU.mult))
            for h in range(2):
                cs = (4 * g_i + 2 * h) % 8
                v.wait_ge(plan.sem("cpe_s1"), 4 * g_i + 2 * h + 2)
                if g_i >= 2:
                    v.wait_ge(plan.sem("cpe_s2"), 4 * (g_i - 2) + 2 * h + 2)
                dst = c4[:, cs * 258:(cs + 2) * 258].rearrange("p (s w) -> p s w", w=258)
                a_v = tw1[:, h * 516:(h + 1) * 516].rearrange("p (s w) -> p s w", w=258)
                b_v = tw2[:, h * 516:(h + 1) * 516].rearrange("p (s w) -> p s w", w=258)
                plan.inc("ctw", v.tensor_tensor(a_v, B1v[:, :, 0:258], t_r, op=ALU.mult))
                n = plan.inc("vself", v.tensor_tensor(b_v, B1v[:, :, 0:258], t_i,
                                                      op=ALU.mult))
                v.wait_ge(plan.sem("vself"), n)
                plan.inc("ctw", v.tensor_tensor(dst[:, :, 0:129], a_v[:, :, 0:129],
                                                b_v[:, :, 129:258], op=ALU.subtract))
                plan.inc("ctw", v.tensor_tensor(dst[:, :, 129:258], a_v[:, :, 129:258],
                                                b_v[:, :, 0:129], op=ALU.add))
            for h in range(2):
                cs = (4 * g_i + 2 * h) % 8
                v.wait_ge(plan.sem("cpe_s2"), 4 * g_i + 2 * h + 2)
                v.wait_ge(plan.sem("cklp%d" % (g_i % 2)), 64 * (g_i // 2 + 1))
                if g_i >= 2:
                    v.wait_ge(plan.sem("cpe_is2"), 4 * (g_i - 2) + 2 * h + 2)
                kfp = kf4[:, cs * 516:(cs + 2) * 516].rearrange("p (s w) -> p s w", w=516)
                yp = y4[:, cs * 258:(cs + 2) * 258].rearrange("p (s w) -> p s w", w=258)
                a_v = pw1[:, h * 516:(h + 1) * 516].rearrange("p (s w) -> p s w", w=258)
                b_v = pw2[:, h * 516:(h + 1) * 516].rearrange("p (s w) -> p s w", w=258)
                plan.inc("cpw0", v.tensor_tensor(a_v, Xv[:, :, 0:258],
                                                 kfp[:, :, 0:258], op=ALU.mult))
                n = plan.inc("vself", v.tensor_tensor(b_v, Xv[:, :, 0:258],
                                                      kfp[:, :, 258:516], op=ALU.mult))
                v.wait_ge(plan.sem("vself"), n)
                plan.inc("cpw0", v.tensor_tensor(yp[:, :, 0:129], a_v[:, :, 0:129],
                                                 b_v[:, :, 129:258], op=ALU.subtract))
                plan.inc("cpw", v.tensor_tensor(yp[:, :, 129:258], a_v[:, :, 129:258],
                                                b_v[:, :, 0:129], op=ALU.add))
            v.wait_ge(plan.sem("cet"), 4 * g_i + 4)
            if g_i >= 2:
                v.wait_ge(plan.sem("cpe_is1"), g_i - 1)
            ptv = pt4[:, gp * 2048:(gp + 1) * 2048].rearrange("p (g w) -> p g w", w=512)
            ppr = pp4[:, gp * 1024:(gp + 1) * 1024].rearrange("p (g w) -> p g w", w=256)
            a4 = it1[:, 0:1024].rearrange("p (g w) -> p g w", w=256)
            b4 = it2[:, 0:1024].rearrange("p (g w) -> p g w", w=256)
            plan.inc("citw0", v.tensor_tensor(
                a4, ptv[:, :, 0:256],
                s["itr2_4"][:].rearrange("p (g w) -> p g w", w=256), op=ALU.mult))
            n = plan.inc("vself", v.tensor_tensor(
                b4, ptv[:, :, 0:256],
                s["iti2_4"][:].rearrange("p (g w) -> p g w", w=256), op=ALU.mult))
            v.wait_ge(plan.sem("vself"), n)
            plan.inc("citw0", v.tensor_tensor(ppr[:, :, 0:128], a4[:, :, 0:128],
                                              b4[:, :, 128:256], op=ALU.subtract))
            plan.inc("citw", v.tensor_tensor(ppr[:, :, 128:256], a4[:, :, 128:256],
                                             b4[:, :, 0:128], op=ALU.add))
            v.wait_ge(plan.sem("cpe_is1"), g_i + 1)
            if g_i >= 2:
                v.wait_ge(plan.sem("cgel"), g_i - 1)
            plan.inc("cepi", v.scalar_tensor_tensor(
                hs4[:, gp * 512:(gp + 1) * 512],
                xn[:, (4 * g_i % 8) * 128:((4 * g_i % 8) + 4) * 128], dcol[:],
                psA[:, 512 * (6 + gp):512 * (7 + gp)], op0=ALU.mult, op1=ALU.add))

    @block.tensor
    def _(te):
        te.wait_ge(plan.sem("cl"), plan.val("cl"))
        plan.inc("cdp", te.matmul(psA[0:128, 0:1], s["ones32r"][0:1, :],
                                  s["p_Dv"][0:1, 0:1], start=True, stop=True,
                                  skip_group_check=True))
        for g_i in range(NG):
            gp = g_i % 2
            for k in range(4):
                sq = 4 * g_i + k
                te.wait_ge(plan.sem("cln"), sq // 4 + 1)
                if sq >= 2:
                    te.wait_ge(plan.sem("ctw"), 3 * (sq // 2))
                if sq == 0:
                    te.wait_ge(plan.sem("cdv"), 1)
                te.ldweights(xn[:, (sq % 8) * 128:(sq % 8 + 1) * 128])
                plan.inc("cpe_s1", te.matmul(psA[:, 512 * (sq % 2):512 * (sq % 2) + 258],
                                             xn[:, (sq % 8) * 128:(sq % 8 + 1) * 128],
                                             s["f1"][:], start=True, stop=True,
                                             skip_group_check=True))
            for half in range(2):
                te.wait_ge(plan.sem("ctw"), 3 * (2 * g_i + half + 1))
                for nm, rhs_lo, rhs_hi, out_lo, acc in [
                        ("f2re", 0, 258, 0, "start"),
                        ("f2imn", 129, 258, 0, "mid"),
                        ("f2im", 0, 129, 129, "stop")]:
                    te.ldweights(s[nm][:])
                    for k in range(2 * half, 2 * half + 2):
                        sq = 4 * g_i + k
                        if nm == "f2re" and sq >= 2:
                            te.wait_ge(plan.sem("cpw"), sq // 2)
                        slot = (sq % 8) * 258
                        cveiw = c4[:, slot + rhs_lo: slot + rhs_hi]
                        _mm = te.matmul(psA[:, 512 * (2 + sq % 2) + out_lo:
                                            512 * (2 + sq % 2) + out_lo + (rhs_hi - rhs_lo)],
                                 s[nm][:], cveiw,
                                 start=(nm == "f2re"), stop=(nm == "f2im"),
                                 skip_group_check=True)
                        if nm == "f2im":
                            plan.inc("cpe_s2", _mm)
            for k in range(4):
                sq = 4 * g_i + k
                te.wait_ge(plan.sem("cpw"), 2 * g_i + k // 2 + 1)
                if sq >= 2:
                    te.wait_ge(plan.sem("cet"), sq - 1)
                slot = (sq % 8) * 258
                pts = psA[:, 512 * (4 + sq % 2):512 * (5 + sq % 2)]
                te.ldweights(y4[:, slot:slot + 128])
                te.matmul(pts[:, 0:256], y4[:, slot:slot + 128],
                          s["ra"][:], start=True, stop=False, skip_group_check=True)
                te.ldweights(y4[:, slot + 129:slot + 257])
                te.matmul(pts[:, 0:256], y4[:, slot + 129:slot + 257],
                          s["rb"][:], start=False, stop=True, skip_group_check=True)
                te.ldweights(y4[:, slot + 128:slot + 129])
                te.matmul(pts[0:1, 256:512], y4[:, slot + 128:slot + 129],
                          s["rae"][:], start=True, stop=False, skip_group_check=True)
                te.ldweights(y4[:, slot + 257:slot + 258])
                plan.inc("cpe_is2", te.matmul(pts[0:1, 256:512], y4[:, slot + 257:slot + 258],
                                              s["rbe"][:], start=False, stop=True,
                                              skip_group_check=True))
            te.wait_ge(plan.sem("citw"), g_i + 1)
            if g_i >= 2:
                te.wait_ge(plan.sem("cepi"), g_i - 1)
            hb = psA[:, 512 * (6 + gp):512 * (7 + gp)]
            gpp = g_i % 2
            for nm, src_off, whalf in [("g1re", 0, 0), ("g1imn", 128, 0),
                                       ("g1er", 0, 1), ("g1ein", 128, 1)]:
                te.ldweights(s[nm][:])
                for k in range(4):
                    sq = 4 * g_i + k
                    if whalf == 0:
                        rhsv = pp4[:, gpp * 1024 + (sq % 4) * 256 + src_off:
                                   gpp * 1024 + (sq % 4) * 256 + src_off + 128]
                    else:
                        rhsv = pt4[0:1, gpp * 2048 + (sq % 4) * 512 + 256 + src_off:
                                   gpp * 2048 + (sq % 4) * 512 + 256 + src_off + 128]
                    _mm = te.matmul(hb[:, (sq % 4) * 128:(sq % 4 + 1) * 128],
                                    s[nm][:], rhsv,
                                    start=(nm == "g1re"), stop=(nm == "g1ein"),
                                    skip_group_check=True)
                    if nm == "g1ein" and k == 3:
                        plan.inc("cpe_is1", _mm)

    @block.scalar
    def _(sc):
        for g_i in range(NG):
            gp = g_i % 2
            for k in range(4):
                sq = 4 * g_i + k
                sc.wait_ge(plan.sem("cpe_is2"), sq + 1)
                if sq >= 8:
                    sc.wait_ge(plan.sem("cpe_is1"), (sq - 8) // 4 + 1)
                plan.inc("cet", sc.copy(pt4[:, (sq % 8) * 512:(sq % 8 + 1) * 512],
                                        psA[:, 512 * (4 + sq % 2):512 * (5 + sq % 2)]))
            sc.wait_ge(plan.sem("cepi"), g_i + 1)
            if g_i >= 2:
                sc.wait_ge(plan.sem("cst%d" % (g_i % 2)), 64 * (g_i // 2))
            ng = plan.inc("cgel", sc.activation(sm4[:, gp * 512:(gp + 1) * 512],
                                                hs4[:, gp * 512:(gp + 1) * 512], AF.Gelu))
            sc.wait_ge(plan.sem("cgel"), ng)
            for k in range(4):
                sq = 4 * g_i + k
                chl, b = sq // 2, sq % 2
                dst = pr.d["ssm_agin%d" % (chl // 32)][chl % 32,
                    b * L:(b + 1) * L].rearrange("(a c) -> a c", c=128)
                plan.inc("cst%d" % (g_i % 2), sc.dma_start(
                    out=dst, in_=sm4[:, gp * 512 + k * 128:gp * 512 + (k + 1) * 128]), 16)

    @block.gpsimd
    def _(g):
        ag_final(pr, g)


def _shard_ssm_inputs(core, x, prm):
    c = consts()
    hc = slice(CPC * core, CPC * (core + 1))
    xf = np.ascontiguousarray(x.reshape(T, HD))
    m = {
        "x_full": xf,
        "x_ts": np.ascontiguousarray(xf[TPC * core:TPC * (core + 1)]),
        "x_ch": np.ascontiguousarray(x[:, :, hc].transpose(2, 0, 1)),
        "B_re": prm["B_re"].reshape(NMODE, 1), "B_im": prm["B_im"].reshape(NMODE, 1),
        "P_re": prm["P_re"].reshape(NMODE, 1), "P_im": prm["P_im"].reshape(NMODE, 1),
        "Q_re": prm["Q_re"].reshape(NMODE, 1), "Q_im": prm["Q_im"].reshape(NMODE, 1),
        "diag_re": prm["diag_re"].reshape(NMODE, 1),
        "diag_im": prm["diag_im"].reshape(NMODE, 1),
        "step": prm["step"].reshape(1, 1), "Dv": prm["D"].reshape(1, 1),
        "C_re": np.ascontiguousarray(prm["C_re"][hc]),
        "C_im": np.ascontiguousarray(prm["C_im"][hc]),
        "w_re": c["w_re"][2048 * core:2048 * (core + 1)].reshape(1, 2048),
        "w_im": c["w_im"][2048 * core:2048 * (core + 1)].reshape(1, 2048),
        "tz_re": c["tz_re"][2048 * core:2048 * (core + 1)].reshape(1, 2048),
        "tz_im": c["tz_im"][2048 * core:2048 * (core + 1)].reshape(1, 2048),
    }
    cm = np.zeros((128, CPC), np.float32)
    for r in range(8):
        for j in range(8):
            cm[16 * r + j] = prm["C_re"][hc][:, 8 * r + j]
            cm[16 * r + 8 + j] = prm["C_im"][hc][:, 8 * r + j]
    m["cmix_pre"] = cm
    gi = np.zeros((128, 16), np.uint32)
    p = np.arange(128)
    for j in range(EPC):
        base = (p // 16) * 2048 + (8 * core + j) * 16 + (p % 16)
        gi[:, 2 * j] = base
        gi[:, 2 * j + 1] = base + 1024
    m["gidx"] = gi
    for nm in CONST_NAMES:
        arr = c[nm]
        m["c_" + nm] = np.asarray(arr)
    return m


def _build_ssm_test():
    import contextlib
    nc = bass.Bass("TRN2", target_bir_lowering=False)
    stack = contextlib.ExitStack()
    nc._test_stack = stack
    pr = Prog(nc, stack, 0)
    declare_conv_io(pr)
    out = pr.dram("ssm_out", [CPC, T], BF16, kind="ExternalOutput")
    agpeek = pr.dram("agpeek", [128, 128], BF16, kind="ExternalOutput")
    with nc.Block() as block:
        plan = pr.plan

        @block.gpsimd
        def _(g):
            conv_const_loads(pr, g)

        pr.push()
        emit_stats(pr, block)
        st_fence = [("sv", plan.val("sv")), ("ss2", plan.val("ss2")),
                    ("sve", plan.val("sve")), ("sgi", 16)]
        pr.pop()
        pr.push()
        _phase_wait(block, plan, pr, st_fence, [])
        pr.sbuf("kb_gidx", [128, 16], U32)
        pr.push()
        emit_kg(pr, block)
        pr.pop()
        emit_basis(pr, block)
        emit_cmix(pr, block)
        cm_fence = [("cmp", plan.val("cmp")), ("cme", plan.val("cme")),
                    ("cmw0", plan.val("cmw0")), ("cmw1", plan.val("cmw1")),
                    ("cmw2", plan.val("cmw2")), ("cmw3", plan.val("cmw3")),
                    ("kbe", plan.val("kbe")), ("kbe2", plan.val("kbe2")),
                    ("cmv", 1)]
        pr.pop()
        pr.push()
        _phase_wait(block, plan, pr, cm_fence, [])
        emit_conv(pr, block)
        pr.pop()

        @block.gpsimd
        def _(g):
            g.wait_ge(plan.sem("cc_ssm"), 1)
            dma_in(plan, g, out, pr.d["ssm_agin"], "fin")
            dma_in(plan, g, agpeek, pr.d["ssm_agout"][0:128, 0:128], "fin")
            g.wait_ge(plan.sem("fin"), 32)
    stack.close()
    return nc


def ssm_test_kernel(x, prm):
    nc = _build_ssm_test()
    in_maps = [_shard_ssm_inputs(cc, x, prm) for cc in range(NCORES)]
    res = run_bass_kernel_spmd(nc, in_maps, core_ids=list(range(NCORES)))
    return res


def declare_ffn_io(pr):
    pr.dram("gate_w", [HD, E], F32, kind="ExternalInput")
    pr.dram("shared_gate_w", [HD, 1], F32, kind="ExternalInput")
    pr.dram("sguw", [HD, 2 * ISH], F32, kind="ExternalInput")
    pr.dram("sdw", [ISH, HD], F32, kind="ExternalInput")
    pr.dram("egu_c", [EPC, HD, 2 * IEX], F32, kind="ExternalInput")
    pr.dram("edw_c", [EPC, IEX, HD], F32, kind="ExternalInput")
    pr.dram("rank_off", [1, 1], F32, kind="ExternalInput")
    pr.dram("tri_rk", [8, 1], F32, kind="ExternalInput")
    pr.dram("rank_i", [1, 1], I32, kind="ExternalInput")
    pr.dram("xnfm", [HD, TPC], BF16)
    pr.dram("idwn", [E * CAP, 4], F32)
    pr.dram("idwn_rs", [EPC * CAP, 4], F32)
    pr.dram("cnt_agin", [64, 1], F32)
    pr.dram("cnt_agout", [512, 1], F32)
    pr.dram("moe_out", [EPC * CAP, HD], BF16)
    pr.dram("moe_all", [E * CAP, HD], BF16)
    pr.dram("ssm_my", [HD, TPC], BF16)


def emit_routing(pr, block):
    plan = pr.plan
    s = pr.s
    nt = TPC // 128

    xts = pr.sbuf("rt_x", [128, 2 * HD], F32)
    xnb = pr.sbuf("rt_xn", [128, HD], BF16)
    fmst = pr.sbuf("rt_fm", [128, 2 * HD], BF16)
    lgt = pr.sbuf("rt_lg", [128, 64], F32)
    masked = pr.sbuf("rt_msk", [128, 64], F32)
    chos = pr.sbuf("rt_ch", [128, 64], F32)
    top8v = pr.sbuf("rt_t8v", [128, 8], F32)
    top8i = pr.sbuf("rt_t8i", [128, 8], U32)
    exp8 = pr.sbuf("rt_e8", [128, 8], F32)
    scr = pr.sbuf("rt_scr", [128, 8], F32)
    idx_all = pr.sbuf("rt_idxa", [128, 8 * nt], F32)
    wn_all = pr.sbuf("rt_wna", [128, 8 * nt], F32)
    chT = pr.sbuf("rt_chT", [64, TPC], F32)
    zt64 = pr.sbuf("rt_z64", [64, TPC], BF16)
    gfull = pr.sbuf("rt_gf", [64, TPC], F32)
    gtm = pr.sbuf("rt_gtm", [128, 64], F32)
    gsel = pr.sbuf("rt_gsel", [128, 8], F32)
    dstf = pr.sbuf("rt_dstf", [128, 8], F32)
    intl = pr.sbuf("rt_intl", [128, 64], F32)
    chos8 = pr.sbuf("rt_ch8", [128, 512], F32)
    msk8 = pr.sbuf("rt_mk8", [128, 512], F32)
    iota64 = pr.sbuf("rt_iota64", [128, 64], F32)
    iotap = pr.sbuf("rt_iotap", [128, 1], F32)
    rko = pr.sbuf("rt_rko", [128, 1], F32)
    cnt8 = pr.sbuf("rt_cnt8", [8, 64], F32)
    cntc = pr.sbuf("rt_cntc", [64, 1], F32)
    pcol = pr.sbuf("rt_pcol", [64, 1], F32)
    gwb = pr.sbuf("rt_gw", [128, 8 * 64], BF16)
    sgwb = pr.sbuf("rt_sgw", [128, 8], BF16)
    ztile = pr.sbuf("rt_ztile", [128, 1024], F32)
    rof = pr.sbuf("rt_rof", [1, 1], F32)
    trk = pr.sbuf("rt_trk", [8, 1], F32)
    pbf = pr.psum("rt_pbf", [128, 512], BF16)
    pbf2 = pr.psum("rt_pbf2", [128, 512], BF16)
    psf = pr.psum("rt_psf", [128, 512], F32)
    psg = pr.psum("rt_psg", [128, 64], F32)
    dsti_all = pr.s["dsti_all"]
    sg_all = pr.s["sg_all"]
    VS = "vself"

    @block.vector
    def _(v):
        def vop(instr, k=1):
            n = plan.inc(VS, instr, k)
            v.wait_ge(plan.sem(VS), n)
            return n
        v.wait_ge(plan.sem("ss2"), nt)
        vop(v.memset(ztile[:], 0.0))
        n = vop(v.memset(zt64[:], 0.0))
        pr.marks["rt_zt"] = n
        v.wait_ge(plan.sem("rgp"), 2)
        v.wait_ge(plan.sem("rt_rko_p"), 1)
        vop(v.tensor_copy(rko[:], psf[0:128, 0:1]))
        plan.inc("rt_rko_done", v.nop())
        for i in range(nt):
            v.wait_ge(plan.sem("rxl%d" % (i % 2)), 16 * (i // 2 + 1))
            n = vop(v.tensor_scalar(xnb[:], xts[:, (i % 2) * HD:(i % 2 + 1) * HD],
                                    s["st_out"][:, 2 * i:2 * i + 1],
                                    s["st_out"][:, 2 * i + 1:2 * i + 2],
                                    op0=ALU.subtract, op1=ALU.mult))
            pr.marks["rt_ln_%d" % i] = n
            v.wait_ge(plan.sem("rt_lgev"), i + 1)
            vop(v.max(out=top8v[:], in_=lgt[:]))
            vop(v.max_index(top8i[:], top8v[:], lgt[:]))
            plan.inc(VS, v.tensor_copy(idx_all[:, 8 * i:8 * i + 8], top8i[:]))
            plan.inc(VS, v.tensor_copy(masked[:], lgt[:]))
            n = plan.inc(VS, v.tensor_scalar(scr[:, 3:4], top8v[:, 0:1], -1.0, None,
                                             op0=ALU.mult))
            v.wait_ge(plan.sem(VS), n)
            pr.marks["rt_nm_%d" % i] = n
            vop(v.match_replace(masked[:], in_to_replace=top8v[:], in_values=masked[:],
                                imm_value=-3.0e38))
            v.wait_ge(plan.sem("rt_exp"), i + 1)
            plan.inc(VS, v.reduce_sum(scr[:, 4:5], exp8[:], axis=mybir.AxisListType.X))
            n = plan.inc(VS, v.tensor_tensor(chos[:], lgt[:], masked[:],
                                             op=ALU.not_equal))
            v.wait_ge(plan.sem(VS), n)
            pr.marks["rt_ch_%d" % i] = n
            vop(v.reciprocal(scr[:, 5:6], scr[:, 4:5]))
            vop(v.tensor_scalar(wn_all[:, 8 * i:8 * i + 8], exp8[:], scr[:, 5:6],
                                None, op0=ALU.mult))
            v.wait_ge(plan.sem("rt_chT"), i + 1)
            vop(v.tensor_copy(chT[:, 128 * i:128 * (i + 1)], psf[0:64, 0:128]))
            plan.inc("rt_chTev", v.nop())
        pr.marks["rt_sga"] = plan.val(VS)
        vop(v.tensor_tensor_scan(gfull[:], zt64[:], chT[:], 0.0,
                                 op0=ALU.add, op1=ALU.add))
        n = vop(v.tensor_copy(cntc[:], gfull[:, TPC - 1:TPC]))
        pr.marks["rt_cnt"] = n
        v.wait_ge(plan.sem("rt_pc"), 1)
        vop(v.tensor_copy(pcol[:], psf[0:64, 300:301]))
        vop(v.tensor_scalar(gfull[:], gfull[:], pcol[:], None, op0=ALU.add))
        n = vop(v.tensor_tensor(gfull[:], gfull[:], chT[:], op=ALU.subtract))
        pr.marks["rt_gf"] = n
        for i in range(nt):
            v.wait_ge(plan.sem("rt_gtm"), i + 1)
            if i >= 2:
                v.wait_ge(plan.sem("rsc%s" % ("" if i % 2 == 0 else "1")),
                          16 * TOPK * (i // 2))
            sl = (i % 2) * 32
            iv = intl[:, sl:sl + 32].rearrange("p (k f) -> p f k", f=4)
            plan.inc(VS, v.tensor_copy(gtm[:], psf[0:128, 128:192]))
            for k in range(TOPK):
                plan.inc(VS, v.tensor_scalar(chos8[:, 64 * k:64 * (k + 1)], iota64[:],
                                             idx_all[:, 8 * i + k:8 * i + k + 1], None,
                                             op0=ALU.is_equal))
            plan.inc(VS, v.tensor_scalar(dstf[:], idx_all[:, 8 * i:8 * i + 8], 512.0,
                                         None, op0=ALU.mult))
            plan.inc(VS, v.tensor_scalar(scr[:, 0:1], iotap[:], rko[:],
                                         float(128 * i), op0=ALU.add, op1=ALU.add))
            plan.inc(VS, v.tensor_copy(iv[:, 1, :], wn_all[:, 8 * i:8 * i + 8]))
            plan.inc(VS, v.tensor_copy(iv[:, 2, :],
                     s["st_out"][:, 2 * i:2 * i + 1].to_broadcast([128, 8])))
            nA = plan.inc(VS, v.tensor_copy(iv[:, 3, :],
                          s["st_out"][:, 2 * i + 1:2 * i + 2].to_broadcast([128, 8])))
            v.wait_ge(plan.sem(VS), nA)
            plan.inc("rt_gtmev", v.nop())
            for k in range(TOPK):
                plan.inc(VS, v.tensor_tensor(msk8[:, 64 * k:64 * (k + 1)], gtm[:],
                                             chos8[:, 64 * k:64 * (k + 1)], op=ALU.mult))
            nB = plan.inc(VS, v.tensor_copy(iv[:, 0, :],
                                            scr[:, 0:1].to_broadcast([128, 8])))
            v.wait_ge(plan.sem(VS), nB)
            for k in range(TOPK):
                plan.inc(VS, v.reduce_sum(gsel[:, k:k + 1],
                                          msk8[:, 64 * k:64 * (k + 1)],
                                          axis=mybir.AxisListType.X))
            nC = plan.val(VS)
            v.wait_ge(plan.sem(VS), nC)
            plan.inc(VS, v.tensor_tensor(dstf[:], dstf[:], gsel[:], op=ALU.add))
            nD = plan.inc(VS, v.tensor_scalar(scr[:, 0:8], gsel[:], float(CAP),
                                              40000.0, op0=ALU.is_ge, op1=ALU.mult))
            v.wait_ge(plan.sem(VS), nD)
            nE = plan.inc(VS, v.tensor_tensor(dsti_all[:, 8 * i:8 * i + 8], dstf[:],
                                              scr[:, 0:8], op=ALU.add))
            v.wait_ge(plan.sem(VS), nE)
            pr.marks["rt_intl_%d" % i] = nE
        pr.marks["rt_v_fin"] = plan.val(VS)


    @block.scalar
    def _(sc):
        for i in range(nt):
            for h in range(2):
                sc.wait_ge(plan.sem("rt_tp"), 8 * i + 4 * (h + 1))
                if i >= 2:
                    sc.wait_ge(plan.sem("rfw%d" % (i % 2)), 128 * (i // 2))
                pb = pbf if h == 0 else pbf2
                plan.inc("rt_fmev", sc.copy(
                    fmst[:, (i % 2) * HD + 512 * h:(i % 2) * HD + 512 * (h + 1)],
                    pb[:, 0:512]))
            sc.wait_ge(plan.sem("rt_lg"), i + 1)
            plan.inc("rt_lgev", sc.copy(lgt[:], psg[:, 0:64]))
            sc.wait_ge(plan.sem("rt_sg"), i + 1)
            plan.inc("rt_sgev", sc.copy(sg_all[:, i:i + 1], psf[:, 256:257]))
            sc.wait_ge(plan.sem(VS), pr.marks["rt_nm_%d" % i])
            plan.inc("rt_exp", sc.activation(exp8[:], top8v[:], AF.Exp,
                                             bias=scr[:, 3:4]))
        sc.wait_ge(plan.sem(VS), pr.marks["rt_sga"])
        plan.inc("rt_sig", sc.activation(sg_all[:], sg_all[:], AF.Sigmoid))

    @block.tensor
    def _(te):
        te.wait_ge(plan.sem("cl"), plan.val("cl"))
        te.wait_ge(plan.sem("rgl"), 64)
        plan.inc("rt_rko_p", te.matmul(psf[0:128, 0:1], s["ones32r"][0:1, :],
                                       rof[0:1, 0:1], start=True, stop=True,
                                       skip_group_check=True))
        for i in range(nt):
            te.wait_ge(plan.sem(VS), pr.marks["rt_ln_%d" % i])
            for cch in range(8):
                h = cch // 4
                if cch % 4 == 0 and i >= 1:
                    te.wait_ge(plan.sem("rt_fmev"), 2 * (i - 1) + h + 1)
                pb = pbf if h == 0 else pbf2
                plan.inc("rt_tp", te.transpose(
                    pb[:, 128 * (cch % 4):128 * (cch % 4 + 1)],
                    xnb[:, 128 * cch:128 * (cch + 1)], s["ident"][:]))
            te.wait_ge(plan.sem("rt_fmev"), 2 * (i + 1))
            if i >= 1:
                te.wait_ge(plan.sem("rt_lgev"), i)
                te.wait_ge(plan.sem("rt_sgev"), i)
            for cch in range(8):
                lhs = fmst[:, (i % 2) * HD + 128 * cch:(i % 2) * HD + 128 * (cch + 1)]
                te.ldweights(lhs)
                _m1 = te.matmul(psg[:, 0:64], lhs, gwb[:, 64 * cch:64 * (cch + 1)],
                                start=(cch == 0), stop=(cch == 7),
                                skip_group_check=True)
                _m2 = te.matmul(psf[:, 256:257], lhs, sgwb[:, cch:cch + 1],
                                start=(cch == 0), stop=(cch == 7),
                                skip_group_check=True)
                if cch == 7:
                    plan.inc("rt_lg", _m1)
                    plan.inc("rt_sg", _m2)
            te.wait_ge(plan.sem(VS), pr.marks["rt_ch_%d" % i])
            if i == 0:
                te.wait_ge(plan.sem("rt_rko_done"), 1)
            if i >= 1:
                te.wait_ge(plan.sem("rt_chTev"), i)
            plan.inc("rt_chT", te.transpose(psf[0:64, 0:128], chos[:], s["ident32"][:]))
        te.wait_ge(plan.sem("rcl"), 16)
        plan.inc("rt_pc", te.matmul(psf[0:64, 300:301], cnt8[:], trk[:],
                                    start=True, stop=True, skip_group_check=True))
        te.wait_ge(plan.sem(VS), pr.marks["rt_gf"])
        for i in range(nt):
            if i >= 1:
                te.wait_ge(plan.sem("rt_gtmev"), i)
            plan.inc("rt_gtm", te.transpose(psf[0:128, 128:192],
                                            gfull[:, 128 * i:128 * (i + 1)],
                                            s["ident32"][0:64, 0:64]))

    @block.sync
    def _(sy):
        sy.wait_ge(plan.sem(VS), pr.marks["rt_zt"])
        plan.inc("rzw", sy.dma_start(
            out=pr.d["idwn"].rearrange("(a b) c -> a (b c)", a=128), in_=ztile[:]), 16)
        for i in range(nt):
            if i >= 2:
                sy.wait_ge(plan.sem(VS), pr.marks["rt_ln_%d" % (i - 2)])
            dma_in(plan, sy, xts[:, (i % 2) * HD:(i % 2 + 1) * HD],
                   pr.d["x_ts"][128 * i:128 * (i + 1), :], "rxl%d" % (i % 2))
            sy.wait_ge(plan.sem("rt_fmev"), 2 * (i + 1))
            fv = pr.d["xnfm"].rearrange("(c p) t -> c p t", p=128)
            for cch in range(8):
                plan.inc("rfw%d" % (i % 2), sy.dma_start(
                    out=fv[cch, :, 128 * i:128 * (i + 1)],
                    in_=fmst[:, (i % 2) * HD + 128 * cch:
                             (i % 2) * HD + 128 * (cch + 1)]), 16)
        sy.wait_ge(plan.sem(VS), pr.marks["rt_cnt"])
        dma_in(plan, sy, pr.d["cnt_agin"], cntc[:], "rcw")
        sy.wait_ge(plan.sem("cc_cnt"), 1)
        dma_in(plan, sy, cnt8[:],
               pr.d["cnt_agout"].rearrange("(r e) a -> r (e a)", e=64), "rcl")

    @block.gpsimd
    def _(g):
        g.wait_ge(plan.sem("ss2"), nt)
        plan.inc("rgp", g.iota(iota64[:], pattern=[[1, 64]], base=0,
                               channel_multiplier=0,
                               allow_small_or_imprecise_dtypes=True))
        plan.inc("rgp", g.iota(iotap[:], pattern=[[1, 1]], base=0,
                               channel_multiplier=1,
                               allow_small_or_imprecise_dtypes=True))
        dma_in(plan, g, gwb[:].rearrange("p (c e) -> p c e", e=64),
               pr.d["gate_w"].rearrange("(c p) e -> p c e", p=128), "rgl")
        plan.inc("rgl", g.dma_start(
            out=sgwb[:].rearrange("p (c e) -> p c e", e=1),
            in_=pr.d["shared_gate_w"].rearrange("(c p) e -> p c e", p=128),
            allow_slow_non_contiguous=True), 16)
        dma_in(plan, g, rof[:], pr.d["rank_off"], "rgl")
        dma_in(plan, g, trk[:], pr.d["tri_rk"], "rgl")
        g.wait_ge(plan.sem("rcw"), 16)
        plan.inc("cc_cnt", g.collective_compute(
            "AllGather", ALU.bypass, replica_groups=[list(range(NCORES))],
            ins=[pr.d["cnt_agin"].opt()], outs=[pr.d["cnt_agout"].opt()]))
        g.wait_ge(plan.sem("rzw"), 16)
        for i in range(nt):
            g.wait_ge(plan.sem(VS), pr.marks["rt_intl_%d" % i])
            for k in range(TOPK):
                plan.inc("rsc%s" % ("" if i % 2 == 0 else "1"), g.indirect_dma_start(
                    out=pr.d["idwn"],
                    out_offset=bass.IndirectOffsetOnAxis(
                        ap=dsti_all[:, 8 * i + k:8 * i + k + 1], axis=0),
                    in_=intl[:, (i % 2) * 32 + 4 * k:(i % 2) * 32 + 4 * k + 4],
                    in_offset=None,
                    bounds_check=pr.bnd32767, oob_is_err=False), 16)
        g.wait_ge(plan.sem("rsc"), 16 * TOPK * (nt // 2))
        g.wait_ge(plan.sem("rsc1"), 16 * TOPK * (nt // 2))
        plan.inc("cc_rs", g.collective_compute(
            "ReduceScatter", ALU.add, replica_groups=[list(range(NCORES))],
            ins=[pr.d["idwn"].opt()], outs=[pr.d["idwn_rs"].opt()]))

def emit_moe(pr, block):
    plan = pr.plan
    s = pr.s

    it4 = pr.sbuf("mo_it", [128, 16], F32)
    nmr = pr.sbuf("mo_nmr", [128, 8], F32)
    idxu = pr.sbuf("mo_idxu", [128, 4], U32)
    xrow = pr.sbuf("mo_xrow", [128, 1024], BF16)
    xnr = pr.sbuf("mo_xnr", [128, 1024], BF16)
    xnT = pr.sbuf("mo_xnT", [128, 8 * 512], BF16)
    wegu = pr.sbuf("mo_wegu", [128, 8 * 1024], BF16)
    wedw = pr.sbuf("mo_wedw", [128, 4 * 1024], BF16)
    sil = pr.sbuf("mo_sil", [128, 4 * 512], BF16)
    actT = pr.sbuf("mo_act", [128, 4 * 512], BF16)
    outb = pr.sbuf("mo_out", [128, 1024], BF16)
    pbf = pr.psum("mo_pbf", [128, 128], BF16)
    pbf2 = pr.psum("mo_pbf2", [128, 128], BF16)
    psg1 = pr.psum("mo_ps1", [128, 512], F32)
    psg2 = pr.psum("mo_ps2", [128, 512], F32)
    psd = pr.psum("mo_psd", [128, 1024], F32)
    VS = "vself"

    @block.vector
    def _(v):
        def vop(instr, k=1):
            n = plan.inc(VS, instr, k)
            v.wait_ge(plan.sem(VS), n)
            return n
        for e in range(EPC):
            for q in range(4):
                v.wait_ge(plan.sem("mil%d" % (q % 2)), 16 * ((4 * e + q) // 2 + 1))
                vop(v.tensor_scalar(nmr[:, 0:1], it4[:, 4 * q:4 * q + 1], 0.0, None,
                                    op0=ALU.is_equal))
                vop(v.tensor_scalar(nmr[:, 1:2], it4[:, 4 * q:4 * q + 1], -1.0, None,
                                    op0=ALU.add))
                vop(v.scalar_tensor_tensor(nmr[:, 1:2], nmr[:, 0:1], 50000.0,
                                           nmr[:, 1:2], op0=ALU.mult, op1=ALU.add))
                n = vop(v.tensor_copy(idxu[:, q:q + 1], nmr[:, 1:2]))
                n = vop(v.memset(xrow[:], 0.0))
                pr.marks["mo_gw_%d_%d" % (e, q)] = n
                v.wait_ge(plan.sem("mgl"), 16 * (4 * e + q + 1))
                vop(v.tensor_tensor(nmr[:, 2:3], it4[:, 4 * q + 2:4 * q + 3],
                                    it4[:, 4 * q + 3:4 * q + 4], op=ALU.mult))
                vop(v.tensor_scalar(nmr[:, 3:4], nmr[:, 2:3], -1.0, None, op0=ALU.mult))
                n = vop(v.tensor_scalar(xnr[:], xrow[:],
                                        it4[:, 4 * q + 3:4 * q + 4],
                                        nmr[:, 3:4], op0=ALU.mult, op1=ALU.add))
                pr.marks["mo_ln_%d_%d" % (e, q)] = n
                for cch in range(8):
                    v.wait_ge(plan.sem("mo_tp"), (4 * e + q) * 8 + cch + 1)
                    pb = pbf if cch % 2 == 0 else pbf2
                    n = vop(v.tensor_copy(
                        xnT[:, 512 * cch + 128 * q:512 * cch + 128 * (q + 1)],
                        pb[:, 0:128]))
                    plan.inc("mo_tpev", v.nop())
                pr.marks["mo_xnT_%d_%d" % (e, q)] = plan.val(VS)
            for ip in range(4):
                v.wait_ge(plan.sem("mo_ev"), 8 * e + ip + 5)
                n = vop(v.scalar_tensor_tensor(
                    actT[:, 512 * ip:512 * (ip + 1)],
                    actT[:, 512 * ip:512 * (ip + 1)], 1.0,
                    sil[:, 512 * ip:512 * (ip + 1)], op0=ALU.mult, op1=ALU.mult))
                pr.marks["mo_act_%d_%d" % (e, ip)] = n
            for q in range(4):
                v.wait_ge(plan.sem("mo_dn"), 4 * e + q + 1)
                v.wait_ge(plan.sem("mow"), 16 * (4 * e + q))
                n = vop(v.tensor_scalar(outb[:], psd[:, 0:1024],
                                        it4[:, 4 * q + 1:4 * q + 2], None, op0=ALU.mult))
                plan.inc("mo_dnev", v.nop())
                pr.marks["mo_out_%d_%d" % (e, q)] = n
            for q in range(4):
                pr.marks["mo_xn_%d_%d" % (e + 1, q)] = plan.val(VS)
                pr.marks["mo_idx_%d_%d" % (e + 1, q)] = plan.val(VS)
        pr.marks["mo_v_fin"] = plan.val(VS)

    @block.tensor
    def _(te):
        nmm = 0
        for e in range(EPC):
            for q in range(4):
                te.wait_ge(plan.sem(VS), pr.marks["mo_ln_%d_%d" % (e, q)])
                for cch in range(8):
                    if (4 * e + q) * 8 + cch >= 2:
                        te.wait_ge(plan.sem("mo_tpev"), (4 * e + q) * 8 + cch - 1)
                    pb = pbf if cch % 2 == 0 else pbf2
                    plan.inc("mo_tp", te.transpose(
                        pb[:, 0:128], xnr[:, 128 * cch:128 * (cch + 1)], s["ident"][:]))
            te.wait_ge(plan.sem("mwl%d" % (e % 2)), 16 * 12 * (e // 2 + 1))
            te.wait_ge(plan.sem(VS), pr.marks["mo_xnT_%d_3" % e])
            for fc in range(8):
                psx = psg1 if fc % 2 == 0 else psg2
                if fc >= 2:
                    te.wait_ge(plan.sem("mo_ev"), 8 * e + fc - 1)
                for cc in range(8):
                    lhs = wegu[:, 1024 * cc + 128 * fc:1024 * cc + 128 * (fc + 1)]
                    te.ldweights(lhs)
                    plan.inc("mo_pe", te.matmul(psx[:, 0:512], lhs, xnT[:, 512 * cc:512 * (cc + 1)],
                                                start=(cc == 0), stop=(cc == 7),
                                                skip_group_check=True))
            for q in range(4):
                if q >= 1:
                    te.wait_ge(plan.sem("mo_dnev"), 4 * e + q)
                elif e >= 1:
                    te.wait_ge(plan.sem("mo_dnev"), 4 * e)
                for ic in range(4):
                    te.wait_ge(plan.sem(VS), pr.marks["mo_act_%d_%d" % (e, ic)])
                    lhs = actT[:, 512 * ic + 128 * q:512 * ic + 128 * (q + 1)]
                    te.ldweights(lhs)
                    for hh in range(2):
                        _mm = te.matmul(psd[:, 512 * hh:512 * (hh + 1)], lhs,
                                        wedw[:, 1024 * ic + 512 * hh:1024 * ic + 512 * (hh + 1)],
                                        start=(ic == 0), stop=(ic == 3),
                                        skip_group_check=True)
                    if ic == 3:
                        plan.inc("mo_dn", _mm)
        pr.marks["mo_t_fin"] = plan.val("mo_pe")

    @block.scalar
    def _(sc):
        for e in range(EPC):
            for fc in range(8):
                sc.wait_ge(plan.sem("mo_pe"), 64 * e + 8 * (fc + 1))
                psx = psg1 if fc % 2 == 0 else psg2
                if fc < 4:
                    if e >= 1:
                        sc.wait_ge(plan.sem(VS), pr.marks["mo_act_%d_%d" % (e - 1, fc)])
                    plan.inc("mo_ev", sc.activation(sil[:, 512 * fc:512 * (fc + 1)],
                                                    psx[:, 0:512], AF.Silu))
                else:
                    ip = fc - 4
                    if e >= 1:
                        sc.wait_ge(plan.sem("mo_dn"), 4 * e)
                    plan.inc("mo_ev", sc.copy(actT[:, 512 * ip:512 * (ip + 1)],
                                              psx[:, 0:512]))
            for q in range(4):
                sc.wait_ge(plan.sem(VS), pr.marks["mo_out_%d_%d" % (e, q)])
                plan.inc("mow", sc.dma_start(
                    out=pr.d["moe_out"][e * CAP + q * 128:e * CAP + (q + 1) * 128, :],
                    in_=outb[:]), 16)

    @block.gpsimd
    def _(g):
        g.wait_ge(plan.sem("cc_rs"), 1)
        for e in range(EPC):
            if e > 0:
                g.wait_ge(plan.sem("mo_pe"), 64 * e)
                g.wait_ge(plan.sem("mo_dn"), 4 * e)
            for cc in range(8):
                dma_in(plan, g, wegu[:, 1024 * cc:1024 * (cc + 1)],
                       pr.d["egu_c"][e].rearrange("(c p) f -> c p f", p=128)[cc],
                       "mwl%d" % (e % 2))
            for ic in range(4):
                dma_in(plan, g, wedw[:, 1024 * ic:1024 * (ic + 1)],
                       pr.d["edw_c"][e].rearrange("(c p) f -> c p f", p=128)[ic],
                       "mwl%d" % (e % 2))
            for q in range(4):
                st = e * CAP + q * 128
                g.wait_ge(plan.sem(VS), pr.marks.get("mo_xn_%d_%d" % (e, q), 0))
                dma_in(plan, g, it4[:, 4 * q:4 * (q + 1)],
                       pr.d["idwn_rs"][st:st + 128, :], "mil%d" % (q % 2))
                g.wait_ge(plan.sem(VS), pr.marks.get("mo_idx_%d_%d" % (e, q), 0))
                g.wait_ge(plan.sem(VS), pr.marks["mo_gw_%d_%d" % (e, q)])
                plan.inc("mgl", g.indirect_dma_start(
                    out=xrow[:], out_offset=None,
                    in_=pr.d["x_full"],
                    in_offset=bass.IndirectOffsetOnAxis(ap=idxu[:, q:q + 1], axis=0),
                    bounds_check=pr.bnd32767, oob_is_err=False), 16)

    @block.gpsimd
    def _(g):
        g.wait_ge(plan.sem("mow"), 16 * 4 * EPC)
        plan.inc("cc_moe", g.collective_compute(
            "AllGather", ALU.bypass, replica_groups=[list(range(NCORES))],
            ins=[pr.d["moe_out"].opt()], outs=[pr.d["moe_all"].opt()]))


def emit_final(pr, block):
    plan = pr.plan
    s = pr.s
    NTC = 4
    nt = TPC // 128

    xnfm_b = pr.sbuf("fi_xnfm", [128, 8 * 1024], BF16)
    wsd = pr.sbuf("fi_wsd", [128, 16 * 1024], BF16)
    wsg_g = pr.sbuf("fi_wg", [128, 2 * 1024], BF16)
    wsg_u = pr.sbuf("fi_wu", [128, 2 * 1024], BF16)
    sil = pr.sbuf("fi_sil", [128, 1024], BF16)
    actT = pr.sbuf("fi_act", [128, 16 * 1024], BF16)
    gt8 = pr.sbuf("fi_gt", [128, 2 * 8 * 1024], BF16)
    ymoe = pr.sbuf("fi_ym", [128, 1024], F32)
    ytile = pr.sbuf("fi_y", [128, 2 * 1024], F32)
    xres = pr.sbuf("fi_xr", [128, 2 * 1024], F32)
    ssmt = pr.sbuf("fi_ssm", [128, 2 * 1024], BF16)
    sstg = pr.sbuf_global("fi_sstg", [32, 4096], BF16)
    psga = pr.psum("fi_pg1", [128, 1024], F32)
    psgb = pr.psum("fi_pg2", [128, 1024], F32)
    psda = pr.psum("fi_pd1", [128, 1024], F32)
    psdb = pr.psum("fi_pd2", [128, 1024], F32)
    VS = "vself"
    SM = {"fxl0": "rxl0", "fxl1": "rxl1", "fsl0": "rfw0", "fsl1": "rfw1",
          "fi_xc": "rcw", "fowp0": "mow", "fowp1": "cmw1",
          "fwg0": "mwl0", "fwg1": "mwl1",
          "fi_gb0": "mil0", "fi_gb1": "mil1", "fi_wl2": "rgl",
          "fi_ev": "mo_ev", "fi_gm": "mo_pe", "fi_up": "mo_tp",
          "fi_dnm": "mo_dn", "fi_actev": "mo_tpev", "fi_dnev": "mo_dnev",
          "fwg0p0": "mwl0", "fwg0p1": "mwl1", "fwg1p0": "mgl", "fwg1p1": "rsc"}
    BASE = {a: plan.val(r) for a, r in SM.items()}

    def fsem(alias):
        return plan.sem(SM[alias])

    def fbase(alias):
        return BASE[alias]

    def finc(alias, instr, k=16):
        return plan.inc(SM[alias], instr, k)

    def fence_eng(eng):
        for nm, cntv in pr.cv_fence:
            eng.wait_ge(plan.sem(nm), cntv)

    @block.vector
    def _(v):
        def vop(instr, k=1):
            n = plan.inc(VS, instr, k)
            v.wait_ge(plan.sem(VS), n)
            return n
        fence_eng(v)
        for tc in range(NTC):
            for ip in range(16):
                v.wait_ge(fsem("fi_ev"), fbase("fi_ev") + 16 * tc + ip + 1)
                v.wait_ge(fsem("fi_up"), fbase("fi_up") + 16 * tc + ip + 1)
                if tc >= 1:
                    v.wait_ge(fsem("fi_dnm"), fbase("fi_dnm") + 8 * tc)
                n = vop(v.scalar_tensor_tensor(
                    actT[:, 1024 * ip:1024 * (ip + 1)], psgb[:, 0:1024], 1.0,
                    sil[:], op0=ALU.mult, op1=ALU.mult))
                finc("fi_actev", v.nop(), 1)
                pr.marks["fi_act_%d_%d" % (tc, ip)] = n
            for tt in range(8):
                i = 8 * tc + tt
                ip2 = i % 2
                v.wait_ge(fsem("fi_gb%d" % ip2), fbase("fi_gb%d" % ip2) + 16 * 8 * (i // 2 + 1))
                base = 8 * ip2 * 1024
                for k in range(4):
                    vop(v.tensor_tensor(gt8[:, base + 2 * k * 1024:base + (2 * k + 1) * 1024],
                                        gt8[:, base + 2 * k * 1024:base + (2 * k + 1) * 1024],
                                        gt8[:, base + (2 * k + 1) * 1024:base + (2 * k + 2) * 1024],
                                        op=ALU.add))
                vop(v.tensor_tensor(gt8[:, base:base + 1024], gt8[:, base:base + 1024],
                                    gt8[:, base + 2048:base + 3072], op=ALU.add))
                vop(v.tensor_tensor(gt8[:, base + 4096:base + 5120],
                                    gt8[:, base + 4096:base + 5120],
                                    gt8[:, base + 6144:base + 7168], op=ALU.add))
                n = vop(v.tensor_tensor(ymoe[:], gt8[:, base:base + 1024],
                                        gt8[:, base + 4096:base + 5120], op=ALU.add))
                pr.marks["fi_tree_%d" % i] = n
                if i >= 2:
                    pw = "fowp%d" % (i % 2)
                    v.wait_ge(fsem(pw), fbase(pw) + 16 * (i // 2))
                v.wait_ge(fsem("fi_dnm"), fbase("fi_dnm") + i + 1)
                v.wait_ge(plan.sem("rt_sig"), 1)
                psd = psda if tt % 2 == 0 else psdb
                vop(v.scalar_tensor_tensor(ytile[:, ip2 * 1024:(ip2 + 1) * 1024],
                                           psd[:, 0:1024], s["sg_all"][:, i:i + 1],
                                           ymoe[:], op0=ALU.mult, op1=ALU.add))
                finc("fi_dnev", v.nop(), 1)
                v.wait_ge(fsem("fxl%d" % ip2), fbase("fxl%d" % ip2) + 16 * (i // 2 + 1))
                vop(v.tensor_tensor(ytile[:, ip2 * 1024:(ip2 + 1) * 1024],
                                    ytile[:, ip2 * 1024:(ip2 + 1) * 1024],
                                    xres[:, ip2 * 1024:(ip2 + 1) * 1024], op=ALU.add))
                v.wait_ge(fsem("fsl%d" % ip2), fbase("fsl%d" % ip2) + 16 * (i // 2 + 1))
                n = vop(v.tensor_tensor(ytile[:, ip2 * 1024:(ip2 + 1) * 1024],
                                        ytile[:, ip2 * 1024:(ip2 + 1) * 1024],
                                        ssmt[:, ip2 * 1024:(ip2 + 1) * 1024], op=ALU.add))
                pr.marks["fi_y_%d" % i] = n

    @block.tensor
    def _(te):
        fence_eng(te)
        te.wait_ge(fsem("fi_wl2"), fbase("fi_wl2") + 16 * 16)
        for tc in range(NTC):
            te.wait_ge(fsem("fi_xc"), fbase("fi_xc") + 16 * (tc + 1))
            for ip in range(16):
                te.wait_ge(fsem("fwg0p%d" % (ip % 2)),
                           fbase("fwg0p%d" % (ip % 2)) + 16 * ((16 * tc + ip) // 2 + 1))
                if 16 * tc + ip >= 1:
                    te.wait_ge(fsem("fi_ev"), fbase("fi_ev") + 16 * tc + ip)
                for cc in range(8):
                    lhs = wsg_g[:, ((ip % 2) * 8 + cc) * 128:((ip % 2) * 8 + cc + 1) * 128]
                    te.ldweights(lhs)
                    for hh in range(2):
                        _mm = te.matmul(psga[:, 512 * hh:512 * (hh + 1)], lhs,
                                        xnfm_b[:, cc * 1024 + 512 * hh:cc * 1024 + 512 * (hh + 1)],
                                        start=(cc == 0), stop=(cc == 7),
                                        skip_group_check=True)
                    if cc == 7:
                        finc("fi_gm", _mm, 1)
                te.wait_ge(fsem("fwg1p%d" % (ip % 2)),
                           fbase("fwg1p%d" % (ip % 2)) + 16 * ((16 * tc + ip) // 2 + 1))
                if 16 * tc + ip >= 1:
                    te.wait_ge(fsem("fi_actev"), fbase("fi_actev") + 16 * tc + ip)
                for cc in range(8):
                    lhs = wsg_u[:, ((ip % 2) * 8 + cc) * 128:((ip % 2) * 8 + cc + 1) * 128]
                    te.ldweights(lhs)
                    for hh in range(2):
                        _mm = te.matmul(psgb[:, 512 * hh:512 * (hh + 1)], lhs,
                                        xnfm_b[:, cc * 1024 + 512 * hh:cc * 1024 + 512 * (hh + 1)],
                                        start=(cc == 0), stop=(cc == 7),
                                        skip_group_check=True)
                    if cc == 7:
                        finc("fi_up", _mm, 1)
            for tt in range(8):
                i = 8 * tc + tt
                psd = psda if tt % 2 == 0 else psdb
                if i >= 2:
                    te.wait_ge(fsem("fi_dnev"), fbase("fi_dnev") + i - 1)
                for ip in range(16):
                    te.wait_ge(plan.sem(VS), pr.marks["fi_act_%d_%d" % (tc, ip)])
                    lhs = actT[:, 1024 * ip + 128 * tt:1024 * ip + 128 * (tt + 1)]
                    te.ldweights(lhs)
                    for hh in range(2):
                        _mm = te.matmul(psd[:, 512 * hh:512 * (hh + 1)], lhs,
                                        wsd[:, 1024 * ip + 512 * hh:1024 * ip + 512 * (hh + 1)],
                                        start=(ip == 0), stop=(ip == 15),
                                        skip_group_check=True)
                    if ip == 15:
                        finc("fi_dnm", _mm, 1)

    @block.scalar
    def _(sc):
        fence_eng(sc)
        for tc in range(NTC):
            for ip in range(16):
                sc.wait_ge(fsem("fi_gm"), fbase("fi_gm") + 16 * tc + ip + 1)
                if 16 * tc + ip >= 1:
                    sc.wait_ge(plan.sem(VS), pr.marks["fi_act_%d_%d" %
                               ((16 * tc + ip - 1) // 16, (16 * tc + ip - 1) % 16)])
                finc("fi_ev", sc.activation(sil[:], psga[:, 0:1024], AF.Silu), 1)
            for tt in range(8):
                i = 8 * tc + tt
                sc.wait_ge(plan.sem(VS), pr.marks["fi_y_%d" % i])
                finc("fowp%d" % (i % 2), sc.dma_start(
                    out=pr.d["y_out"][128 * i:128 * (i + 1), :],
                    in_=ytile[:, (i % 2) * 1024:(i % 2 + 1) * 1024]))
        sc.wait_ge(fsem("fowp0"), fbase("fowp0") + 16 * (nt // 2))
        sc.wait_ge(fsem("fowp1"), fbase("fowp1") + 16 * (nt // 2))

    @block.sync
    def _(sy):
        sy.wait_ge(plan.sem("cl"), plan.val("cl"))
        with sy.register("srk") as srk:
            sy.load(srk, pr.s["rk_i_sb"][0:1, 0:1])
            off = sy.snap(srk)
            bb = plan.val("kgw")
            kidx = 0
            for j in range(4):
                sy.wait_ge(plan.sem("cc_ssm"), j + 1)
                for cc in range(8):
                    if kidx >= 1:
                        sy.wait_ge(plan.sem("kgw"), bb + 16 * (2 * kidx))
                    plan.inc("kgw", sy.dma_start(
                        out=sstg[:],
                        in_=pr.d["ssm_ago%d" % j][32 * cc:32 * (cc + 1),
                                                  bass.ds(off, TPC)]), 16)
                    sy.wait_ge(plan.sem("kgw"), bb + 16 * (2 * kidx + 1))
                    plan.inc("kgw", sy.dma_start(
                        out=pr.d["ssm_my"][128 * cc + 32 * j:128 * cc + 32 * (j + 1), :],
                        in_=sstg[:]), 16)
                    kidx += 1
        fence_eng(sy)
        dma_in(plan, sy,
               xnfm_b[:].rearrange("p (c t) -> p c t", t=1024),
               pr.d["xnfm"].rearrange("(c p) t -> p c t", p=128)
               [:, :, 0:1024], SM["fi_xc"])
        for i0 in range(2):
            dma_in(plan, sy, xres[:, i0 * 1024:(i0 + 1) * 1024],
                   pr.d["x_ts"][128 * i0:128 * (i0 + 1), :], SM["fxl%d" % i0])
        if True:
            sy.wait_ge(plan.sem("kgw"), bb + 16 * 64)
            for tc in range(NTC):
                if tc >= 1:
                    sy.wait_ge(fsem("fi_up"), fbase("fi_up") + 16 * tc)
                    dma_in(plan, sy,
                           xnfm_b[:].rearrange("p (c t) -> p c t", t=1024),
                           pr.d["xnfm"].rearrange("(c p) t -> p c t", p=128)
                           [:, :, 1024 * tc:1024 * (tc + 1)], SM["fi_xc"])
                for tt in range(8):
                    i = 8 * tc + tt
                    ip2 = i % 2
                    if i >= 2:
                        sy.wait_ge(plan.sem(VS), pr.marks["fi_y_%d" % (i - 2)])
                        dma_in(plan, sy, xres[:, ip2 * 1024:(ip2 + 1) * 1024],
                               pr.d["x_ts"][128 * i:128 * (i + 1), :], SM["fxl%d" % ip2])
                    finc("fsl%d" % ip2, sy.dma_start(
                        out=ssmt[:, ip2 * 1024:(ip2 + 1) * 1024],
                        in_=pr.d["ssm_my"][:, 128 * i:128 * (i + 1)],
                        transpose=True))

    @block.gpsimd
    def _(g):
        fence_eng(g)
        for ic in range(16):
            dma_in(plan, g, wsd[:, 1024 * ic:1024 * (ic + 1)],
                   pr.d["sdw"].rearrange("(c p) f -> c p f", p=128)[ic], SM["fi_wl2"])
        for tc in range(NTC):
            for ip in range(16):
                if 16 * tc + ip >= 2:
                    g.wait_ge(fsem("fi_gm"), fbase("fi_gm") + 16 * tc + ip - 1)
                    g.wait_ge(fsem("fi_up"), fbase("fi_up") + 16 * tc + ip - 1)
                dma_in(plan, g,
                       wsg_g[:, (ip % 2) * 1024:(ip % 2 + 1) * 1024].rearrange(
                           "p (c f) -> p c f", f=128),
                       pr.d["sguw"][:, 128 * ip:128 * (ip + 1)].rearrange(
                           "(c p) f -> p c f", p=128), SM["fwg0p%d" % (ip % 2)])
                dma_in(plan, g,
                       wsg_u[:, (ip % 2) * 1024:(ip % 2 + 1) * 1024].rearrange(
                           "p (c f) -> p c f", f=128),
                       pr.d["sguw"][:, ISH + 128 * ip:ISH + 128 * (ip + 1)].rearrange(
                           "(c p) f -> p c f", p=128), SM["fwg1p%d" % (ip % 2)])
            for tt in range(8):
                i = 8 * tc + tt
                ip2 = i % 2
                if i == 0:
                    g.wait_ge(plan.sem("cc_moe"), 1)
                if i >= 2:
                    g.wait_ge(plan.sem(VS), pr.marks["fi_tree_%d" % (i - 2)])
                for k in range(8):
                    plan.inc("gms", g.memset(
                        gt8[:, (8 * ip2 + k) * 1024:(8 * ip2 + k + 1) * 1024], 0.0))
                g.wait_ge(plan.sem("gms"), plan.val("gms"))
                for k in range(8):
                    finc("fi_gb%d" % ip2, g.indirect_dma_start(
                        out=gt8[:, (8 * ip2 + k) * 1024:(8 * ip2 + k + 1) * 1024],
                        out_offset=None,
                        in_=pr.d["moe_all"],
                        in_offset=bass.IndirectOffsetOnAxis(
                            ap=pr.s["dsti_all"][:, 8 * i + k:8 * i + k + 1], axis=0),
                        bounds_check=pr.bnd32767, oob_is_err=False))


def _phase_wait(block, plan, pr, sems_counts, vs_marks):
    for engname in ["vector", "scalar", "tensor", "sync", "gpsimd"]:
        dec = getattr(block, engname)

        @dec
        def _(eng):
            for nm, cntv in sems_counts:
                eng.wait_ge(plan.sem(nm), cntv)
            for nm, mk in vs_marks:
                eng.wait_ge(plan.sem(nm), mk)


def build_full():
    import contextlib
    nc = bass.Bass("TRN2", target_bir_lowering=False)
    stack = contextlib.ExitStack()
    pr = Prog(nc, stack, 0)
    declare_conv_io(pr)
    declare_ffn_io(pr)
    pr.dram("y_out", [TPC, HD], F32, kind="ExternalOutput")

    with nc.Block() as block:
        plan = pr.plan
        pr.sbuf("dsti_all", [128, 8 * (TPC // 128)], U32)
        pr.sbuf("sg_all", [128, TPC // 128], F32)
        pr.sbuf("rk_i_sb", [1, 1], I32)

        @block.gpsimd
        def _(g):
            conv_const_loads(pr, g)
            dma_in(plan, g, pr.s["rk_i_sb"][:], pr.d["rank_i"], "cl")

        pr.push()
        emit_stats(pr, block)
        st_fence = [("sv", plan.val("sv")), ("ss2", plan.val("ss2")),
                    ("sve", plan.val("sve")), ("sgi", 16)]
        pr.pop()

        pr.push()
        _phase_wait(block, plan, pr, st_fence, [])
        emit_routing(pr, block)
        rt_fence = [("rvs", plan.val("rvs")), ("rt_exp", plan.val("rt_exp")),
                    ("rt_sig", 1), ("rsc", plan.val("rsc")),
                    ("rsc1", plan.val("rsc1")),
                    ("rt_chT", plan.val("rt_chT")), ("rt_gtm", plan.val("rt_gtm")),
                    ("rfw0", plan.val("rfw0")), ("rfw1", plan.val("rfw1")),
                    ("rzw", 16), ("rcw", 16)]
        pr.pop()

        pr.push()
        _phase_wait(block, plan, pr, rt_fence, [])
        pr.sbuf("kb_gidx", [128, 16], U32)
        pr.push()
        emit_kg(pr, block)
        pr.pop()
        emit_basis(pr, block)
        emit_cmix(pr, block)
        cm_fence = [("cmp", plan.val("cmp")), ("cme", plan.val("cme")),
                    ("cmw0", plan.val("cmw0")), ("cmw1", plan.val("cmw1")),
                    ("cmw2", plan.val("cmw2")), ("cmw3", plan.val("cmw3")),
                    ("kbv", plan.val("kbv")), ("kbv2", plan.val("kbv2")),
                    ("kbe", plan.val("kbe")), ("kbe2", plan.val("kbe2")),
                    ("cmv", 1)]
        pr.pop()

        pr.push()
        _phase_wait(block, plan, pr, cm_fence, [])
        emit_moe(pr, block)
        mo_fence = [("mvs", plan.val("mvs")), ("mo_pe", plan.val("mo_pe")),
                    ("mo_dn", plan.val("mo_dn")), ("mo_ev", plan.val("mo_ev")),
                    ("mow", plan.val("mow")), ("mgl", plan.val("mgl")),
                    ("mo_tp", plan.val("mo_tp"))]
        pr.pop()

        pr.push()
        _phase_wait(block, plan, pr, mo_fence, [])
        c0 = dict(plan.cnt)
        emit_conv(pr, block)
        pr.cv_fence = [(nm, v) for nm, v in plan.cnt.items()
                       if v != c0.get(nm, 0) and nm != "cc_ssm"]
        pr.pop()

        pr.push()
        emit_final(pr, block)
        pr.pop()

    stack.close()
    return nc


_NC_CACHE = None


def kernel(**inputs):
    global _NC_CACHE
    x = np.asarray(inputs["L0_attn_r"])
    prm = dict(B_re=np.asarray(inputs["L0_attn_B_re"]),
               B_im=np.asarray(inputs["L0_attn_B_im"]),
               C_re=np.asarray(inputs["L0_attn_C_re"]),
               C_im=np.asarray(inputs["L0_attn_C_im"]),
               P_re=np.asarray(inputs["L0_attn_P_re"]),
               P_im=np.asarray(inputs["L0_attn_P_im"]),
               Q_re=np.asarray(inputs["L0_attn_Q_re"]),
               Q_im=np.asarray(inputs["L0_attn_Q_im"]),
               diag_re=np.asarray(inputs["L0_attn_diag_re"]),
               diag_im=np.asarray(inputs["L0_attn_diag_im"]),
               step=np.asarray(inputs["L0_attn_step"]),
               D=np.asarray(inputs["L0_attn_D"]))
    if _NC_CACHE is None:
        _NC_CACHE = build_full()
    nc = _NC_CACHE
    in_maps = []
    for c in range(NCORES):
        m = _shard_ssm_inputs(c, x, prm)
        m["gate_w"] = np.asarray(inputs["L0_ffn_gate_w"])
        m["shared_gate_w"] = np.asarray(inputs["L0_ffn_shared_gate_w"])
        m["sguw"] = np.asarray(inputs["L0_ffn_shared_gate_up_w"])
        m["sdw"] = np.asarray(inputs["L0_ffn_shared_down_w"])
        m["egu_c"] = np.ascontiguousarray(
            np.asarray(inputs["L0_ffn_expert_gate_up_w"])[EPC * c:EPC * (c + 1)])
        m["edw_c"] = np.ascontiguousarray(
            np.asarray(inputs["L0_ffn_expert_down_w"])[EPC * c:EPC * (c + 1)])
        m["rank_off"] = np.array([[4096.0 * c + 1.0]], np.float32)
        m["tri_rk"] = (np.arange(8) < c).astype(np.float32).reshape(8, 1)
        m["rank_i"] = np.array([[4096 * c]], np.int32)
        in_maps.append(m)
    res = run_bass_kernel_spmd(nc, in_maps, core_ids=list(range(NCORES)))
    out = np.concatenate([res.results[c]["y_out"] for c in range(NCORES)], axis=0)
    return out.reshape(B, L, HD)
